# version 04
# baseline (speedup 1.0000x reference)
import numpy as np
import ml_dtypes
from contextlib import ExitStack

import concourse.bass as bass
import concourse.mybir as mybir
from concourse.bass_utils import run_bass_kernel_spmd

F32 = mybir.dt.float32
BF16 = mybir.dt.bfloat16
AF = mybir.ActivationFunctionType
ALU = mybir.AluOpType
AX = mybir.AxisListType

NEG = -1.0e9


class Ctr:

    def __init__(self, nc, ctx, name, step=1):
        self.sem = ctx.enter_context(nc.semaphore(name))
        self.step = step
        self.n = 0

    def inc(self, inst):
        inst.then_inc(self.sem, self.step)
        self.n += self.step
        return self.n


class DStream:

    def __init__(self, nc, ctx, name, depth=1):
        self.sems = [ctx.enter_context(nc.semaphore(f"{name}_{i}"))
                     for i in range(depth)]
        self.n = [0] * depth

    def inc(self, inst, slot=0):
        inst.then_inc(self.sems[slot], 16)
        self.n[slot] += 16
        return (slot, self.n[slot])

    def wait(self, eng, tok):
        eng.wait_ge(self.sems[tok[0]], tok[1])

    def wait_all(self, eng):
        for i, s in enumerate(self.sems):
            if self.n[i]:
                eng.wait_ge(s, self.n[i])


class _TE:

    def __init__(self, t):
        self._t = t

    def __getattr__(self, n):
        return getattr(self._t, n)

    def matmul(self, *a, **kw):
        kw.setdefault("skip_group_check", True)
        return self._t.matmul(*a, **kw)

    def transpose(self, out, in_, identity, tile_position=None):
        return self._t.matmul(out, in_, identity, start=True, stop=True,
                              is_transpose=True, skip_group_check=True,
                              tile_position=tile_position)


class Builder:
    def __init__(self, TOK=2048, D=1024, H=16, BS=64, WIN=512, E=8, INTER=4096):
        self.TOK, self.D, self.H, self.BS, self.WIN = TOK, D, H, BS, WIN
        self.E, self.INTER = E, INTER
        self.HALO = WIN
        self.HALF = TOK // 2
        self.HQB = self.HALF // BS
        self.POFF = self.HQB // 2
        assert self.POFF % 2 == 0
        self.NP = self.POFF
        self.KV = self.HALF + self.HALO
        self.KVT = self.KV // 128
        self.NTT = TOK // 128
        self.DT = D // 128
        self.WINW = WIN + BS
        self.IC = INTER // 512
        self.ITL = INTER // 128
        self.NPAIR_T = 2 * self.NP
        self.st = {}

    def build(self):
        TOK, D, E, INTER = self.TOK, self.D, self.E, self.INTER
        DT, ITL, WINW = self.DT, self.ITL, self.WINW
        nc = self.nc = bass.Bass("TRN2", target_bir_lowering=False)

        self.x_main = nc.dram_tensor("x_main", [TOK, D], F32, kind="ExternalInput")
        self.x_halo = nc.dram_tensor("x_halo", [self.HALO, D], F32, kind="ExternalInput")
        self.w_in = [nc.dram_tensor(n, [D, D], F32, kind="ExternalInput")
                     for n in ("wq", "wk", "wv", "wo")]
        self.wg = nc.dram_tensor("wg", [E, D], F32, kind="ExternalInput")
        self.w1 = nc.dram_tensor("w1", [E, D, INTER], F32, kind="ExternalInput")
        self.b1 = nc.dram_tensor("b1", [128, E, INTER // 128], F32,
                                 kind="ExternalInput")
        self.w2 = nc.dram_tensor("w2", [E, INTER, D], F32, kind="ExternalInput")
        self.b2 = nc.dram_tensor("b2", [E, D], F32, kind="ExternalInput")
        self.mask = nc.dram_tensor("mask", [self.NPAIR_T, 128, WINW], F32,
                                   kind="ExternalInput")
        self.idenb_d = nc.dram_tensor("idenb", [128, 128], BF16, kind="ExternalInput")
        self.idenf_d = nc.dram_tensor("idenf", [128, 128], F32, kind="ExternalInput")
        self.iot_d = nc.dram_tensor("ioteps", [128, E], F32, kind="ExternalInput")
        self.yout = nc.dram_tensor("yout", [TOK, D], F32, kind="ExternalOutput")
        self.wT_scr = [nc.dram_tensor(f"wT{i}_scr", [DT, 128, D], BF16, kind="Internal")
                       for i in range(4)]
        self.x1_scr = nc.dram_tensor("x1_scr", [TOK, D], F32, kind="Internal")

        ctx = ExitStack()
        with ctx:
            e = ctx.enter_context
            self.X1fm = e(nc.sbuf_tensor("X1fm", [128, DT, TOK], BF16))
            self.idb = e(nc.sbuf_tensor("idb", [128, 128], BF16))
            self.idf = e(nc.sbuf_tensor("idf", [128, 128], F32))
            self.b1sb = e(nc.sbuf_tensor("b1sb", [128, E, ITL], F32))
            self.b2sb = e(nc.sbuf_tensor("b2sb", [E, D], BF16))
            self.wgT = e(nc.sbuf_tensor("wgT", [128, DT, E], BF16))
            self.iot = e(nc.sbuf_tensor("iot", [128, E], F32))
            self.gates = e(nc.sbuf_tensor("gates", [128, self.NTT, E], F32))
            self.gatesT = e(nc.sbuf_tensor("gatesT", [E, TOK], BF16))
            self.w1sl = e(nc.sbuf_tensor("w1sl", [128, 2, DT, 512], BF16))
            self.w2sl = e(nc.sbuf_tensor("w2sl", [128, 2, 4, 512], BF16))

            self.cPE = Ctr(nc, ctx, "cPE")
            self.cDV = Ctr(nc, ctx, "cDV")
            self.cAC = Ctr(nc, ctx, "cAC")
            self.g_wld = DStream(nc, ctx, "g_wld", 2)
            self.g_wg = DStream(nc, ctx, "g_wg")
            self.g_bias = DStream(nc, ctx, "g_bias")
            self.g_xld = DStream(nc, ctx, "g_xld", 2)
            self.g_wTa = DStream(nc, ctx, "g_wTa")
            self.g_woT = DStream(nc, ctx, "g_woT")
            self.g_w1 = DStream(nc, ctx, "g_w1", 2)
            self.g_w2 = DStream(nc, ctx, "g_w2", 2)
            self.s_iden = DStream(nc, ctx, "s_iden")
            self.s_p0st = DStream(nc, ctx, "s_p0st", 2)
            self.s_mask = DStream(nc, ctx, "s_mask", 2)
            self.s_x0 = DStream(nc, ctx, "s_x0", 1)
            self.s_x1 = DStream(nc, ctx, "s_x1", 2)
            self.s_yin = DStream(nc, ctx, "s_yin")
            self.s_out = DStream(nc, ctx, "s_out", 2)

            self.block = e(nc.Block())

            self.p0(ctx)
            with nc.sbuf_tensor("Xfm", [128, DT, self.KV], BF16) as Xfm, \
                 nc.sbuf_tensor("Qf", [128, DT, self.HALF], BF16) as Qf, \
                 nc.sbuf_tensor("Kf", [128, DT, self.KV], BF16) as Kf, \
                 nc.sbuf_tensor("Vt", [128, self.KVT, D], BF16) as Vt, \
                 nc.sbuf_tensor("wTa", [128, DT, D], BF16) as wTa, \
                 nc.sbuf_tensor("xld", [128, 2, D], BF16) as xld, \
                 nc.sbuf_tensor("x0r", [128, 1, D], F32) as x0r, \
                 nc.sbuf_tensor("Ssb", [128, 2, WINW], F32) as Ssb, \
                 nc.sbuf_tensor("Pbf", [128, 2, WINW], BF16) as Pbf, \
                 nc.sbuf_tensor("PTb", [128, 2, 5, 128], BF16) as PTb, \
                 nc.sbuf_tensor("attnC", [128, 2, DT, 128], BF16) as attnC, \
                 nc.sbuf_tensor("x1c", [128, 2, D], F32) as x1c, \
                 nc.sbuf_tensor("maskb", [128, 2, WINW], F32) as maskb, \
                 nc.sbuf_tensor("smal", [128, 2, 16], F32) as smal, \
                 nc.sbuf_tensor("sqj", [128, D], BF16) as sqj:
                self.Xfm, self.Qf, self.Kf, self.Vt = Xfm, Qf, Kf, Vt
                self.wTa, self.woT, self.xld, self.x0r = wTa, wTa, xld, x0r
                self.Ssb, self.Pbf, self.PTb, self.attnC = Ssb, Pbf, PTb, attnC
                self.x1c, self.maskb, self.smal, self.sqj = x1c, maskb, smal, sqj
                for ha in range(2):
                    self.half(ha)

            self.gates_phase()
            with nc.sbuf_tensor("hbuf", [128, ITL, self.HALF], BF16) as hbuf, \
                 nc.sbuf_tensor("y_tm", [128, self.NTT, D], F32) as y_tm:
                self.hbuf, self.y_tm = hbuf, y_tm
                self.ffn()
                with nc.sbuf_tensor("outst", [128, 2, D], F32) as outst, \
                     nc.sbuf_tensor("lnt2", [128, 16], F32) as lnt2, \
                     nc.sbuf_tensor("sqj2", [128, D], BF16) as sqj2:
                    self.outst, self.lnt2, self.sqj2 = outst, lnt2, sqj2
                    self.ln2()
        return nc

    def on(self, eng, fn):
        if eng == "tensor":
            getattr(self.block, eng)(lambda t: fn(_TE(t)))
        else:
            getattr(self.block, eng)(fn)

    def p0(self, ctx):
        nc, st = self.nc, self.st
        DT, D, E, ITL, INTER = self.DT, self.D, self.E, self.ITL, self.INTER
        cPE, cDV = self.cPE, self.cDV

        with nc.sbuf_tensor("wld", [128, 2, D], BF16) as wld, \
             nc.sbuf_tensor("wstg", [128, 2, DT, 128], BF16) as wstg, \
             nc.sbuf_tensor("wgld", [E, D], BF16) as wgld, \
             nc.psum_tensor("wtp", [128, 2, D], BF16) as wtp, \
             nc.psum_tensor("wgp", [128, DT * E], BF16) as wgp:

            def sy0(sy):
                self.s_iden.inc(sy.dma_start(self.idb[:, :], self.idenb_d[:, :]))
                self.s_iden.inc(sy.dma_start(self.idf[:, :], self.idenf_d[:, :]))
                self.s_iden.inc(sy.dma_start(self.iot[:, :], self.iot_d[:, :]))

            def gp0a(gp):
                st["wgld"] = self.g_wg.inc(gp.dma_start(wgld[:, :], self.wg[:, :]))
                self.g_bias.inc(gp.dma_start(self.b1sb[:, :, :], self.b1[:, :, :]))
                self.g_bias.inc(gp.dma_start(self.b2sb[:, :], self.b2[:, :]))

            self.on("sync", sy0)
            self.on("gpsimd", gp0a)

            for i in range(4 * DT):
                wi, rt = divmod(i, DT)
                s = i % 2

                def gp0(gp, i=i, wi=wi, rt=rt, s=s):
                    if i >= 2:
                        gp.wait_ge(cPE.sem, st[("p0tp", i - 2)])
                    st[("p0ld", i)] = self.g_wld.inc(gp.dma_start(
                        wld[:, s, :],
                        self.w_in[wi][rt * 128:(rt + 1) * 128, :]), s)

                def pe0(t, i=i, s=s):
                    if i == 0:
                        self.s_iden.wait_all(t)
                    self.g_wld.wait(t, st[("p0ld", i)])
                    if i >= 2:
                        t.wait_ge(cDV.sem, st[("p0ev", i - 2)])
                    last = None
                    for c in range(DT):
                        last = t.transpose(
                            wtp[:, s, c * 128:(c + 1) * 128],
                            wld[:, s, c * 128:(c + 1) * 128],
                            self.idb[:, :])
                    cPE.inc(last)
                    st[("p0tp", i)] = cPE.n

                def dv0(v, i=i, s=s):
                    v.wait_ge(cPE.sem, st[("p0tp", i)])
                    if i >= 2:
                        self.s_p0st.wait(v, st[("p0st", i - 2)])
                    cDV.inc(v.tensor_copy(
                        wstg[:, s, :, :],
                        bass.AP(wtp, s * D, [[2 * D, 128], [128, DT], [1, 128]])))
                    st[("p0ev", i)] = cDV.n

                def sy0b(sy, i=i, wi=wi, rt=rt, s=s):
                    sy.wait_ge(cDV.sem, st[("p0ev", i)])
                    st[("p0st", i)] = self.s_p0st.inc(sy.dma_start(
                        bass.AP(self.wT_scr[wi], rt * 128,
                                [[D, 128], [128 * D, DT], [1, 128]]),
                        wstg[:, s, :, :]), s)

                self.on("gpsimd", gp0)
                self.on("tensor", pe0)
                self.on("vector", dv0)
                self.on("sync", sy0b)

            def pe0g(t):
                self.g_wg.wait(t, st["wgld"])
                last = None
                for c in range(DT):
                    last = t.transpose(wgp[:, c * E:(c + 1) * E],
                                       wgld[:, c * 128:(c + 1) * 128],
                                       self.idb[0:E, 0:E])
                cPE.inc(last)
                st["wgtp"] = cPE.n

            def dv0g(v):
                v.wait_ge(cPE.sem, st["wgtp"])
                cDV.inc(v.tensor_copy(
                    self.wgT[:, :, :],
                    bass.AP(wgp, 0, [[DT * E, 128], [E, DT], [1, E]])))
                st["wgev"] = cDV.n

            self.on("tensor", pe0g)
            self.on("vector", dv0g)

    def half(self, ha):
        nc, st = self.nc, self.st
        D, DT, KV, KVT, HALF, HALO = self.D, self.DT, self.KV, self.KVT, self.HALF, self.HALO
        BS, NP, POFF, WINW, TOK = self.BS, self.NP, self.POFF, self.WINW, self.TOK
        cPE, cDV, cAC = self.cPE, self.cDV, self.cAC
        Xfm, Qf, Kf, Vt, wTa, woT = self.Xfm, self.Qf, self.Kf, self.Vt, self.wTa, self.woT
        xld, x0r = self.xld, self.x0r

        with nc.psum_tensor(f"xtp{ha}", [128, D], BF16) as xtp, \
             nc.psum_tensor(f"prj{ha}", [128, 2, 512], F32) as prj:

            def kv_src(kvt):
                base = ha * HALF - HALO + kvt * 128
                if base < 0:
                    return (self.x_halo, base + HALO)
                return (self.x_main, base)

            for kvt in range(KVT):
                def gp_x(gp, kvt=kvt):
                    if kvt >= 2:
                        gp.wait_ge(cPE.sem, st[("xT", ha, kvt - 2)])
                    elif ha == 1:
                        gp.wait_ge(cPE.sem, st[("xT", 0, KVT - 2 + kvt)])
                    else:
                        gp.wait_ge(cPE.sem, st["wgtp"])
                        self.s_p0st.wait_all(gp)
                    src, r0 = kv_src(kvt)
                    st[("xld", ha, kvt)] = self.g_xld.inc(
                        gp.dma_start(xld[:, kvt % 2, :],
                                     src[r0:r0 + 128, :]), kvt % 2)

                def pe_xT(t, kvt=kvt):
                    if ha == 0 and kvt == 0:
                        t.wait_ge(cDV.sem, st["wgev"])
                    self.g_xld.wait(t, st[("xld", ha, kvt)])
                    if kvt >= 1:
                        t.wait_ge(cDV.sem, st[("xTe", ha, kvt - 1)])
                    elif ha == 1:
                        t.wait_ge(cDV.sem, st[("xTe", 0, KVT - 1)])
                    last = None
                    for c in range(DT):
                        last = t.transpose(xtp[:, c * 128:(c + 1) * 128],
                                           xld[:, kvt % 2, c * 128:(c + 1) * 128],
                                           self.idb[:, :])
                    cPE.inc(last)
                    st[("xT", ha, kvt)] = cPE.n

                def dv_xT(v, kvt=kvt):
                    v.wait_ge(cPE.sem, st[("xT", ha, kvt)])
                    if ha == 1 and kvt == 0:
                        v.wait_ge(cPE.sem, st[("proj_done", 0, 2)])
                    elif ha == 0 and kvt == 0:
                        self.s_p0st.wait_all(v)
                    cDV.inc(v.tensor_copy(
                        Xfm[:, :, kvt * 128:(kvt + 1) * 128],
                        bass.AP(xtp, 0, [[D, 128], [128, DT], [1, 128]])))
                    st[("xTe", ha, kvt)] = cDV.n

                self.on("gpsimd", gp_x)
                self.on("tensor", pe_xT)
                self.on("vector", dv_xT)

            plan = []
            QCH, KCH = HALF // 512, KV // 512
            for dq in range(DT):
                for ch in range(QCH):
                    plan.append((0, "q", dq, ch))
            for dq in range(DT):
                for ch in range(KCH):
                    plan.append((1, "k", dq, ch))
            for kvt in range(KVT):
                for ch in range(2):
                    plan.append((2, "v", kvt, ch))
            NPL = len(plan)

            for i, (wi, kind, a, chn) in enumerate(plan):
                if i == 0 or plan[i - 1][0] != wi:
                    def gp_w(gp, wi=wi):
                        if (ha, wi) == (0, 0):
                            self.s_p0st.wait_all(gp)
                        elif wi == 0:
                            gp.wait_ge(cPE.sem, st[("opj_pe", NP - 1, 1)])
                        else:
                            gp.wait_ge(cPE.sem, st[("proj_done", ha, wi - 1)])
                        st[("wT", ha, wi)] = self.g_wTa.inc(gp.dma_start(
                            wTa[:, :, :],
                            bass.AP(self.wT_scr[wi], 0,
                                    [[D, 128], [128 * D, DT], [1, D]])))

                    self.on("gpsimd", gp_w)

                def pe_proj(t, i=i, wi=wi, kind=kind, a=a, chn=chn):
                    if i == 0 or plan[i - 1][0] != wi:
                        self.g_wTa.wait(t, st[("wT", ha, wi)])
                    slot = i % 2
                    if i >= 2:
                        t.wait_ge(cDV.sem, st[("prj_ev", ha, i - 2)])
                    elif ha == 1:
                        t.wait_ge(cDV.sem, st[("prj_ev", 0, NPL - 2 + i)])
                    last = None
                    if kind == "q":
                        t.wait_ge(cDV.sem,
                                  st[("xTe", ha, HALO // 128 + (chn + 1) * 4 - 1)])
                        for k in range(DT):
                            last = t.matmul(
                                prj[:, slot, :],
                                wTa[:, k, a * 128:(a + 1) * 128],
                                Xfm[:, k, HALO + chn * 512: HALO + (chn + 1) * 512],
                                start=(k == 0), stop=(k == DT - 1))
                    elif kind == "k":
                        t.wait_ge(cDV.sem, st[("xTe", ha, (chn + 1) * 4 - 1)])
                        for k in range(DT):
                            last = t.matmul(
                                prj[:, slot, :],
                                wTa[:, k, a * 128:(a + 1) * 128],
                                Xfm[:, k, chn * 512: (chn + 1) * 512],
                                start=(k == 0), stop=(k == DT - 1))
                    else:
                        t.wait_ge(cDV.sem, st[("xTe", ha, a)])
                        for k in range(DT):
                            last = t.matmul(
                                prj[:, slot, :],
                                Xfm[:, k, a * 128:(a + 1) * 128],
                                wTa[:, k, chn * 512: (chn + 1) * 512],
                                start=(k == 0), stop=(k == DT - 1))
                    cPE.inc(last)
                    st[("prj_pe", ha, i)] = cPE.n
                    if i == NPL - 1 or plan[i + 1][0] != wi:
                        st[("proj_done", ha, wi)] = cPE.n

                def dv_proj(v, i=i, kind=kind, a=a, chn=chn):
                    slot = i % 2
                    v.wait_ge(cPE.sem, st[("prj_pe", ha, i)])
                    if kind == "q":
                        cDV.inc(v.tensor_scalar_mul(
                            Qf[:, a, chn * 512:(chn + 1) * 512],
                            prj[:, slot, :], 0.125))
                    elif kind == "k":
                        cDV.inc(v.tensor_copy(
                            Kf[:, a, chn * 512:(chn + 1) * 512],
                            prj[:, slot, :]))
                    else:
                        cDV.inc(v.tensor_copy(
                            Vt[:, a, chn * 512:(chn + 1) * 512],
                            prj[:, slot, :]))
                    st[("prj_ev", ha, i)] = cDV.n

                self.on("tensor", pe_proj)
                self.on("vector", dv_proj)
            st[("qkv_dv", ha)] = cDV.n

            def gp_wo(gp):
                gp.wait_ge(cPE.sem, st[("proj_done", ha, 2)])
                st[("woT", ha)] = self.g_woT.inc(gp.dma_start(
                    wTa[:, :, :],
                    bass.AP(self.wT_scr[3], 0,
                            [[D, 128], [128 * D, DT], [1, D]])))

            self.on("gpsimd", gp_wo)

        with nc.psum_tensor(f"Sp0{ha}", [128, WINW], F32) as Sp0, \
             nc.psum_tensor(f"Sp1{ha}", [128, WINW], F32) as Sp1, \
             nc.psum_tensor(f"PTp{ha}", [128, 5, 128], BF16) as PTp, \
             nc.psum_tensor(f"pv0{ha}", [128, 512], F32) as pv0, \
             nc.psum_tensor(f"pv1{ha}", [128, 512], F32) as pv1, \
             nc.psum_tensor(f"x1p{ha}", [128, 512], F32) as x1p:
            Sp = [Sp0, Sp1]
            pvp = [pv0, pv1]

            def pe_ptinit(t):
                cPE.inc(t.transpose(PTp[:, 4, :], self.idb[:, :],
                                    self.idb[:, :]))
                st[("ptinit", ha)] = cPE.n

            self.on("tensor", pe_ptinit)
            for p in range(NP):
                self.pair(ha, p, Sp, PTp, pvp, x1p)
            self.tail(ha, NP - 1, pvp, x1p)

        st[("half_done", ha)] = cDV.n

    def pair(self, ha, p, Sp, PTp, pvp, x1p):
        nc, st = self.nc, self.st
        D, DT, HALF, BS, NP, POFF, WINW = (self.D, self.DT, self.HALF, self.BS,
                                           self.NP, self.POFF, self.WINW)
        TOK = self.TOK
        cPE, cDV, cAC = self.cPE, self.cDV, self.cAC
        Qf, Kf, Vt = self.Qf, self.Kf, self.Vt
        Ssb, Pbf, PTb, attnC = self.Ssb, self.Pbf, self.PTb, self.attnC
        maskb, smal = self.maskb, self.smal
        H_ = self.H

        pg = ha * NP + p
        s2 = p % 2
        qA, qB = p, p + POFF
        win0, win1 = qA * BS, qB * BS
        odd = (qA % 2 == 1)
        S_pe, S_ev, exp_ac, nrm_dv, PT_pe, PT_ev, pv_pe = \
            [], [], [], [], [], [], []
        pv_started = set()

        def sy_pair(sy):
            if pg >= 2:
                sy.wait_ge(cDV.sem, st[("mask_free", pg - 2)])
            st[("mload", pg)] = self.s_mask.inc(
                sy.dma_start(maskb[:, s2, :], self.mask[pg, :, :]), s2)
            if pg >= 1:
                sy.wait_ge(cDV.sem, st[("x0_free", pg - 1)])
            r0 = ha * HALF + qA * BS
            r1 = ha * HALF + qB * BS
            self.s_x0.inc(sy.dma_start(self.x0r[0:64, 0, :],
                                       self.x_main[r0:r0 + 64, :]))
            st[("x0ld", pg)] = self.s_x0.inc(
                sy.dma_start(self.x0r[64:128, 0, :],
                             self.x_main[r1:r1 + 64, :]))

        def pe_S(t, h):
            hp, hh = divmod(h, 2)
            r0 = hh * 64
            hs = h % 2
            if h == 0 and p == 0:
                t.wait_ge(cDV.sem, st[("qkv_dv", ha)])
                self.g_woT.wait(t, st[("woT", ha)])
            if h >= 2:
                t.wait_ge(cDV.sem, S_ev[h - 2])
            elif pg >= 1:
                t.wait_ge(cDV.sem, st[("S_ev", pg - 1, H_ - 2 + h)])
            t.matmul(Sp[hs][0:64, 0:512],
                     Qf[r0:r0 + 64, hp, qA * 64:(qA + 1) * 64],
                     Kf[r0:r0 + 64, hp, win0: win0 + 512],
                     start=True, stop=True, tile_position=(r0, 0))
            t.matmul(Sp[hs][64:128, 0:512],
                     Qf[r0:r0 + 64, hp, qB * 64:(qB + 1) * 64],
                     Kf[r0:r0 + 64, hp, win1: win1 + 512],
                     start=True, stop=True, tile_position=(r0, 64))
            t.matmul(Sp[hs][0:64, 512:WINW],
                     Qf[r0:r0 + 64, hp, qA * 64:(qA + 1) * 64],
                     Kf[r0:r0 + 64, hp, win0 + 512: win0 + 576],
                     start=True, stop=True, tile_position=(r0, 0))
            last = t.matmul(Sp[hs][64:128, 512:WINW],
                            Qf[r0:r0 + 64, hp, qB * 64:(qB + 1) * 64],
                            Kf[r0:r0 + 64, hp, win1 + 512: win1 + 576],
                            start=True, stop=True, tile_position=(r0, 64))
            cPE.inc(last)
            S_pe.append(cPE.n)
            st[("S_pe", pg, h)] = cPE.n

        def dv_S(v, h):
            hs = h % 2
            v.wait_ge(cPE.sem, S_pe[h])
            if h == 0:
                self.s_mask.wait(v, st[("mload", pg)])
            if h >= 2:
                v.wait_ge(cAC.sem, exp_ac[h - 2])
            elif pg >= 1:
                v.wait_ge(cAC.sem, st[("exp_last", pg - 1)])
            cDV.inc(v.reduce_max(smal[:, hs, 0:1], Sp[hs][:, :],
                                 axis=AX.X, negate=True))
            v.wait_ge(cDV.sem, cDV.n)
            cDV.inc(v.scalar_tensor_tensor(Ssb[:, hs, :], Sp[hs][:, :],
                                           smal[:, hs, 0:1], maskb[:, s2, :],
                                           op0=ALU.add, op1=ALU.add))
            S_ev.append(cDV.n)
            st[("S_ev", pg, h)] = cDV.n

        def ac_exp(s, h):
            hs = h % 2
            s.wait_ge(cDV.sem, S_ev[h])
            if h >= 2:
                s.wait_ge(cPE.sem, PT_pe[h - 2])
            elif pg >= 1:
                s.wait_ge(cPE.sem, st[("PT_last", pg - 1)])
            cAC.inc(s.activation(Pbf[:, hs, :], Ssb[:, hs, :],
                                 AF.Exp, accum_out=smal[:, hs, 1:2]))
            exp_ac.append(cAC.n)

        def dv_nrm(v, h):
            hs = h % 2
            v.wait_ge(cAC.sem, exp_ac[h])
            cDV.inc(v.reciprocal(smal[:, hs, 2:3], smal[:, hs, 1:2]))
            v.wait_ge(cDV.sem, cDV.n)
            cDV.inc(v.tensor_scalar_mul(Pbf[:, hs, :], Pbf[:, hs, :],
                                        smal[:, hs, 2:3]))
            nrm_dv.append(cDV.n)

        def pe_PT(t, h):
            hs = h % 2
            t.wait_ge(cDV.sem, nrm_dv[h])
            if h >= 1:
                t.wait_ge(cDV.sem, PT_ev[h - 1])
            elif pg >= 1:
                t.wait_ge(cDV.sem, st[("PT_ev_last", pg - 1)])
            else:
                t.wait_ge(cPE.sem, st[("ptinit", ha)])
            last = None
            if not odd:
                for cc in range(4):
                    last = t.transpose(PTp[:, cc, :],
                                       Pbf[:, hs, cc * 128:(cc + 1) * 128],
                                       self.idb[:, :])
                last = t.transpose(PTp[0:64, 4, :],
                                   Pbf[:, hs, 512:576], self.idb[:, :])
            else:
                last = t.transpose(PTp[64:128, 0, :],
                                   Pbf[:, hs, 0:64], self.idb[:, :],
                                   tile_position=(0, 64))
                for cc in range(4):
                    last = t.transpose(
                        PTp[:, cc + 1, :],
                        Pbf[:, hs, 64 + cc * 128: 64 + (cc + 1) * 128],
                        self.idb[:, :])
            cPE.inc(last)
            PT_pe.append(cPE.n)
            st[("PT_pe", pg, h)] = cPE.n

        def dv_PT(v, h):
            hs = h % 2
            v.wait_ge(cPE.sem, PT_pe[h])
            if h >= 2:
                v.wait_ge(cPE.sem, pv_pe[h - 2])
            elif pg >= 1:
                v.wait_ge(cPE.sem, st[("pv_last", pg - 1)])
            cDV.inc(v.tensor_copy(PTb[:, hs, :, :], PTp[:, :, :]))
            PT_ev.append(cDV.n)

        def pe_PV(t, h):
            hs = h % 2
            hp, hh = divmod(h, 2)
            bank = pvp[hp // 4]
            fo = (hp % 4) * 128
            t.wait_ge(cDV.sem, PT_ev[h])
            if h == 0 and pg >= 1:
                t.wait_ge(cDV.sem, st[("attn_ev", pg - 1)])
            if h == 8 and pg >= 1:
                t.wait_ge(cDV.sem, st[("res_dv", pg - 1, 1)])
            lastmm = None
            for qi, qb in enumerate((qA, qB)):
                win = qb * BS
                if not odd:
                    chunks = [(win // 128 + cc, 0, 128, cc) for cc in range(4)]
                    chunks.append(((win + 512) // 128, 0, 64, 4))
                else:
                    chunks = [(win // 128, 64, 64, 0)]
                    chunks += [((win + 64) // 128 + cc, 0, 128, cc + 1)
                               for cc in range(4)]
                last_h = (hp // 4) * 8 + 6 + hh
                for ci, (vt_t, pr0, klen, slot) in enumerate(chunks):
                    key = (hp // 4, hh)
                    first = key not in pv_started
                    pv_started.add(key)
                    stop = (h == last_h and qi == 1 and ci == len(chunks) - 1)
                    lastmm = t.matmul(
                        bank[hh * 64:(hh + 1) * 64,
                             fo + qi * 64: fo + (qi + 1) * 64],
                        Vt[pr0:pr0 + klen, vt_t, h * 64:(h + 1) * 64],
                        PTb[pr0:pr0 + klen, hs, slot, qi * 64:(qi + 1) * 64],
                        start=first, stop=stop, tile_position=(pr0, hh * 64))
            cPE.inc(lastmm)
            pv_pe.append(cPE.n)
            st[("pv_pe", pg, h)] = cPE.n

        for h in range(H_):
            def pe_chunk(t, h=h):
                if h == 0:
                    pe_S(t, 0)
                    pe_S(t, 1)
                else:
                    if h + 1 < H_:
                        pe_S(t, h + 1)
            self.on("tensor", pe_chunk)
            if h == 0 and p >= 1:
                self.emit_tail_pe_opj(ha, p - 1, pvp)
            if h == 0:
                self.on("sync", sy_pair)
            if h == 2 and p >= 1:
                self.emit_tail_rest(ha, p - 1, x1p)
                self.emit_tail_sync(ha, p - 1)
            self.on("vector", lambda v, h=h: dv_S(v, h))
            self.on("scalar", lambda s, h=h: ac_exp(s, h))
            self.on("vector", lambda v, h=h: dv_nrm(v, h))
            self.on("tensor", lambda t, h=h: pe_PT(t, h))
            self.on("vector", lambda v, h=h: dv_PT(v, h))
            self.on("tensor", lambda t, h=h: pe_PV(t, h))

        st[("exp_last", pg)] = exp_ac[H_ - 1]
        st[("PT_last", pg)] = PT_pe[H_ - 1]
        st[("PT_ev_last", pg)] = PT_ev[H_ - 1]
        st[("mask_free", pg)] = S_ev[H_ - 1]
        st[("pv_last", pg)] = pv_pe[H_ - 1]

        def dv_attn(v):
            v.wait_ge(cPE.sem, st[("pv_last", pg)])
            v.tensor_copy(attnC[:, s2, 0:4, :],
                          bass.AP(pvp[0], 0, [[512, 128], [128, 4], [1, 128]]))
            cDV.inc(v.tensor_copy(attnC[:, s2, 4:8, :],
                                  bass.AP(pvp[1], 0, [[512, 128], [128, 4], [1, 128]])))
            st[("attn_ev", pg)] = cDV.n

        self.on("vector", dv_attn)

    def tail(self, ha, p, pvp, x1p):
        self.emit_tail_pe_opj(ha, p, pvp)
        self.emit_tail_rest(ha, p, x1p)
        self.emit_tail_sync(ha, p)

    def emit_tail_pe_opj(self, ha, p, pvp):
        nc, st = self.nc, self.st
        D, DT, HALF, BS, NP, POFF = (self.D, self.DT, self.HALF, self.BS,
                                     self.NP, self.POFF)
        cPE, cDV, cAC = self.cPE, self.cDV, self.cAC
        attnC, wTa = self.attnC, self.woT
        pg = ha * NP + p
        s2 = p % 2

        def pe_opj0(t):
            t.wait_ge(cDV.sem, st[("attn_ev", pg)])
            last = None
            for k in range(DT):
                last = t.matmul(pvp[1][:, :], attnC[:, s2, k, :],
                                wTa[:, k, 0:512],
                                start=(k == 0), stop=(k == DT - 1))
            cPE.inc(last)
            st[("opj_pe", pg, 0)] = cPE.n

        def dv_res0(v):
            v.wait_ge(cPE.sem, st[("opj_pe", pg, 0)])
            self.s_x0.wait(v, st[("x0ld", pg)])
            if pg >= 2:
                self.s_x1.wait(v, st[("x1st", pg - 2)])
                v.wait_ge(cPE.sem, st[("x1T", pg - 2)])
                v.wait_ge(cAC.sem, st[("lnsq", pg - 2)])
            cDV.inc(v.tensor_add(self.x1c[:, s2, 0:512], pvp[1][:, :],
                                 self.x0r[:, 0, 0:512]))
            st[("res_dv", pg, 0)] = cDV.n

        def pe_opj1(t):
            t.wait_ge(cDV.sem, st[("res_dv", pg, 0)])
            last = None
            for k in range(DT):
                last = t.matmul(pvp[1][:, :], attnC[:, s2, k, :],
                                wTa[:, k, 512:1024],
                                start=(k == 0), stop=(k == DT - 1))
            cPE.inc(last)
            st[("opj_pe", pg, 1)] = cPE.n

        def dv_res1(v):
            v.wait_ge(cPE.sem, st[("opj_pe", pg, 1)])
            cDV.inc(v.tensor_add(self.x1c[:, s2, 512:1024], pvp[1][:, :],
                                 self.x0r[:, 0, 512:1024]))
            st[("res_dv", pg, 1)] = cDV.n
            v.wait_ge(cDV.sem, cDV.n)
            cDV.inc(v.reduce_sum(self.smal[:, s2, 4:5], self.x1c[:, s2, :],
                                 axis=AX.X))
            st[("lnsum", pg)] = cDV.n

        self.on("tensor", pe_opj0)
        self.on("vector", dv_res0)
        self.on("tensor", pe_opj1)
        self.on("vector", dv_res1)
        st[("x0_free", pg)] = st[("res_dv", pg, 1)]

    def emit_tail_rest(self, ha, p, x1p):
        nc, st = self.nc, self.st
        D, DT, HALF, BS, NP, POFF, TOK = (self.D, self.DT, self.HALF, self.BS,
                                          self.NP, self.POFF, self.TOK)
        cPE, cDV, cAC = self.cPE, self.cDV, self.cAC
        x1c, smal, sqj, X1fm = self.x1c, self.smal, self.sqj, self.X1fm
        pg = ha * NP + p
        s2 = p % 2
        qA = p

        def ac_lnsq(s):
            s.wait_ge(cDV.sem, st[("res_dv", pg, 1)])
            if pg >= 1:
                s.wait_ge(cAC.sem, st[("lnsq", pg - 1)])
            cAC.inc(s.activation(sqj[:, :], x1c[:, s2, :], AF.Square,
                                 accum_out=smal[:, s2, 5:6]))
            st[("lnsq", pg)] = cAC.n

        def dv_ln(v):
            v.wait_ge(cDV.sem, st[("lnsum", pg)])
            v.wait_ge(cAC.sem, st[("lnsq", pg)])
            cDV.inc(v.tensor_scalar_mul(smal[:, s2, 6:7], smal[:, s2, 4:5],
                                        1.0 / D))
            cDV.inc(v.tensor_scalar_mul(smal[:, s2, 7:8], smal[:, s2, 5:6],
                                        1.0 / D))
            v.wait_ge(cDV.sem, cDV.n)
            cDV.inc(v.tensor_mul(smal[:, s2, 8:9], smal[:, s2, 6:7],
                                 smal[:, s2, 6:7]))
            v.wait_ge(cDV.sem, cDV.n)
            cDV.inc(v.tensor_sub(smal[:, s2, 9:10], smal[:, s2, 7:8],
                                 smal[:, s2, 8:9]))
            v.wait_ge(cDV.sem, cDV.n)
            cDV.inc(v.tensor_scalar_add(smal[:, s2, 10:11], smal[:, s2, 9:10],
                                        1e-5))
            st[("lnvar", pg)] = cDV.n

        def ac_sqrt(s):
            s.wait_ge(cDV.sem, st[("lnvar", pg)])
            cAC.inc(s.activation(smal[:, s2, 11:12], smal[:, s2, 10:11],
                                 AF.Sqrt))
            st[("lnstd", pg)] = cAC.n

        def dv_ln2(v):
            v.wait_ge(cAC.sem, st[("lnstd", pg)])
            cDV.inc(v.reciprocal(smal[:, s2, 12:13], smal[:, s2, 11:12]))
            v.wait_ge(cDV.sem, cDV.n)
            cDV.inc(v.tensor_mul(smal[:, s2, 13:14], smal[:, s2, 6:7],
                                 smal[:, s2, 12:13]))
            v.wait_ge(cDV.sem, cDV.n)
            cDV.inc(v.tensor_scalar_mul(smal[:, s2, 14:15], smal[:, s2, 13:14],
                                        -1.0))
            st[("lnab", pg)] = cDV.n

        def ac_apply(s):
            s.wait_ge(cDV.sem, st[("lnab", pg)])
            cAC.inc(s.activation(x1c[:, s2, :], x1c[:, s2, :],
                                 AF.Identity, bias=smal[:, s2, 14:15],
                                 scale=smal[:, s2, 12:13]))
            st[("ln_done", pg)] = cAC.n

        def pe_x1T(t, half4):
            if half4 == 0:
                t.wait_ge(cAC.sem, st[("ln_done", pg)])
                if pg >= 1:
                    t.wait_ge(cDV.sem, st[("x1T_ev", pg - 1, 1)])
            else:
                t.wait_ge(cDV.sem, st[("x1T_ev", pg, 0)])
            last = None
            for c in range(4):
                cc = half4 * 4 + c
                last = t.transpose(x1p[:, c * 128:(c + 1) * 128],
                                   x1c[:, s2, cc * 128:(cc + 1) * 128],
                                   self.idf[:, :])
            cPE.inc(last)
            st[("x1T", pg, half4)] = cPE.n
            if half4 == 1:
                st[("x1T", pg)] = cPE.n

        def dv_x1T(v, half4):
            v.wait_ge(cPE.sem, st[("x1T", pg, half4)])
            dst = bass.AP(X1fm,
                          half4 * 4 * TOK + ha * HALF + qA * BS,
                          [[DT * TOK, 128], [TOK, 4], [POFF * BS, 2],
                           [1, 64]])
            src = bass.AP(x1p, 0, [[512, 128], [128, 4], [64, 2], [1, 64]])
            cDV.inc(v.tensor_copy(dst, src))
            st[("x1T_ev", pg, half4)] = cDV.n
            if half4 == 1:
                st[("x1T_ev", pg)] = cDV.n

        self.on("scalar", ac_lnsq)
        self.on("vector", dv_ln)
        self.on("scalar", ac_sqrt)
        self.on("vector", dv_ln2)
        self.on("scalar", ac_apply)
        self.on("tensor", lambda t: pe_x1T(t, 0))
        self.on("vector", lambda v: dv_x1T(v, 0))
        self.on("tensor", lambda t: pe_x1T(t, 1))
        self.on("vector", lambda v: dv_x1T(v, 1))

    def emit_tail_sync(self, ha, p):
        nc, st = self.nc, self.st
        HALF, BS, NP, POFF = self.HALF, self.BS, self.NP, self.POFF
        cAC = self.cAC
        pg = ha * NP + p
        s2 = p % 2
        qA, qB = p, p + POFF

        def sy_x1(sy):
            sy.wait_ge(cAC.sem, st[("ln_done", pg)])
            r0 = ha * HALF + qA * BS
            r1 = ha * HALF + qB * BS
            self.s_x1.inc(sy.dma_start(self.x1_scr[r0:r0 + 64, :],
                                       self.x1c[0:64, s2, :]), s2)
            st[("x1st", pg)] = self.s_x1.inc(
                sy.dma_start(self.x1_scr[r1:r1 + 64, :],
                             self.x1c[64:128, s2, :]), s2)

        self.on("sync", sy_x1)

    def gates_phase(self):
        nc, st = self.nc, self.st
        D, DT, E, NTT, TOK = self.D, self.DT, self.E, self.NTT, self.TOK
        cPE, cDV, cAC = self.cPE, self.cDV, self.cAC
        X1fm, wgT, iot, gates, gatesT, idf = (self.X1fm, self.wgT, self.iot,
                                              self.gates, self.gatesT, self.idf)
        last_x1 = st[("x1T_ev", 2 * self.NP - 1)]

        with nc.sbuf_tensor("gl", [128, 2, E], F32) as gl, \
             nc.sbuf_tensor("gsc", [128, 2, 8], F32) as gsc, \
             nc.psum_tensor("glp", [128, 2, E], F32) as glp, \
             nc.psum_tensor("gtp", [E, 2, 128], F32) as gtp:
            for tt in range(NTT):
                s2 = tt % 2

                def pe_gl(t, tt=tt, s2=s2):
                    if tt == 0:
                        t.wait_ge(cDV.sem, last_x1)
                        t.wait_ge(cDV.sem, st["wgev"])
                    if tt >= 2:
                        t.wait_ge(cDV.sem, st[("glev", tt - 2)])
                    last = None
                    for k in range(DT):
                        last = t.matmul(glp[:, s2, :],
                                        X1fm[:, k, tt * 128:(tt + 1) * 128],
                                        wgT[:, k, :],
                                        start=(k == 0), stop=(k == DT - 1))
                    cPE.inc(last)
                    st[("glpe", tt)] = cPE.n

                def dv_g1(v, tt=tt, s2=s2):
                    v.wait_ge(cPE.sem, st[("glpe", tt)])
                    if tt == 0:
                        self.s_iden.wait_all(v)
                    cDV.inc(v.tensor_add(gl[:, s2, :],
                                         glp[:, s2, :], iot[:, :]))
                    st[("glev", tt)] = cDV.n
                    v.wait_ge(cDV.sem, cDV.n)
                    cDV.inc(v.reduce_max(gsc[:, s2, 0:1],
                                         gl[:, s2, :], axis=AX.X))
                    v.wait_ge(cDV.sem, cDV.n)
                    cDV.inc(v.tensor_single_scalar(gates[:, tt, :],
                                                   gl[:, s2, :],
                                                   gsc[:, s2, 0:1],
                                                   ALU.is_ge))
                    v.wait_ge(cDV.sem, cDV.n)
                    cDV.inc(v.scalar_tensor_tensor(gl[:, s2, :],
                                                   gates[:, tt, :], NEG,
                                                   gl[:, s2, :],
                                                   op0=ALU.mult, op1=ALU.add))
                    v.wait_ge(cDV.sem, cDV.n)
                    cDV.inc(v.reduce_max(gsc[:, s2, 1:2],
                                         gl[:, s2, :], axis=AX.X))
                    v.wait_ge(cDV.sem, cDV.n)
                    cDV.inc(v.tensor_sub(gsc[:, s2, 2:3],
                                         gsc[:, s2, 0:1],
                                         gsc[:, s2, 1:2]))
                    st[("gd", tt)] = cDV.n

                def ac_g(s, tt=tt, s2=s2):
                    s.wait_ge(cDV.sem, st[("gd", tt)])
                    cAC.inc(s.activation(gsc[:, s2, 3:4],
                                         gsc[:, s2, 2:3], AF.Exp,
                                         scale=-1.0))
                    st[("ge", tt)] = cAC.n

                def dv_g2(v, tt=tt, s2=s2):
                    v.wait_ge(cAC.sem, st[("ge", tt)])
                    cDV.inc(v.tensor_scalar_add(gsc[:, s2, 4:5],
                                                gsc[:, s2, 3:4], 1.0))
                    v.wait_ge(cDV.sem, cDV.n)
                    cDV.inc(v.reciprocal(gsc[:, s2, 5:6],
                                         gsc[:, s2, 4:5]))
                    v.wait_ge(cDV.sem, cDV.n)
                    cDV.inc(v.tensor_mul(gsc[:, s2, 6:7],
                                         gsc[:, s2, 3:4],
                                         gsc[:, s2, 5:6]))
                    v.wait_ge(cDV.sem, cDV.n)
                    cDV.inc(v.tensor_single_scalar(gl[:, s2, :],
                                                   gl[:, s2, :],
                                                   gsc[:, s2, 1:2],
                                                   ALU.is_ge))
                    v.wait_ge(cDV.sem, cDV.n)
                    cDV.inc(v.tensor_scalar_mul(gates[:, tt, :],
                                                gates[:, tt, :],
                                                gsc[:, s2, 5:6]))
                    v.wait_ge(cDV.sem, cDV.n)
                    cDV.inc(v.scalar_tensor_tensor(gates[:, tt, :],
                                                   gl[:, s2, :],
                                                   gsc[:, s2, 6:7],
                                                   gates[:, tt, :],
                                                   op0=ALU.mult, op1=ALU.add))
                    st[("gates_dv", tt)] = cDV.n

                def pe_gT(t, tt=tt, s2=s2):
                    t.wait_ge(cDV.sem, st[("gates_dv", tt)])
                    if tt >= 2:
                        t.wait_ge(cDV.sem, st[("gTev", tt - 2)])
                    cPE.inc(t.transpose(gtp[:, s2, :],
                                        gates[:, tt, :], idf[:, :]))
                    st[("gTpe", tt)] = cPE.n

                def dv_gT(v, tt=tt, s2=s2):
                    v.wait_ge(cPE.sem, st[("gTpe", tt)])
                    cDV.inc(v.tensor_copy(gatesT[:, tt * 128:(tt + 1) * 128],
                                          gtp[:, s2, :]))
                    st[("gTev", tt)] = cDV.n

                self.on("tensor", pe_gl)
                self.on("vector", dv_g1)
                self.on("scalar", ac_g)
                self.on("vector", dv_g2)
                self.on("tensor", pe_gT)
                self.on("vector", dv_gT)

    def ffn(self):
        nc, st = self.nc, self.st
        D, DT, E, NTT, INTER, IC, HALF = (self.D, self.DT, self.E, self.NTT,
                                          self.INTER, self.IC, self.HALF)
        cPE, cDV, cAC = self.cPE, self.cDV, self.cAC
        X1fm, gates, gatesT, b1sb, b2sb = (self.X1fm, self.gates, self.gatesT,
                                           self.b1sb, self.b2sb)
        hbuf, w1sl, w2sl, y_tm = self.hbuf, self.w1sl, self.w2sl, self.y_tm
        HTT = HALF // 128
        TCH = HALF // 512
        TQ = 2
        NG = IC * 4 * TCH

        pe_at_ffn = cPE.n
        ac_at_ffn = cAC.n
        dv_at_ffn = cDV.n

        def sy_yinit(sy):
            self.s_x1.wait_all(sy)
            sy.wait_ge(cPE.sem, pe_at_ffn)
            sy.wait_ge(cDV.sem, dv_at_ffn)
            sy.wait_ge(cAC.sem, ac_at_ffn)
            for tt in range(NTT):
                self.s_yin.inc(sy.dma_start(
                    y_tm[:, tt, :],
                    self.x1_scr[tt * 128:(tt + 1) * 128, :]))

        self.on("sync", sy_yinit)

        with nc.psum_tensor("mp", [128, 8, 512], F32) as mp:

            for ei in range(E):
                for tq in range(TQ):
                    pidx = ei * TQ + tq

                    for c in range(IC):
                        idx = pidx * IC + c

                        def gp_w1(gp, ei=ei, c=c, idx=idx):
                            if idx >= 2:
                                gp.wait_ge(cPE.sem, st[("mm1cd", idx - 2)])
                            st[("w1ld", idx)] = self.g_w1.inc(gp.dma_start(
                                w1sl[:, idx % 2, :, :],
                                bass.AP(self.w1, ei * D * INTER + c * 512,
                                        [[INTER, 128], [128 * INTER, DT],
                                         [1, 512]])), idx % 2)

                        self.on("gpsimd", gp_w1)
                        for it in range(4):
                            def pe_mm1(t, tq=tq, pidx=pidx, c=c, idx=idx, it=it):
                                if it == 0:
                                    self.g_w1.wait(t, st[("w1ld", idx)])
                                for tch in range(TCH):
                                    g = (c * 4 + it) * TCH + tch
                                    slot = g % 2
                                    if g >= 2:
                                        t.wait_ge(cAC.sem, st[("gelu", pidx, g - 2)])
                                    elif pidx >= 1:
                                        t.wait_ge(cAC.sem,
                                                  st[("gelu", pidx - 1, NG - 2 + g)])
                                        t.wait_ge(cDV.sem,
                                                  st[("mm2ev", (pidx - 1) * 2 + 1,
                                                      slot)])
                                    else:
                                        t.wait_ge(cDV.sem, dv_at_ffn)
                                    last = None
                                    for k in range(DT):
                                        last = t.matmul(
                                            mp[:, slot, :],
                                            w1sl[:, idx % 2, k,
                                                 it * 128:(it + 1) * 128],
                                            X1fm[:, k, tq * HALF + tch * 512:
                                                 tq * HALF + (tch + 1) * 512],
                                            start=(k == 0), stop=(k == DT - 1))
                                    cPE.inc(last)
                                    st[("mm1pe", pidx, g)] = cPE.n
                                if it == 3:
                                    st[("mm1cd", idx)] = cPE.n

                            def ac_gelu(s, ei=ei, pidx=pidx, c=c, it=it):
                                for tch in range(TCH):
                                    g = (c * 4 + it) * TCH + tch
                                    slot = g % 2
                                    s.wait_ge(cPE.sem, st[("mm1pe", pidx, g)])
                                    if g == 0 and pidx == 0:
                                        self.g_bias.wait_all(s)
                                    if g == 0 and pidx >= 1:
                                        s.wait_ge(cPE.sem,
                                                  st[("mm2alldone", pidx - 1)])
                                    cAC.inc(s.activation(
                                        hbuf[:, c * 4 + it,
                                             tch * 512:(tch + 1) * 512],
                                        mp[:, slot, :], AF.Gelu,
                                        bias=b1sb[:, ei, c * 4 + it:
                                                  c * 4 + it + 1]))
                                    st[("gelu", pidx, g)] = cAC.n

                            self.on("tensor", pe_mm1)
                            self.on("scalar", ac_gelu)

                    for Hh in range(2):
                        for c in range(IC):
                            idx2 = (pidx * 2 + Hh) * IC + c

                            def gp_w2(gp, ei=ei, Hh=Hh, c=c, idx2=idx2):
                                if idx2 >= 2:
                                    gp.wait_ge(cPE.sem, st[("mm2cd", idx2 - 2)])
                                st[("w2ld", idx2)] = self.g_w2.inc(gp.dma_start(
                                    w2sl[:, idx2 % 2, :, :],
                                    bass.AP(self.w2,
                                            ei * INTER * D + c * 512 * D + Hh * 512,
                                            [[D, 128], [128 * D, 4], [1, 512]])),
                                    idx2 % 2)

                            def pe_mm2(t, pidx=pidx, Hh=Hh, c=c, idx2=idx2):
                                self.g_w2.wait(t, st[("w2ld", idx2)])
                                lastc = None
                                for tt in range(HTT):
                                    if c == 0:
                                        prev = st.get(("mm2ev",
                                                       pidx * 2 + Hh - 1, tt))
                                        if prev is not None:
                                            t.wait_ge(cDV.sem, prev)
                                        elif pidx == 0 and Hh == 0:
                                            t.wait_ge(cDV.sem, dv_at_ffn)
                                        if Hh == 0 and tt < 2:
                                            t.wait_ge(cAC.sem,
                                                      st[("gelu", pidx,
                                                          NG - 2 + tt)])
                                    last = None
                                    for k in range(4):
                                        h_it = c * 4 + k
                                        t.wait_ge(cAC.sem,
                                                  st[("gelu", pidx,
                                                      (h_it + 1) * TCH - 1)])
                                        last = t.matmul(
                                            mp[:, tt, :],
                                            hbuf[:, h_it,
                                                 tt * 128:(tt + 1) * 128],
                                            w2sl[:, idx2 % 2, k, :],
                                            start=(c == 0 and k == 0),
                                            stop=(c == IC - 1 and k == 3))
                                    if c == IC - 1:
                                        cPE.inc(last)
                                        st[("mm2pe", pidx * 2 + Hh, tt)] = cPE.n
                                    lastc = last
                                if c != IC - 1:
                                    cPE.inc(lastc)
                                st[("mm2cd", idx2)] = cPE.n
                                if Hh == 1 and c == IC - 1:
                                    st[("mm2alldone", pidx)] = cPE.n

                            self.on("gpsimd", gp_w2)
                            self.on("tensor", pe_mm2)

                        def dv_mm2(v, ei=ei, tq=tq, pidx=pidx, Hh=Hh):
                            for tt in range(HTT):
                                v.wait_ge(cPE.sem, st[("mm2pe", pidx * 2 + Hh, tt)])
                                if ei == 0:
                                    self.s_yin.wait_all(v)
                                    v.wait_ge(cDV.sem, st[("gates_dv",
                                                           tq * HTT + tt)])
                                gtt = tq * HTT + tt
                                cDV.inc(v.scalar_tensor_tensor(
                                    y_tm[:, gtt, Hh * 512:(Hh + 1) * 512],
                                    mp[:, tt, :],
                                    gates[:, gtt, ei:ei + 1],
                                    y_tm[:, gtt, Hh * 512:(Hh + 1) * 512],
                                    op0=ALU.mult, op1=ALU.add))
                                st[("mm2ev", pidx * 2 + Hh, tt)] = cDV.n

                        self.on("vector", dv_mm2)

            for tt in range(NTT):
                tq, ltt = divmod(tt, HTT)
                for ch in range(2):
                    fi = tt * 2 + ch
                    slot = fi % 2

                    def pe_b2(t, tt=tt, ch=ch, fi=fi, slot=slot):
                        if fi == 0:
                            t.wait_ge(cDV.sem, st[("gTev", NTT - 1)])
                            self.g_bias.wait_all(t)
                        if fi >= 2:
                            t.wait_ge(cDV.sem, st[("b2ev", fi - 2)])
                        else:
                            t.wait_ge(cDV.sem,
                                      st[("mm2ev", (E * TQ - 1) * 2 + 1, slot)])
                        cPE.inc(t.matmul(
                            mp[:, slot, :],
                            self.gatesT[:, tt * 128:(tt + 1) * 128],
                            b2sb[:, ch * 512:(ch + 1) * 512],
                            start=True, stop=True))
                        st[("b2pe", fi)] = cPE.n

                    def dv_b2(v, tt=tt, ch=ch, fi=fi, slot=slot, tq=tq, ltt=ltt):
                        v.wait_ge(cPE.sem, st[("b2pe", fi)])
                        v.wait_ge(cDV.sem,
                                  st[("mm2ev",
                                      ((E - 1) * TQ + tq) * 2 + ch, ltt)])
                        cDV.inc(v.tensor_add(
                            y_tm[:, tt, ch * 512:(ch + 1) * 512],
                            y_tm[:, tt, ch * 512:(ch + 1) * 512],
                            mp[:, slot, :]))
                        st[("b2ev", fi)] = cDV.n
                        if ch == 1:
                            st[("b2evt", tt)] = cDV.n

                    self.on("tensor", pe_b2)
                    self.on("vector", dv_b2)

    def ln2(self):
        nc, st = self.nc, self.st
        D, NTT = self.D, self.NTT
        cDV, cAC = self.cDV, self.cAC
        y_tm, outst, lnt2, sqj2 = self.y_tm, self.outst, self.lnt2, self.sqj2

        for tt in range(NTT):
            s2 = tt % 2

            def dv_a(v, tt=tt, s2=s2):
                v.wait_ge(cDV.sem, st[("b2evt", tt)])
                cDV.inc(v.reduce_sum(lnt2[:, s2 * 8 + 0: s2 * 8 + 1],
                                     y_tm[:, tt, :], axis=AX.X))
                st[("l2sum", tt)] = cDV.n

            def ac_a(s, tt=tt, s2=s2):
                s.wait_ge(cDV.sem, st[("l2sum", tt)])
                if tt >= 2:
                    s.wait_ge(cAC.sem, st[("l2done", tt - 2)])
                cAC.inc(s.activation(sqj2[:, :], y_tm[:, tt, :],
                                     AF.Square,
                                     accum_out=lnt2[:, s2 * 8 + 1: s2 * 8 + 2]))
                st[("l2sq", tt)] = cAC.n

            def dv_b(v, tt=tt, s2=s2):
                o = s2 * 8
                v.wait_ge(cAC.sem, st[("l2sq", tt)])
                cDV.inc(v.tensor_scalar_mul(lnt2[:, o + 2:o + 3],
                                            lnt2[:, o + 0:o + 1], 1.0 / D))
                cDV.inc(v.tensor_scalar_mul(lnt2[:, o + 3:o + 4],
                                            lnt2[:, o + 1:o + 2], 1.0 / D))
                v.wait_ge(cDV.sem, cDV.n)
                cDV.inc(v.tensor_mul(lnt2[:, o + 4:o + 5], lnt2[:, o + 2:o + 3],
                                     lnt2[:, o + 2:o + 3]))
                v.wait_ge(cDV.sem, cDV.n)
                cDV.inc(v.tensor_sub(lnt2[:, o + 5:o + 6], lnt2[:, o + 3:o + 4],
                                     lnt2[:, o + 4:o + 5]))
                v.wait_ge(cDV.sem, cDV.n)
                cDV.inc(v.tensor_scalar_add(lnt2[:, o + 6:o + 7],
                                            lnt2[:, o + 5:o + 6], 1e-5))
                st[("l2var", tt)] = cDV.n

            def ac_b(s, tt=tt, s2=s2):
                s.wait_ge(cDV.sem, st[("l2var", tt)])
                cAC.inc(s.activation(lnt2[:, s2 * 8 + 7: s2 * 8 + 8],
                                     lnt2[:, s2 * 8 + 6: s2 * 8 + 7], AF.Sqrt))
                st[("l2std", tt)] = cAC.n

            def dv_c(v, tt=tt, s2=s2):
                o = s2 * 8
                v.wait_ge(cAC.sem, st[("l2std", tt)])
                cDV.inc(v.reciprocal(lnt2[:, o + 4:o + 5], lnt2[:, o + 7:o + 8]))
                v.wait_ge(cDV.sem, cDV.n)
                cDV.inc(v.tensor_mul(lnt2[:, o + 5:o + 6], lnt2[:, o + 2:o + 3],
                                     lnt2[:, o + 4:o + 5]))
                v.wait_ge(cDV.sem, cDV.n)
                cDV.inc(v.tensor_scalar_mul(lnt2[:, o + 6:o + 7],
                                            lnt2[:, o + 5:o + 6], -1.0))
                st[("l2ab", tt)] = cDV.n

            def ac_c(s, tt=tt, s2=s2):
                s.wait_ge(cDV.sem, st[("l2ab", tt)])
                if tt >= 2:
                    self.s_out.wait(s, st[("outst", tt - 2)])
                cAC.inc(s.activation(outst[:, s2, :],
                                     y_tm[:, tt, :], AF.Identity,
                                     bias=lnt2[:, s2 * 8 + 6: s2 * 8 + 7],
                                     scale=lnt2[:, s2 * 8 + 4: s2 * 8 + 5]))
                st[("l2done", tt)] = cAC.n

            def sy_out(sy, tt=tt, s2=s2):
                sy.wait_ge(cAC.sem, st[("l2done", tt)])
                st[("outst", tt)] = self.s_out.inc(
                    sy.dma_start(self.yout[tt * 128:(tt + 1) * 128, :],
                                 outst[:, s2, :]), s2)

            self.on("vector", dv_a)
            self.on("scalar", ac_a)
            self.on("vector", dv_b)
            self.on("scalar", ac_b)
            self.on("vector", dv_c)
            self.on("scalar", ac_c)
            self.on("sync", sy_out)

        def sy_fin(sy):
            self.s_out.wait_all(sy)

        self.on("sync", sy_fin)


_CACHE = {}


def _masks(TOK, BS=64, WIN=512):
    WINW = WIN + BS
    HALF = TOK // 2
    HQB = HALF // BS
    POFF = HQB // 2
    NP = POFF
    WB = WIN // BS
    out = []
    qi = np.arange(BS)[:, None]
    ki = np.arange(WINW)[None, :]
    wi = ki // BS
    kwi = ki % BS
    for parity in range(2):
        m = np.zeros((2 * NP, 128, WINW), np.float32)
        for ha in range(2):
            for r in range(NP):
                for half_row, qb in ((0, r), (1, r + POFF)):
                    g = ha * HQB + qb
                    g_seq = g + parity * (TOK // BS)
                    qpos = g_seq * BS + qi
                    kpos = (g_seq + wi - WB) * BS + kwi
                    valid = ((g_seq + wi - WB) >= 0) & (kpos <= qpos) & \
                            (qpos - kpos < WIN)
                    m[ha * NP + r, half_row * 64:(half_row + 1) * 64, :] = \
                        np.where(valid, 0.0, NEG).astype(np.float32)
        out.append(m)
    return out


def kernel(L0_attn_x, L0_attn_Wq, L0_attn_Wk, L0_attn_Wv, L0_attn_Wo,
           L0_ffn_Wg, L0_ffn_W1, L0_ffn_b1, L0_ffn_W2, L0_ffn_b2):
    TOK, D, E, INTER = 2048, 1024, 8, 4096
    x = np.ascontiguousarray(np.asarray(L0_attn_x, dtype=np.float32)).reshape(-1, D)
    NCORE = 8
    if "nc" not in _CACHE:
        _CACHE["nc"] = Builder(TOK=TOK, D=D, E=E, INTER=INTER).build()
        _CACHE["masks"] = _masks(TOK)
    nc = _CACHE["nc"]
    mev, mod = _CACHE["masks"]

    idenb = np.eye(128, dtype=ml_dtypes.bfloat16)
    idenf = np.eye(128, dtype=np.float32)
    ioteps = np.broadcast_to((np.arange(E) * -1e-6).astype(np.float32),
                             (128, E)).copy()
    common = dict(
        wq=np.ascontiguousarray(L0_attn_Wq, np.float32),
        wk=np.ascontiguousarray(L0_attn_Wk, np.float32),
        wv=np.ascontiguousarray(L0_attn_Wv, np.float32),
        wo=np.ascontiguousarray(L0_attn_Wo, np.float32),
        wg=np.ascontiguousarray(L0_ffn_Wg, np.float32),
        w1=np.ascontiguousarray(L0_ffn_W1, np.float32),
        b1=np.ascontiguousarray(
            np.asarray(L0_ffn_b1, np.float32).reshape(E, INTER // 128, 128)
            .transpose(2, 0, 1)),
        w2=np.ascontiguousarray(L0_ffn_W2, np.float32),
        b2=np.ascontiguousarray(L0_ffn_b2, np.float32),
        idenb=idenb, idenf=idenf, ioteps=ioteps,
    )
    in_maps = []
    for c in range(NCORE):
        xm = x[c * TOK:(c + 1) * TOK]
        xh = (np.zeros((512, D), np.float32) if c % 2 == 0
              else x[c * TOK - 512: c * TOK])
        in_maps.append(dict(common, x_main=np.ascontiguousarray(xm),
                            x_halo=np.ascontiguousarray(xh),
                            mask=(mev if c % 2 == 0 else mod)))
    res = run_bass_kernel_spmd(nc, in_maps, core_ids=list(range(NCORE)))
    out = np.concatenate([r["yout"] for r in res.results], axis=0)
    return out.reshape(4, 4096, 1024).astype(np.float32)


# version 06
# speedup vs baseline: 1.0331x; 1.0331x over previous
import numpy as np
import ml_dtypes
from contextlib import ExitStack

import concourse.bass as bass
import concourse.mybir as mybir
from concourse.bass_utils import run_bass_kernel_spmd

F32 = mybir.dt.float32
BF16 = mybir.dt.bfloat16
AF = mybir.ActivationFunctionType
ALU = mybir.AluOpType
AX = mybir.AxisListType

NEG = -1.0e9


class Ctr:

    def __init__(self, nc, ctx, name, step=1):
        self.sem = ctx.enter_context(nc.semaphore(name))
        self.step = step
        self.n = 0

    def inc(self, inst):
        inst.then_inc(self.sem, self.step)
        self.n += self.step
        return self.n


class DStream:

    def __init__(self, nc, ctx, name, depth=1):
        self.sems = [ctx.enter_context(nc.semaphore(f"{name}_{i}"))
                     for i in range(depth)]
        self.n = [0] * depth

    def inc(self, inst, slot=0):
        inst.then_inc(self.sems[slot], 16)
        self.n[slot] += 16
        return (slot, self.n[slot])

    def wait(self, eng, tok):
        eng.wait_ge(self.sems[tok[0]], tok[1])

    def wait_all(self, eng):
        for i, s in enumerate(self.sems):
            if self.n[i]:
                eng.wait_ge(s, self.n[i])


class _TE:

    def __init__(self, t):
        self._t = t

    def __getattr__(self, n):
        return getattr(self._t, n)

    def matmul(self, *a, **kw):
        kw.setdefault("skip_group_check", True)
        return self._t.matmul(*a, **kw)

    def transpose(self, out, in_, identity, tile_position=None):
        return self._t.matmul(out, in_, identity, start=True, stop=True,
                              is_transpose=True, skip_group_check=True,
                              tile_position=tile_position)


class Builder:
    def __init__(self, TOK=2048, D=1024, H=16, BS=64, WIN=512, E=8, INTER=4096):
        self.TOK, self.D, self.H, self.BS, self.WIN = TOK, D, H, BS, WIN
        self.E, self.INTER = E, INTER
        self.HALO = WIN
        self.HALF = TOK // 2
        self.HQB = self.HALF // BS
        self.POFF = self.HQB // 2
        assert self.POFF % 2 == 0
        self.NP = self.POFF
        self.KV = self.HALF + self.HALO
        self.KVT = self.KV // 128
        self.NTT = TOK // 128
        self.DT = D // 128
        self.WINW = WIN + BS
        self.IC = INTER // 512
        self.ITL = INTER // 128
        self.NPAIR_T = 2 * self.NP
        self.st = {}

    def build(self):
        TOK, D, E, INTER = self.TOK, self.D, self.E, self.INTER
        DT, ITL, WINW = self.DT, self.ITL, self.WINW
        nc = self.nc = bass.Bass("TRN2", target_bir_lowering=False)

        self.x_main = nc.dram_tensor("x_main", [TOK, D], F32, kind="ExternalInput")
        self.x_halo = nc.dram_tensor("x_halo", [self.HALO, D], F32, kind="ExternalInput")
        self.w_in = [nc.dram_tensor(n, [D, D], F32, kind="ExternalInput")
                     for n in ("wq", "wk", "wv", "wo")]
        self.wg = nc.dram_tensor("wg", [E, D], F32, kind="ExternalInput")
        self.w1 = nc.dram_tensor("w1", [E, D, INTER], F32, kind="ExternalInput")
        self.b1 = nc.dram_tensor("b1", [128, E, INTER // 128], F32,
                                 kind="ExternalInput")
        self.w2 = nc.dram_tensor("w2", [E, INTER, D], F32, kind="ExternalInput")
        self.b2 = nc.dram_tensor("b2", [E, D], F32, kind="ExternalInput")
        self.mask = nc.dram_tensor("mask", [self.NPAIR_T, 128, WINW], BF16,
                                   kind="ExternalInput")
        self.idenb_d = nc.dram_tensor("idenb", [128, 128], BF16, kind="ExternalInput")
        self.idenf_d = nc.dram_tensor("idenf", [128, 128], F32, kind="ExternalInput")
        self.iot_d = nc.dram_tensor("ioteps", [128, E], F32, kind="ExternalInput")
        self.yout = nc.dram_tensor("yout", [TOK, D], F32, kind="ExternalOutput")
        self.wT_scr = [nc.dram_tensor(f"wT{i}_scr", [DT, 128, D], BF16, kind="Internal")
                       for i in range(4)]
        self.x1_scr = nc.dram_tensor("x1_scr", [TOK, D], F32, kind="Internal")

        ctx = ExitStack()
        with ctx:
            e = ctx.enter_context
            self.X1fm = e(nc.sbuf_tensor("X1fm", [128, DT, TOK], BF16))
            self.idb = e(nc.sbuf_tensor("idb", [128, 128], BF16))
            self.idf = e(nc.sbuf_tensor("idf", [128, 128], F32))
            self.b1sb = e(nc.sbuf_tensor("b1sb", [128, E, ITL], F32))
            self.b2sb = e(nc.sbuf_tensor("b2sb", [E, D], BF16))
            self.wgT = e(nc.sbuf_tensor("wgT", [128, DT, E], BF16))
            self.iot = e(nc.sbuf_tensor("iot", [128, E], F32))
            self.gates = e(nc.sbuf_tensor("gates", [128, self.NTT, E], F32))
            self.gatesT = e(nc.sbuf_tensor("gatesT", [E, TOK], BF16))
            self.w1sl = e(nc.sbuf_tensor("w1sl", [128, 2, DT, 512], BF16))
            self.w2sl = e(nc.sbuf_tensor("w2sl", [128, 2, 4, 512], BF16))

            self.cPE = Ctr(nc, ctx, "cPE")
            self.cDV = Ctr(nc, ctx, "cDV")
            self.cAC = Ctr(nc, ctx, "cAC")
            self.g_wld = DStream(nc, ctx, "g_wld", 2)
            self.g_wg = DStream(nc, ctx, "g_wg")
            self.g_bias = DStream(nc, ctx, "g_bias")
            self.g_xld = DStream(nc, ctx, "g_xld", 2)
            self.g_wTa = DStream(nc, ctx, "g_wTa")
            self.g_woT = DStream(nc, ctx, "g_woT")
            self.g_w1 = DStream(nc, ctx, "g_w1", 2)
            self.g_w2 = DStream(nc, ctx, "g_w2", 2)
            self.s_iden = DStream(nc, ctx, "s_iden")
            self.s_p0st = DStream(nc, ctx, "s_p0st", 2)
            self.s_mask = DStream(nc, ctx, "s_mask", 2)
            self.s_x0 = DStream(nc, ctx, "s_x0", 1)
            self.s_x1 = DStream(nc, ctx, "s_x1", 2)
            self.s_yin = DStream(nc, ctx, "s_yin")
            self.s_out = DStream(nc, ctx, "s_out", 2)

            self.block = e(nc.Block())

            self.p0(ctx)
            with nc.sbuf_tensor("Xfm", [128, DT, self.KV], BF16) as Xfm, \
                 nc.sbuf_tensor("Qf", [128, DT, self.HALF], BF16) as Qf, \
                 nc.sbuf_tensor("Kf", [128, DT, self.KV], BF16) as Kf, \
                 nc.sbuf_tensor("Vt", [128, self.KVT, D], BF16) as Vt, \
                 nc.sbuf_tensor("wTa", [128, DT, D], BF16) as wTa, \
                 nc.sbuf_tensor("xld", [128, 2, D], BF16) as xld, \
                 nc.sbuf_tensor("x0r", [128, 1, D], F32) as x0r, \
                 nc.sbuf_tensor("Pbf", [128, 2, WINW], BF16) as Pbf, \
                 nc.sbuf_tensor("PTb", [128, 2, 5, 128], BF16) as PTb, \
                 nc.sbuf_tensor("attnC", [128, 2, DT, 128], BF16) as attnC, \
                 nc.sbuf_tensor("x1c", [128, 2, D], F32) as x1c, \
                 nc.sbuf_tensor("maskb", [128, 2, WINW], BF16) as maskb, \
                 nc.sbuf_tensor("smal", [128, 2, 16], F32) as smal, \
                 nc.sbuf_tensor("sqj", [128, D], BF16) as sqj:
                self.Xfm, self.Qf, self.Kf, self.Vt = Xfm, Qf, Kf, Vt
                self.wTa, self.woT, self.xld, self.x0r = wTa, wTa, xld, x0r
                self.Pbf, self.PTb, self.attnC = Pbf, PTb, attnC
                self.x1c, self.maskb, self.smal, self.sqj = x1c, maskb, smal, sqj
                for ha in range(2):
                    self.half(ha)

            self.gates_phase()
            with nc.sbuf_tensor("hbuf", [128, ITL, self.HALF], BF16) as hbuf, \
                 nc.sbuf_tensor("y_tm", [128, self.NTT, D], F32) as y_tm:
                self.hbuf, self.y_tm = hbuf, y_tm
                self.ffn()
                with nc.sbuf_tensor("outst", [128, 2, D], F32) as outst, \
                     nc.sbuf_tensor("lnt2", [128, 16], F32) as lnt2, \
                     nc.sbuf_tensor("sqj2", [128, D], BF16) as sqj2:
                    self.outst, self.lnt2, self.sqj2 = outst, lnt2, sqj2
                    self.ln2()
        return nc

    def on(self, eng, fn):
        if eng == "tensor":
            getattr(self.block, eng)(lambda t: fn(_TE(t)))
        else:
            getattr(self.block, eng)(fn)

    def p0(self, ctx):
        nc, st = self.nc, self.st
        DT, D, E, ITL, INTER = self.DT, self.D, self.E, self.ITL, self.INTER
        cPE, cDV = self.cPE, self.cDV

        with nc.sbuf_tensor("wld", [128, 2, D], BF16) as wld, \
             nc.sbuf_tensor("wstg", [128, 2, DT, 128], BF16) as wstg, \
             nc.sbuf_tensor("wgld", [E, D], BF16) as wgld, \
             nc.psum_tensor("wtp", [128, 2, D], BF16) as wtp, \
             nc.psum_tensor("wgp", [128, DT * E], BF16) as wgp:

            def sy0(sy):
                self.s_iden.inc(sy.dma_start(self.idb[:, :], self.idenb_d[:, :]))
                self.s_iden.inc(sy.dma_start(self.idf[:, :], self.idenf_d[:, :]))
                self.s_iden.inc(sy.dma_start(self.iot[:, :], self.iot_d[:, :]))

            def gp0a(gp):
                st["wgld"] = self.g_wg.inc(gp.dma_start(wgld[:, :], self.wg[:, :]))
                self.g_bias.inc(gp.dma_start(self.b1sb[:, :, :], self.b1[:, :, :]))
                self.g_bias.inc(gp.dma_start(self.b2sb[:, :], self.b2[:, :]))

            self.on("sync", sy0)
            self.on("gpsimd", gp0a)

            for i in range(4 * DT):
                wi, rt = divmod(i, DT)
                s = i % 2

                def gp0(gp, i=i, wi=wi, rt=rt, s=s):
                    if i >= 2:
                        gp.wait_ge(cPE.sem, st[("p0tp", i - 2)])
                    st[("p0ld", i)] = self.g_wld.inc(gp.dma_start(
                        wld[:, s, :],
                        self.w_in[wi][rt * 128:(rt + 1) * 128, :]), s)

                def pe0(t, i=i, s=s):
                    if i == 0:
                        self.s_iden.wait_all(t)
                    self.g_wld.wait(t, st[("p0ld", i)])
                    if i >= 2:
                        t.wait_ge(cDV.sem, st[("p0ev", i - 2)])
                    last = None
                    for c in range(DT):
                        last = t.transpose(
                            wtp[:, s, c * 128:(c + 1) * 128],
                            wld[:, s, c * 128:(c + 1) * 128],
                            self.idb[:, :])
                    cPE.inc(last)
                    st[("p0tp", i)] = cPE.n

                def dv0(v, i=i, s=s):
                    v.wait_ge(cPE.sem, st[("p0tp", i)])
                    if i >= 2:
                        self.s_p0st.wait(v, st[("p0st", i - 2)])
                    cDV.inc(v.tensor_copy(
                        wstg[:, s, :, :],
                        bass.AP(wtp, s * D, [[2 * D, 128], [128, DT], [1, 128]])))
                    st[("p0ev", i)] = cDV.n

                def sy0b(sy, i=i, wi=wi, rt=rt, s=s):
                    sy.wait_ge(cDV.sem, st[("p0ev", i)])
                    st[("p0st", i)] = self.s_p0st.inc(sy.dma_start(
                        bass.AP(self.wT_scr[wi], rt * 128,
                                [[D, 128], [128 * D, DT], [1, 128]]),
                        wstg[:, s, :, :]), s)

                self.on("gpsimd", gp0)
                self.on("tensor", pe0)
                self.on("vector", dv0)
                self.on("sync", sy0b)

            def pe0g(t):
                self.g_wg.wait(t, st["wgld"])
                last = None
                for c in range(DT):
                    last = t.transpose(wgp[:, c * E:(c + 1) * E],
                                       wgld[:, c * 128:(c + 1) * 128],
                                       self.idb[0:E, 0:E])
                cPE.inc(last)
                st["wgtp"] = cPE.n

            def dv0g(v):
                v.wait_ge(cPE.sem, st["wgtp"])
                cDV.inc(v.tensor_copy(
                    self.wgT[:, :, :],
                    bass.AP(wgp, 0, [[DT * E, 128], [E, DT], [1, E]])))
                st["wgev"] = cDV.n

            self.on("tensor", pe0g)
            self.on("vector", dv0g)

    def half(self, ha):
        nc, st = self.nc, self.st
        D, DT, KV, KVT, HALF, HALO = self.D, self.DT, self.KV, self.KVT, self.HALF, self.HALO
        BS, NP, POFF, WINW, TOK = self.BS, self.NP, self.POFF, self.WINW, self.TOK
        cPE, cDV, cAC = self.cPE, self.cDV, self.cAC
        Xfm, Qf, Kf, Vt, wTa, woT = self.Xfm, self.Qf, self.Kf, self.Vt, self.wTa, self.woT
        xld, x0r = self.xld, self.x0r

        with nc.psum_tensor(f"xtp{ha}", [128, D], BF16) as xtp, \
             nc.psum_tensor(f"prj{ha}", [128, 2, 512], F32) as prj:

            def kv_src(kvt):
                base = ha * HALF - HALO + kvt * 128
                if base < 0:
                    return (self.x_halo, base + HALO)
                return (self.x_main, base)

            for kvt in range(KVT):
                def gp_x(gp, kvt=kvt):
                    if kvt >= 2:
                        gp.wait_ge(cPE.sem, st[("xT", ha, kvt - 2)])
                    elif ha == 1:
                        gp.wait_ge(cPE.sem, st[("xT", 0, KVT - 2 + kvt)])
                    else:
                        gp.wait_ge(cPE.sem, st["wgtp"])
                        self.s_p0st.wait_all(gp)
                    src, r0 = kv_src(kvt)
                    st[("xld", ha, kvt)] = self.g_xld.inc(
                        gp.dma_start(xld[:, kvt % 2, :],
                                     src[r0:r0 + 128, :]), kvt % 2)

                def pe_xT(t, kvt=kvt):
                    if ha == 0 and kvt == 0:
                        t.wait_ge(cDV.sem, st["wgev"])
                    self.g_xld.wait(t, st[("xld", ha, kvt)])
                    if kvt >= 1:
                        t.wait_ge(cDV.sem, st[("xTe", ha, kvt - 1)])
                    elif ha == 1:
                        t.wait_ge(cDV.sem, st[("xTe", 0, KVT - 1)])
                    last = None
                    for c in range(DT):
                        last = t.transpose(xtp[:, c * 128:(c + 1) * 128],
                                           xld[:, kvt % 2, c * 128:(c + 1) * 128],
                                           self.idb[:, :])
                    cPE.inc(last)
                    st[("xT", ha, kvt)] = cPE.n

                def dv_xT(v, kvt=kvt):
                    v.wait_ge(cPE.sem, st[("xT", ha, kvt)])
                    if ha == 1 and kvt == 0:
                        v.wait_ge(cPE.sem, st[("proj_done", 0, 2)])
                    elif ha == 0 and kvt == 0:
                        self.s_p0st.wait_all(v)
                    cDV.inc(v.tensor_copy(
                        Xfm[:, :, kvt * 128:(kvt + 1) * 128],
                        bass.AP(xtp, 0, [[D, 128], [128, DT], [1, 128]])))
                    st[("xTe", ha, kvt)] = cDV.n

                self.on("gpsimd", gp_x)
                self.on("tensor", pe_xT)
                self.on("vector", dv_xT)

            plan = []
            QCH, KCH = HALF // 512, KV // 512
            for dq in range(DT):
                for ch in range(QCH):
                    plan.append((0, "q", dq, ch))
            for dq in range(DT):
                for ch in range(KCH):
                    plan.append((1, "k", dq, ch))
            for kvt in range(KVT):
                for ch in range(2):
                    plan.append((2, "v", kvt, ch))
            NPL = len(plan)

            for i, (wi, kind, a, chn) in enumerate(plan):
                if i == 0 or plan[i - 1][0] != wi:
                    def gp_w(gp, wi=wi):
                        if (ha, wi) == (0, 0):
                            self.s_p0st.wait_all(gp)
                        elif wi == 0:
                            gp.wait_ge(cPE.sem, st[("opj_pe", NP - 1, 1)])
                        else:
                            gp.wait_ge(cPE.sem, st[("proj_done", ha, wi - 1)])
                        st[("wT", ha, wi)] = self.g_wTa.inc(gp.dma_start(
                            wTa[:, :, :],
                            bass.AP(self.wT_scr[wi], 0,
                                    [[D, 128], [128 * D, DT], [1, D]])))

                    self.on("gpsimd", gp_w)

                def pe_proj(t, i=i, wi=wi, kind=kind, a=a, chn=chn):
                    if i == 0 or plan[i - 1][0] != wi:
                        self.g_wTa.wait(t, st[("wT", ha, wi)])
                    slot = i % 2
                    if i >= 2:
                        t.wait_ge(cDV.sem, st[("prj_ev", ha, i - 2)])
                    elif ha == 1:
                        t.wait_ge(cDV.sem, st[("prj_ev", 0, NPL - 2 + i)])
                    last = None
                    if kind == "q":
                        t.wait_ge(cDV.sem,
                                  st[("xTe", ha, HALO // 128 + (chn + 1) * 4 - 1)])
                        for k in range(DT):
                            last = t.matmul(
                                prj[:, slot, :],
                                wTa[:, k, a * 128:(a + 1) * 128],
                                Xfm[:, k, HALO + chn * 512: HALO + (chn + 1) * 512],
                                start=(k == 0), stop=(k == DT - 1))
                    elif kind == "k":
                        t.wait_ge(cDV.sem, st[("xTe", ha, (chn + 1) * 4 - 1)])
                        for k in range(DT):
                            last = t.matmul(
                                prj[:, slot, :],
                                wTa[:, k, a * 128:(a + 1) * 128],
                                Xfm[:, k, chn * 512: (chn + 1) * 512],
                                start=(k == 0), stop=(k == DT - 1))
                    else:
                        t.wait_ge(cDV.sem, st[("xTe", ha, a)])
                        for k in range(DT):
                            last = t.matmul(
                                prj[:, slot, :],
                                Xfm[:, k, a * 128:(a + 1) * 128],
                                wTa[:, k, chn * 512: (chn + 1) * 512],
                                start=(k == 0), stop=(k == DT - 1))
                    cPE.inc(last)
                    st[("prj_pe", ha, i)] = cPE.n
                    if i == NPL - 1 or plan[i + 1][0] != wi:
                        st[("proj_done", ha, wi)] = cPE.n

                def dv_proj(v, i=i, kind=kind, a=a, chn=chn):
                    slot = i % 2
                    v.wait_ge(cPE.sem, st[("prj_pe", ha, i)])
                    if kind == "q":
                        cDV.inc(v.tensor_scalar_mul(
                            Qf[:, a, chn * 512:(chn + 1) * 512],
                            prj[:, slot, :], 0.125))
                    elif kind == "k":
                        cDV.inc(v.tensor_copy(
                            Kf[:, a, chn * 512:(chn + 1) * 512],
                            prj[:, slot, :]))
                    else:
                        cDV.inc(v.tensor_copy(
                            Vt[:, a, chn * 512:(chn + 1) * 512],
                            prj[:, slot, :]))
                    st[("prj_ev", ha, i)] = cDV.n

                self.on("tensor", pe_proj)
                self.on("vector", dv_proj)
            st[("qkv_dv", ha)] = cDV.n

            def gp_wo(gp):
                gp.wait_ge(cPE.sem, st[("proj_done", ha, 2)])
                st[("woT", ha)] = self.g_woT.inc(gp.dma_start(
                    wTa[:, :, :],
                    bass.AP(self.wT_scr[3], 0,
                            [[D, 128], [128 * D, DT], [1, D]])))

            self.on("gpsimd", gp_wo)

        with nc.psum_tensor(f"Sp0{ha}", [128, WINW], F32) as Sp0, \
             nc.psum_tensor(f"Sp1{ha}", [128, WINW], F32) as Sp1, \
             nc.psum_tensor(f"PTp{ha}", [128, 5, 128], BF16) as PTp, \
             nc.psum_tensor(f"pv0{ha}", [128, 512], F32) as pv0, \
             nc.psum_tensor(f"pv1{ha}", [128, 512], F32) as pv1, \
             nc.psum_tensor(f"x1p{ha}", [128, 512], F32) as x1p:
            Sp = [Sp0, Sp1]
            pvp = [pv0, pv1]

            def pe_ptinit(t):
                cPE.inc(t.transpose(PTp[:, 4, :], self.idb[:, :],
                                    self.idb[:, :]))
                st[("ptinit", ha)] = cPE.n

            self.on("tensor", pe_ptinit)
            for p in range(NP):
                self.pair(ha, p, Sp, PTp, pvp, x1p)
            self.tail(ha, NP - 1, pvp, x1p)

        st[("half_done", ha)] = cDV.n

    def pair(self, ha, p, Sp, PTp, pvp, x1p):
        nc, st = self.nc, self.st
        D, DT, HALF, BS, NP, POFF, WINW = (self.D, self.DT, self.HALF, self.BS,
                                           self.NP, self.POFF, self.WINW)
        TOK = self.TOK
        cPE, cDV, cAC = self.cPE, self.cDV, self.cAC
        Qf, Kf, Vt = self.Qf, self.Kf, self.Vt
        Pbf, PTb, attnC = self.Pbf, self.PTb, self.attnC
        maskb, smal = self.maskb, self.smal
        H_ = self.H

        pg = ha * NP + p
        s2 = p % 2
        qA, qB = p, p + POFF
        win0, win1 = qA * BS, qB * BS
        odd = (qA % 2 == 1)
        S_pe, S_ev, exp_ac, nrm_dv, PT_pe, PT_ev, pv_pe = \
            [], [], [], [], [], [], []
        pv_started = set()

        def sy_pair(sy):
            if pg >= 1:
                sy.wait_ge(cDV.sem, st[("x0_free", pg - 1)])
            r0 = ha * HALF + qA * BS
            r1 = ha * HALF + qB * BS
            self.s_x0.inc(sy.dma_start(self.x0r[0:64, 0, :],
                                       self.x_main[r0:r0 + 64, :]))
            st[("x0ld", pg)] = self.s_x0.inc(
                sy.dma_start(self.x0r[64:128, 0, :],
                             self.x_main[r1:r1 + 64, :]))

        def pe_S(t, h):
            hp, hh = divmod(h, 2)
            r0 = hh * 64
            hs = h % 2
            if h == 0 and p == 0:
                t.wait_ge(cDV.sem, st[("qkv_dv", ha)])
                self.g_woT.wait(t, st[("woT", ha)])
            if h == 0:
                self.s_mask.wait(t, st[("mload", pg)])
            if h >= 2:
                t.wait_ge(cAC.sem, exp_ac[h - 2])
            elif pg >= 1:
                t.wait_ge(cAC.sem, st[("exp_last", pg - 1)])
            t.matmul(Sp[hs][0:64, 0:512],
                     Qf[r0:r0 + 64, hp, qA * 64:(qA + 1) * 64],
                     Kf[r0:r0 + 64, hp, win0: win0 + 512],
                     start=True, stop=False, tile_position=(r0, 0))
            t.matmul(Sp[hs][64:128, 0:512],
                     Qf[r0:r0 + 64, hp, qB * 64:(qB + 1) * 64],
                     Kf[r0:r0 + 64, hp, win1: win1 + 512],
                     start=True, stop=False, tile_position=(r0, 64))
            t.matmul(Sp[hs][:, 0:512], self.idb[:, :], maskb[:, s2, 0:512],
                     start=False, stop=True)
            t.matmul(Sp[hs][0:64, 512:WINW],
                     Qf[r0:r0 + 64, hp, qA * 64:(qA + 1) * 64],
                     Kf[r0:r0 + 64, hp, win0 + 512: win0 + 576],
                     start=True, stop=False, tile_position=(r0, 0))
            t.matmul(Sp[hs][64:128, 512:WINW],
                     Qf[r0:r0 + 64, hp, qB * 64:(qB + 1) * 64],
                     Kf[r0:r0 + 64, hp, win1 + 512: win1 + 576],
                     start=True, stop=False, tile_position=(r0, 64))
            last = t.matmul(Sp[hs][:, 512:WINW], self.idb[:, :],
                            maskb[:, s2, 512:WINW],
                            start=False, stop=True)
            cPE.inc(last)
            S_pe.append(cPE.n)
            st[("S_pe", pg, h)] = cPE.n

        def dv_S(v, h):
            hs = h % 2
            v.wait_ge(cPE.sem, S_pe[h])
            if h >= 2:
                v.wait_ge(cAC.sem, exp_ac[h - 2])
            elif pg >= 1:
                v.wait_ge(cAC.sem, st[("exp_last", pg - 1)])
            cDV.inc(v.reduce_max(smal[:, hs, 0:1], Sp[hs][:, :],
                                 axis=AX.X, negate=True))
            S_ev.append(cDV.n)
            st[("S_ev", pg, h)] = cDV.n

        def ac_exp(s, h):
            hs = h % 2
            s.wait_ge(cDV.sem, S_ev[h])
            if h >= 2:
                s.wait_ge(cPE.sem, PT_pe[h - 2])
            elif pg >= 1:
                s.wait_ge(cPE.sem, st[("PT_last", pg - 1)])
            cAC.inc(s.activation(Pbf[:, hs, :], Sp[hs][:, :],
                                 AF.Exp, bias=smal[:, hs, 0:1],
                                 accum_out=smal[:, hs, 1:2]))
            exp_ac.append(cAC.n)

        def dv_nrm(v, h):
            hs = h % 2
            v.wait_ge(cAC.sem, exp_ac[h])
            cDV.inc(v.reciprocal(smal[:, hs, 2:3], smal[:, hs, 1:2]))
            v.wait_ge(cDV.sem, cDV.n)
            cDV.inc(v.tensor_scalar_mul(Pbf[:, hs, :], Pbf[:, hs, :],
                                        smal[:, hs, 2:3]))
            nrm_dv.append(cDV.n)

        def pe_PT(t, h):
            hs = h % 2
            t.wait_ge(cDV.sem, nrm_dv[h])
            if h >= 1:
                t.wait_ge(cDV.sem, PT_ev[h - 1])
            elif pg >= 1:
                t.wait_ge(cDV.sem, st[("PT_ev_last", pg - 1)])
            else:
                t.wait_ge(cPE.sem, st[("ptinit", ha)])
            last = None
            if not odd:
                for cc in range(4):
                    last = t.transpose(PTp[:, cc, :],
                                       Pbf[:, hs, cc * 128:(cc + 1) * 128],
                                       self.idb[:, :])
                last = t.transpose(PTp[0:64, 4, :],
                                   Pbf[:, hs, 512:576], self.idb[:, :])
            else:
                last = t.transpose(PTp[64:128, 0, :],
                                   Pbf[:, hs, 0:64], self.idb[:, :],
                                   tile_position=(0, 64))
                for cc in range(4):
                    last = t.transpose(
                        PTp[:, cc + 1, :],
                        Pbf[:, hs, 64 + cc * 128: 64 + (cc + 1) * 128],
                        self.idb[:, :])
            cPE.inc(last)
            PT_pe.append(cPE.n)
            st[("PT_pe", pg, h)] = cPE.n

        def dv_PT(v, h):
            hs = h % 2
            v.wait_ge(cPE.sem, PT_pe[h])
            if h >= 2:
                v.wait_ge(cPE.sem, pv_pe[h - 2])
            elif pg >= 1:
                v.wait_ge(cPE.sem, st[("pv_last", pg - 1)])
            cDV.inc(v.tensor_copy(PTb[:, hs, :, :], PTp[:, :, :]))
            PT_ev.append(cDV.n)

        def pe_PV(t, h):
            hs = h % 2
            hp, hh = divmod(h, 2)
            bank = pvp[hp // 4]
            fo = (hp % 4) * 128
            t.wait_ge(cDV.sem, PT_ev[h])
            if h == 0 and pg >= 1:
                t.wait_ge(cDV.sem, st[("attn_ev", pg - 1)])
            if h == 8 and pg >= 1:
                t.wait_ge(cDV.sem, st[("res_dv", pg - 1, 1)])
            lastmm = None
            for qi, qb in enumerate((qA, qB)):
                win = qb * BS
                if not odd:
                    chunks = [(win // 128 + cc, 0, 128, cc) for cc in range(4)]
                    chunks.append(((win + 512) // 128, 0, 64, 4))
                else:
                    chunks = [(win // 128, 64, 64, 0)]
                    chunks += [((win + 64) // 128 + cc, 0, 128, cc + 1)
                               for cc in range(4)]
                last_h = (hp // 4) * 8 + 6 + hh
                for ci, (vt_t, pr0, klen, slot) in enumerate(chunks):
                    key = (hp // 4, hh)
                    first = key not in pv_started
                    pv_started.add(key)
                    stop = (h == last_h and qi == 1 and ci == len(chunks) - 1)
                    lastmm = t.matmul(
                        bank[hh * 64:(hh + 1) * 64,
                             fo + qi * 64: fo + (qi + 1) * 64],
                        Vt[pr0:pr0 + klen, vt_t, h * 64:(h + 1) * 64],
                        PTb[pr0:pr0 + klen, hs, slot, qi * 64:(qi + 1) * 64],
                        start=first, stop=stop, tile_position=(pr0, hh * 64))
            cPE.inc(lastmm)
            pv_pe.append(cPE.n)
            st[("pv_pe", pg, h)] = cPE.n

        def sy_mask(sy, mpg):
            ms = mpg % 2
            if mpg >= 2:
                sy.wait_ge(cPE.sem, st[("mask_free", mpg - 2)])
            st[("mload", mpg)] = self.s_mask.inc(
                sy.dma_start(maskb[:, ms, :], self.mask[mpg, :, :]), ms)

        if p == 0:
            self.on("sync", lambda sy: sy_mask(sy, pg))
        for i in range(H_ + 1):
            def pe_c1(t, i=i):
                if i == 0:
                    pe_S(t, 0)
                    pe_S(t, 1)
                else:
                    if i + 1 < H_:
                        pe_S(t, i + 1)
                    pe_PT(t, i - 1)
            self.on("tensor", pe_c1)
            if i == 0 and p >= 1:
                self.emit_tail_pe_opj(ha, p - 1, pvp)
            if i == 0:
                self.on("sync", sy_pair)
            if i == 2:
                if p >= 1:
                    self.emit_tail_rest(ha, p - 1, x1p)
                    self.emit_tail_sync(ha, p - 1)
                if p + 1 < NP:
                    self.on("sync", lambda sy: sy_mask(sy, pg + 1))

            def dv_c2(v, i=i):
                if i >= 1:
                    dv_PT(v, i - 1)
                if i < H_:
                    dv_S(v, i)
            self.on("vector", dv_c2)
            if i >= 1:
                self.on("tensor", lambda t, i=i: pe_PV(t, i - 1))
            if i < H_:
                self.on("scalar", lambda s, i=i: ac_exp(s, i))
                self.on("vector", lambda v, i=i: dv_nrm(v, i))

        st[("exp_last", pg)] = exp_ac[H_ - 1]
        st[("PT_last", pg)] = PT_pe[H_ - 1]
        st[("PT_ev_last", pg)] = PT_ev[H_ - 1]
        st[("mask_free", pg)] = S_pe[H_ - 1]
        st[("pv_last", pg)] = pv_pe[H_ - 1]

        def dv_attn(v):
            v.wait_ge(cPE.sem, st[("pv_last", pg)])
            v.tensor_copy(attnC[:, s2, 0:4, :],
                          bass.AP(pvp[0], 0, [[512, 128], [128, 4], [1, 128]]))
            cDV.inc(v.tensor_copy(attnC[:, s2, 4:8, :],
                                  bass.AP(pvp[1], 0, [[512, 128], [128, 4], [1, 128]])))
            st[("attn_ev", pg)] = cDV.n

        self.on("vector", dv_attn)

    def tail(self, ha, p, pvp, x1p):
        self.emit_tail_pe_opj(ha, p, pvp)
        self.emit_tail_rest(ha, p, x1p)
        self.emit_tail_sync(ha, p)

    def emit_tail_pe_opj(self, ha, p, pvp):
        nc, st = self.nc, self.st
        D, DT, HALF, BS, NP, POFF = (self.D, self.DT, self.HALF, self.BS,
                                     self.NP, self.POFF)
        cPE, cDV, cAC = self.cPE, self.cDV, self.cAC
        attnC, wTa = self.attnC, self.woT
        pg = ha * NP + p
        s2 = p % 2

        def pe_opj0(t):
            t.wait_ge(cDV.sem, st[("attn_ev", pg)])
            last = None
            for k in range(DT):
                last = t.matmul(pvp[1][:, :], attnC[:, s2, k, :],
                                wTa[:, k, 0:512],
                                start=(k == 0), stop=(k == DT - 1))
            cPE.inc(last)
            st[("opj_pe", pg, 0)] = cPE.n

        def dv_res0(v):
            v.wait_ge(cPE.sem, st[("opj_pe", pg, 0)])
            self.s_x0.wait(v, st[("x0ld", pg)])
            if pg >= 2:
                self.s_x1.wait(v, st[("x1st", pg - 2)])
                v.wait_ge(cPE.sem, st[("x1T", pg - 2)])
                v.wait_ge(cAC.sem, st[("lnsq", pg - 2)])
            cDV.inc(v.tensor_add(self.x1c[:, s2, 0:512], pvp[1][:, :],
                                 self.x0r[:, 0, 0:512]))
            st[("res_dv", pg, 0)] = cDV.n

        def pe_opj1(t):
            t.wait_ge(cDV.sem, st[("res_dv", pg, 0)])
            last = None
            for k in range(DT):
                last = t.matmul(pvp[1][:, :], attnC[:, s2, k, :],
                                wTa[:, k, 512:1024],
                                start=(k == 0), stop=(k == DT - 1))
            cPE.inc(last)
            st[("opj_pe", pg, 1)] = cPE.n

        def dv_res1(v):
            v.wait_ge(cPE.sem, st[("opj_pe", pg, 1)])
            cDV.inc(v.tensor_add(self.x1c[:, s2, 512:1024], pvp[1][:, :],
                                 self.x0r[:, 0, 512:1024]))
            st[("res_dv", pg, 1)] = cDV.n
            v.wait_ge(cDV.sem, cDV.n)
            cDV.inc(v.reduce_sum(self.smal[:, s2, 4:5], self.x1c[:, s2, :],
                                 axis=AX.X))
            st[("lnsum", pg)] = cDV.n

        self.on("tensor", pe_opj0)
        self.on("vector", dv_res0)
        self.on("tensor", pe_opj1)
        self.on("vector", dv_res1)
        st[("x0_free", pg)] = st[("res_dv", pg, 1)]

    def emit_tail_rest(self, ha, p, x1p):
        nc, st = self.nc, self.st
        D, DT, HALF, BS, NP, POFF, TOK = (self.D, self.DT, self.HALF, self.BS,
                                          self.NP, self.POFF, self.TOK)
        cPE, cDV, cAC = self.cPE, self.cDV, self.cAC
        x1c, smal, sqj, X1fm = self.x1c, self.smal, self.sqj, self.X1fm
        pg = ha * NP + p
        s2 = p % 2
        qA = p

        def ac_lnsq(s):
            s.wait_ge(cDV.sem, st[("res_dv", pg, 1)])
            if pg >= 1:
                s.wait_ge(cAC.sem, st[("lnsq", pg - 1)])
            cAC.inc(s.activation(sqj[:, :], x1c[:, s2, :], AF.Square,
                                 accum_out=smal[:, s2, 5:6]))
            st[("lnsq", pg)] = cAC.n

        def dv_ln(v):
            v.wait_ge(cDV.sem, st[("lnsum", pg)])
            v.wait_ge(cAC.sem, st[("lnsq", pg)])
            cDV.inc(v.tensor_scalar_mul(smal[:, s2, 6:7], smal[:, s2, 4:5],
                                        1.0 / D))
            cDV.inc(v.tensor_scalar_mul(smal[:, s2, 7:8], smal[:, s2, 5:6],
                                        1.0 / D))
            v.wait_ge(cDV.sem, cDV.n)
            cDV.inc(v.tensor_mul(smal[:, s2, 8:9], smal[:, s2, 6:7],
                                 smal[:, s2, 6:7]))
            v.wait_ge(cDV.sem, cDV.n)
            cDV.inc(v.tensor_sub(smal[:, s2, 9:10], smal[:, s2, 7:8],
                                 smal[:, s2, 8:9]))
            v.wait_ge(cDV.sem, cDV.n)
            cDV.inc(v.tensor_scalar_add(smal[:, s2, 10:11], smal[:, s2, 9:10],
                                        1e-5))
            st[("lnvar", pg)] = cDV.n

        def ac_sqrt(s):
            s.wait_ge(cDV.sem, st[("lnvar", pg)])
            cAC.inc(s.activation(smal[:, s2, 11:12], smal[:, s2, 10:11],
                                 AF.Sqrt))
            st[("lnstd", pg)] = cAC.n

        def dv_ln2(v):
            v.wait_ge(cAC.sem, st[("lnstd", pg)])
            cDV.inc(v.reciprocal(smal[:, s2, 12:13], smal[:, s2, 11:12]))
            v.wait_ge(cDV.sem, cDV.n)
            cDV.inc(v.tensor_mul(smal[:, s2, 13:14], smal[:, s2, 6:7],
                                 smal[:, s2, 12:13]))
            v.wait_ge(cDV.sem, cDV.n)
            cDV.inc(v.tensor_scalar_mul(smal[:, s2, 14:15], smal[:, s2, 13:14],
                                        -1.0))
            st[("lnab", pg)] = cDV.n

        def ac_apply(s):
            s.wait_ge(cDV.sem, st[("lnab", pg)])
            cAC.inc(s.activation(x1c[:, s2, :], x1c[:, s2, :],
                                 AF.Identity, bias=smal[:, s2, 14:15],
                                 scale=smal[:, s2, 12:13]))
            st[("ln_done", pg)] = cAC.n

        def pe_x1T(t, half4):
            if half4 == 0:
                t.wait_ge(cAC.sem, st[("ln_done", pg)])
                if pg >= 1:
                    t.wait_ge(cDV.sem, st[("x1T_ev", pg - 1, 1)])
            else:
                t.wait_ge(cDV.sem, st[("x1T_ev", pg, 0)])
            last = None
            for c in range(4):
                cc = half4 * 4 + c
                last = t.transpose(x1p[:, c * 128:(c + 1) * 128],
                                   x1c[:, s2, cc * 128:(cc + 1) * 128],
                                   self.idf[:, :])
            cPE.inc(last)
            st[("x1T", pg, half4)] = cPE.n
            if half4 == 1:
                st[("x1T", pg)] = cPE.n

        def dv_x1T(v, half4):
            v.wait_ge(cPE.sem, st[("x1T", pg, half4)])
            dst = bass.AP(X1fm,
                          half4 * 4 * TOK + ha * HALF + qA * BS,
                          [[DT * TOK, 128], [TOK, 4], [POFF * BS, 2],
                           [1, 64]])
            src = bass.AP(x1p, 0, [[512, 128], [128, 4], [64, 2], [1, 64]])
            cDV.inc(v.tensor_copy(dst, src))
            st[("x1T_ev", pg, half4)] = cDV.n
            if half4 == 1:
                st[("x1T_ev", pg)] = cDV.n

        self.on("scalar", ac_lnsq)
        self.on("vector", dv_ln)
        self.on("scalar", ac_sqrt)
        self.on("vector", dv_ln2)
        self.on("scalar", ac_apply)
        self.on("tensor", lambda t: pe_x1T(t, 0))
        self.on("vector", lambda v: dv_x1T(v, 0))
        self.on("tensor", lambda t: pe_x1T(t, 1))
        self.on("vector", lambda v: dv_x1T(v, 1))

    def emit_tail_sync(self, ha, p):
        nc, st = self.nc, self.st
        HALF, BS, NP, POFF = self.HALF, self.BS, self.NP, self.POFF
        cAC = self.cAC
        pg = ha * NP + p
        s2 = p % 2
        qA, qB = p, p + POFF

        def sy_x1(sy):
            sy.wait_ge(cAC.sem, st[("ln_done", pg)])
            r0 = ha * HALF + qA * BS
            r1 = ha * HALF + qB * BS
            self.s_x1.inc(sy.dma_start(self.x1_scr[r0:r0 + 64, :],
                                       self.x1c[0:64, s2, :]), s2)
            st[("x1st", pg)] = self.s_x1.inc(
                sy.dma_start(self.x1_scr[r1:r1 + 64, :],
                             self.x1c[64:128, s2, :]), s2)

        self.on("sync", sy_x1)

    def gates_phase(self):
        nc, st = self.nc, self.st
        D, DT, E, NTT, TOK = self.D, self.DT, self.E, self.NTT, self.TOK
        cPE, cDV, cAC = self.cPE, self.cDV, self.cAC
        X1fm, wgT, iot, gates, gatesT, idf = (self.X1fm, self.wgT, self.iot,
                                              self.gates, self.gatesT, self.idf)
        last_x1 = st[("x1T_ev", 2 * self.NP - 1)]

        with nc.sbuf_tensor("gl", [128, 2, E], F32) as gl, \
             nc.sbuf_tensor("gsc", [128, 2, 8], F32) as gsc, \
             nc.psum_tensor("glp", [128, 2, E], F32) as glp, \
             nc.psum_tensor("gtp", [E, 2, 128], F32) as gtp:
            for tt in range(NTT):
                s2 = tt % 2

                def pe_gl(t, tt=tt, s2=s2):
                    if tt == 0:
                        t.wait_ge(cDV.sem, last_x1)
                        t.wait_ge(cDV.sem, st["wgev"])
                    if tt >= 2:
                        t.wait_ge(cDV.sem, st[("glev", tt - 2)])
                    last = None
                    for k in range(DT):
                        last = t.matmul(glp[:, s2, :],
                                        X1fm[:, k, tt * 128:(tt + 1) * 128],
                                        wgT[:, k, :],
                                        start=(k == 0), stop=(k == DT - 1))
                    cPE.inc(last)
                    st[("glpe", tt)] = cPE.n

                def dv_g1(v, tt=tt, s2=s2):
                    v.wait_ge(cPE.sem, st[("glpe", tt)])
                    if tt == 0:
                        self.s_iden.wait_all(v)
                    cDV.inc(v.tensor_add(gl[:, s2, :],
                                         glp[:, s2, :], iot[:, :]))
                    st[("glev", tt)] = cDV.n
                    v.wait_ge(cDV.sem, cDV.n)
                    cDV.inc(v.reduce_max(gsc[:, s2, 0:1],
                                         gl[:, s2, :], axis=AX.X))
                    v.wait_ge(cDV.sem, cDV.n)
                    cDV.inc(v.tensor_single_scalar(gates[:, tt, :],
                                                   gl[:, s2, :],
                                                   gsc[:, s2, 0:1],
                                                   ALU.is_ge))
                    v.wait_ge(cDV.sem, cDV.n)
                    cDV.inc(v.scalar_tensor_tensor(gl[:, s2, :],
                                                   gates[:, tt, :], NEG,
                                                   gl[:, s2, :],
                                                   op0=ALU.mult, op1=ALU.add))
                    v.wait_ge(cDV.sem, cDV.n)
                    cDV.inc(v.reduce_max(gsc[:, s2, 1:2],
                                         gl[:, s2, :], axis=AX.X))
                    v.wait_ge(cDV.sem, cDV.n)
                    cDV.inc(v.tensor_sub(gsc[:, s2, 2:3],
                                         gsc[:, s2, 0:1],
                                         gsc[:, s2, 1:2]))
                    st[("gd", tt)] = cDV.n

                def ac_g(s, tt=tt, s2=s2):
                    s.wait_ge(cDV.sem, st[("gd", tt)])
                    cAC.inc(s.activation(gsc[:, s2, 3:4],
                                         gsc[:, s2, 2:3], AF.Exp,
                                         scale=-1.0))
                    st[("ge", tt)] = cAC.n

                def dv_g2(v, tt=tt, s2=s2):
                    v.wait_ge(cAC.sem, st[("ge", tt)])
                    cDV.inc(v.tensor_scalar_add(gsc[:, s2, 4:5],
                                                gsc[:, s2, 3:4], 1.0))
                    v.wait_ge(cDV.sem, cDV.n)
                    cDV.inc(v.reciprocal(gsc[:, s2, 5:6],
                                         gsc[:, s2, 4:5]))
                    v.wait_ge(cDV.sem, cDV.n)
                    cDV.inc(v.tensor_mul(gsc[:, s2, 6:7],
                                         gsc[:, s2, 3:4],
                                         gsc[:, s2, 5:6]))
                    v.wait_ge(cDV.sem, cDV.n)
                    cDV.inc(v.tensor_single_scalar(gl[:, s2, :],
                                                   gl[:, s2, :],
                                                   gsc[:, s2, 1:2],
                                                   ALU.is_ge))
                    v.wait_ge(cDV.sem, cDV.n)
                    cDV.inc(v.tensor_scalar_mul(gates[:, tt, :],
                                                gates[:, tt, :],
                                                gsc[:, s2, 5:6]))
                    v.wait_ge(cDV.sem, cDV.n)
                    cDV.inc(v.scalar_tensor_tensor(gates[:, tt, :],
                                                   gl[:, s2, :],
                                                   gsc[:, s2, 6:7],
                                                   gates[:, tt, :],
                                                   op0=ALU.mult, op1=ALU.add))
                    st[("gates_dv", tt)] = cDV.n

                def pe_gT(t, tt=tt, s2=s2):
                    t.wait_ge(cDV.sem, st[("gates_dv", tt)])
                    if tt >= 2:
                        t.wait_ge(cDV.sem, st[("gTev", tt - 2)])
                    cPE.inc(t.transpose(gtp[:, s2, :],
                                        gates[:, tt, :], idf[:, :]))
                    st[("gTpe", tt)] = cPE.n

                def dv_gT(v, tt=tt, s2=s2):
                    v.wait_ge(cPE.sem, st[("gTpe", tt)])
                    cDV.inc(v.tensor_copy(gatesT[:, tt * 128:(tt + 1) * 128],
                                          gtp[:, s2, :]))
                    st[("gTev", tt)] = cDV.n

                self.on("tensor", pe_gl)
                self.on("vector", dv_g1)
                self.on("scalar", ac_g)
                self.on("vector", dv_g2)
                self.on("tensor", pe_gT)
                self.on("vector", dv_gT)

    def ffn(self):
        nc, st = self.nc, self.st
        D, DT, E, NTT, INTER, IC, HALF = (self.D, self.DT, self.E, self.NTT,
                                          self.INTER, self.IC, self.HALF)
        cPE, cDV, cAC = self.cPE, self.cDV, self.cAC
        X1fm, gates, gatesT, b1sb, b2sb = (self.X1fm, self.gates, self.gatesT,
                                           self.b1sb, self.b2sb)
        hbuf, w1sl, w2sl, y_tm = self.hbuf, self.w1sl, self.w2sl, self.y_tm
        HTT = HALF // 128
        TCH = HALF // 512
        TQ = 2
        NG = IC * 4 * TCH

        pe_at_ffn = cPE.n
        ac_at_ffn = cAC.n
        dv_at_ffn = cDV.n

        def sy_yinit(sy):
            self.s_x1.wait_all(sy)
            sy.wait_ge(cPE.sem, pe_at_ffn)
            sy.wait_ge(cDV.sem, dv_at_ffn)
            sy.wait_ge(cAC.sem, ac_at_ffn)
            for tt in range(NTT):
                self.s_yin.inc(sy.dma_start(
                    y_tm[:, tt, :],
                    self.x1_scr[tt * 128:(tt + 1) * 128, :]))

        self.on("sync", sy_yinit)

        with nc.psum_tensor("mp", [128, 8, 512], F32) as mp:

            for ei in range(E):
                for tq in range(TQ):
                    pidx = ei * TQ + tq

                    for c in range(IC):
                        idx = pidx * IC + c

                        def gp_w1(gp, ei=ei, c=c, idx=idx):
                            if idx >= 2:
                                gp.wait_ge(cPE.sem, st[("mm1cd", idx - 2)])
                            st[("w1ld", idx)] = self.g_w1.inc(gp.dma_start(
                                w1sl[:, idx % 2, :, :],
                                bass.AP(self.w1, ei * D * INTER + c * 512,
                                        [[INTER, 128], [128 * INTER, DT],
                                         [1, 512]])), idx % 2)

                        self.on("gpsimd", gp_w1)
                        for it in range(4):
                            def pe_mm1(t, tq=tq, pidx=pidx, c=c, idx=idx, it=it):
                                if it == 0:
                                    self.g_w1.wait(t, st[("w1ld", idx)])
                                for tch in range(TCH):
                                    g = (c * 4 + it) * TCH + tch
                                    slot = g % 2
                                    if g >= 2:
                                        t.wait_ge(cAC.sem, st[("gelu", pidx, g - 2)])
                                    elif pidx >= 1:
                                        t.wait_ge(cAC.sem,
                                                  st[("gelu", pidx - 1, NG - 2 + g)])
                                        t.wait_ge(cDV.sem,
                                                  st[("mm2ev", (pidx - 1) * 2 + 1,
                                                      slot)])
                                    else:
                                        t.wait_ge(cDV.sem, dv_at_ffn)
                                    last = None
                                    for k in range(DT):
                                        last = t.matmul(
                                            mp[:, slot, :],
                                            w1sl[:, idx % 2, k,
                                                 it * 128:(it + 1) * 128],
                                            X1fm[:, k, tq * HALF + tch * 512:
                                                 tq * HALF + (tch + 1) * 512],
                                            start=(k == 0), stop=(k == DT - 1))
                                    cPE.inc(last)
                                    st[("mm1pe", pidx, g)] = cPE.n
                                if it == 3:
                                    st[("mm1cd", idx)] = cPE.n

                            def ac_gelu(s, ei=ei, pidx=pidx, c=c, it=it):
                                for tch in range(TCH):
                                    g = (c * 4 + it) * TCH + tch
                                    slot = g % 2
                                    s.wait_ge(cPE.sem, st[("mm1pe", pidx, g)])
                                    if g == 0 and pidx == 0:
                                        self.g_bias.wait_all(s)
                                    if g == 0 and pidx >= 1:
                                        s.wait_ge(cPE.sem,
                                                  st[("mm2alldone", pidx - 1)])
                                    cAC.inc(s.activation(
                                        hbuf[:, c * 4 + it,
                                             tch * 512:(tch + 1) * 512],
                                        mp[:, slot, :], AF.Gelu,
                                        bias=b1sb[:, ei, c * 4 + it:
                                                  c * 4 + it + 1]))
                                    st[("gelu", pidx, g)] = cAC.n

                            self.on("tensor", pe_mm1)
                            self.on("scalar", ac_gelu)

                    for Hh in range(2):
                        for c in range(IC):
                            idx2 = (pidx * 2 + Hh) * IC + c

                            def gp_w2(gp, ei=ei, Hh=Hh, c=c, idx2=idx2):
                                if idx2 >= 2:
                                    gp.wait_ge(cPE.sem, st[("mm2cd", idx2 - 2)])
                                st[("w2ld", idx2)] = self.g_w2.inc(gp.dma_start(
                                    w2sl[:, idx2 % 2, :, :],
                                    bass.AP(self.w2,
                                            ei * INTER * D + c * 512 * D + Hh * 512,
                                            [[D, 128], [128 * D, 4], [1, 512]])),
                                    idx2 % 2)

                            def pe_mm2(t, pidx=pidx, Hh=Hh, c=c, idx2=idx2):
                                self.g_w2.wait(t, st[("w2ld", idx2)])
                                lastc = None
                                for tt in range(HTT):
                                    if c == 0:
                                        prev = st.get(("mm2ev",
                                                       pidx * 2 + Hh - 1, tt))
                                        if prev is not None:
                                            t.wait_ge(cDV.sem, prev)
                                        elif pidx == 0 and Hh == 0:
                                            t.wait_ge(cDV.sem, dv_at_ffn)
                                        if Hh == 0 and tt < 2:
                                            t.wait_ge(cAC.sem,
                                                      st[("gelu", pidx,
                                                          NG - 2 + tt)])
                                    last = None
                                    for k in range(4):
                                        h_it = c * 4 + k
                                        t.wait_ge(cAC.sem,
                                                  st[("gelu", pidx,
                                                      (h_it + 1) * TCH - 1)])
                                        last = t.matmul(
                                            mp[:, tt, :],
                                            hbuf[:, h_it,
                                                 tt * 128:(tt + 1) * 128],
                                            w2sl[:, idx2 % 2, k, :],
                                            start=(c == 0 and k == 0),
                                            stop=(c == IC - 1 and k == 3))
                                    if c == IC - 1:
                                        cPE.inc(last)
                                        st[("mm2pe", pidx * 2 + Hh, tt)] = cPE.n
                                    lastc = last
                                if c != IC - 1:
                                    cPE.inc(lastc)
                                st[("mm2cd", idx2)] = cPE.n
                                if Hh == 1 and c == IC - 1:
                                    st[("mm2alldone", pidx)] = cPE.n

                            self.on("gpsimd", gp_w2)
                            self.on("tensor", pe_mm2)

                        def dv_mm2(v, ei=ei, tq=tq, pidx=pidx, Hh=Hh):
                            for tt in range(HTT):
                                v.wait_ge(cPE.sem, st[("mm2pe", pidx * 2 + Hh, tt)])
                                if ei == 0:
                                    self.s_yin.wait_all(v)
                                    v.wait_ge(cDV.sem, st[("gates_dv",
                                                           tq * HTT + tt)])
                                gtt = tq * HTT + tt
                                cDV.inc(v.scalar_tensor_tensor(
                                    y_tm[:, gtt, Hh * 512:(Hh + 1) * 512],
                                    mp[:, tt, :],
                                    gates[:, gtt, ei:ei + 1],
                                    y_tm[:, gtt, Hh * 512:(Hh + 1) * 512],
                                    op0=ALU.mult, op1=ALU.add))
                                st[("mm2ev", pidx * 2 + Hh, tt)] = cDV.n

                        self.on("vector", dv_mm2)

            for tt in range(NTT):
                tq, ltt = divmod(tt, HTT)
                for ch in range(2):
                    fi = tt * 2 + ch
                    slot = fi % 2

                    def pe_b2(t, tt=tt, ch=ch, fi=fi, slot=slot):
                        if fi == 0:
                            t.wait_ge(cDV.sem, st[("gTev", NTT - 1)])
                            self.g_bias.wait_all(t)
                        if fi >= 2:
                            t.wait_ge(cDV.sem, st[("b2ev", fi - 2)])
                        else:
                            t.wait_ge(cDV.sem,
                                      st[("mm2ev", (E * TQ - 1) * 2 + 1, slot)])
                        cPE.inc(t.matmul(
                            mp[:, slot, :],
                            self.gatesT[:, tt * 128:(tt + 1) * 128],
                            b2sb[:, ch * 512:(ch + 1) * 512],
                            start=True, stop=True))
                        st[("b2pe", fi)] = cPE.n

                    def dv_b2(v, tt=tt, ch=ch, fi=fi, slot=slot, tq=tq, ltt=ltt):
                        v.wait_ge(cPE.sem, st[("b2pe", fi)])
                        v.wait_ge(cDV.sem,
                                  st[("mm2ev",
                                      ((E - 1) * TQ + tq) * 2 + ch, ltt)])
                        cDV.inc(v.tensor_add(
                            y_tm[:, tt, ch * 512:(ch + 1) * 512],
                            y_tm[:, tt, ch * 512:(ch + 1) * 512],
                            mp[:, slot, :]))
                        st[("b2ev", fi)] = cDV.n
                        if ch == 1:
                            st[("b2evt", tt)] = cDV.n

                    self.on("tensor", pe_b2)
                    self.on("vector", dv_b2)

    def ln2(self):
        nc, st = self.nc, self.st
        D, NTT = self.D, self.NTT
        cDV, cAC = self.cDV, self.cAC
        y_tm, outst, lnt2, sqj2 = self.y_tm, self.outst, self.lnt2, self.sqj2

        for tt in range(NTT):
            s2 = tt % 2

            def dv_a(v, tt=tt, s2=s2):
                v.wait_ge(cDV.sem, st[("b2evt", tt)])
                cDV.inc(v.reduce_sum(lnt2[:, s2 * 8 + 0: s2 * 8 + 1],
                                     y_tm[:, tt, :], axis=AX.X))
                st[("l2sum", tt)] = cDV.n

            def ac_a(s, tt=tt, s2=s2):
                s.wait_ge(cDV.sem, st[("l2sum", tt)])
                if tt >= 2:
                    s.wait_ge(cAC.sem, st[("l2done", tt - 2)])
                cAC.inc(s.activation(sqj2[:, :], y_tm[:, tt, :],
                                     AF.Square,
                                     accum_out=lnt2[:, s2 * 8 + 1: s2 * 8 + 2]))
                st[("l2sq", tt)] = cAC.n

            def dv_b(v, tt=tt, s2=s2):
                o = s2 * 8
                v.wait_ge(cAC.sem, st[("l2sq", tt)])
                cDV.inc(v.tensor_scalar_mul(lnt2[:, o + 2:o + 3],
                                            lnt2[:, o + 0:o + 1], 1.0 / D))
                cDV.inc(v.tensor_scalar_mul(lnt2[:, o + 3:o + 4],
                                            lnt2[:, o + 1:o + 2], 1.0 / D))
                v.wait_ge(cDV.sem, cDV.n)
                cDV.inc(v.tensor_mul(lnt2[:, o + 4:o + 5], lnt2[:, o + 2:o + 3],
                                     lnt2[:, o + 2:o + 3]))
                v.wait_ge(cDV.sem, cDV.n)
                cDV.inc(v.tensor_sub(lnt2[:, o + 5:o + 6], lnt2[:, o + 3:o + 4],
                                     lnt2[:, o + 4:o + 5]))
                v.wait_ge(cDV.sem, cDV.n)
                cDV.inc(v.tensor_scalar_add(lnt2[:, o + 6:o + 7],
                                            lnt2[:, o + 5:o + 6], 1e-5))
                st[("l2var", tt)] = cDV.n

            def ac_b(s, tt=tt, s2=s2):
                s.wait_ge(cDV.sem, st[("l2var", tt)])
                cAC.inc(s.activation(lnt2[:, s2 * 8 + 7: s2 * 8 + 8],
                                     lnt2[:, s2 * 8 + 6: s2 * 8 + 7], AF.Sqrt))
                st[("l2std", tt)] = cAC.n

            def dv_c(v, tt=tt, s2=s2):
                o = s2 * 8
                v.wait_ge(cAC.sem, st[("l2std", tt)])
                cDV.inc(v.reciprocal(lnt2[:, o + 4:o + 5], lnt2[:, o + 7:o + 8]))
                v.wait_ge(cDV.sem, cDV.n)
                cDV.inc(v.tensor_mul(lnt2[:, o + 5:o + 6], lnt2[:, o + 2:o + 3],
                                     lnt2[:, o + 4:o + 5]))
                v.wait_ge(cDV.sem, cDV.n)
                cDV.inc(v.tensor_scalar_mul(lnt2[:, o + 6:o + 7],
                                            lnt2[:, o + 5:o + 6], -1.0))
                st[("l2ab", tt)] = cDV.n

            def ac_c(s, tt=tt, s2=s2):
                s.wait_ge(cDV.sem, st[("l2ab", tt)])
                if tt >= 2:
                    self.s_out.wait(s, st[("outst", tt - 2)])
                cAC.inc(s.activation(outst[:, s2, :],
                                     y_tm[:, tt, :], AF.Identity,
                                     bias=lnt2[:, s2 * 8 + 6: s2 * 8 + 7],
                                     scale=lnt2[:, s2 * 8 + 4: s2 * 8 + 5]))
                st[("l2done", tt)] = cAC.n

            def sy_out(sy, tt=tt, s2=s2):
                sy.wait_ge(cAC.sem, st[("l2done", tt)])
                st[("outst", tt)] = self.s_out.inc(
                    sy.dma_start(self.yout[tt * 128:(tt + 1) * 128, :],
                                 outst[:, s2, :]), s2)

            self.on("vector", dv_a)
            self.on("scalar", ac_a)
            self.on("vector", dv_b)
            self.on("scalar", ac_b)
            self.on("vector", dv_c)
            self.on("scalar", ac_c)
            self.on("sync", sy_out)

        def sy_fin(sy):
            self.s_out.wait_all(sy)

        self.on("sync", sy_fin)


_CACHE = {}


def _masks(TOK, BS=64, WIN=512):
    WINW = WIN + BS
    HALF = TOK // 2
    HQB = HALF // BS
    POFF = HQB // 2
    NP = POFF
    WB = WIN // BS
    out = []
    qi = np.arange(BS)[:, None]
    ki = np.arange(WINW)[None, :]
    wi = ki // BS
    kwi = ki % BS
    for parity in range(2):
        m = np.zeros((2 * NP, 128, WINW), np.float32)
        for ha in range(2):
            for r in range(NP):
                for half_row, qb in ((0, r), (1, r + POFF)):
                    g = ha * HQB + qb
                    g_seq = g + parity * (TOK // BS)
                    qpos = g_seq * BS + qi
                    kpos = (g_seq + wi - WB) * BS + kwi
                    valid = ((g_seq + wi - WB) >= 0) & (kpos <= qpos) & \
                            (qpos - kpos < WIN)
                    m[ha * NP + r, half_row * 64:(half_row + 1) * 64, :] = \
                        np.where(valid, 0.0, NEG).astype(np.float32)
        out.append(m.astype(ml_dtypes.bfloat16))
    return out


def kernel(L0_attn_x, L0_attn_Wq, L0_attn_Wk, L0_attn_Wv, L0_attn_Wo,
           L0_ffn_Wg, L0_ffn_W1, L0_ffn_b1, L0_ffn_W2, L0_ffn_b2):
    TOK, D, E, INTER = 2048, 1024, 8, 4096
    x = np.ascontiguousarray(np.asarray(L0_attn_x, dtype=np.float32)).reshape(-1, D)
    NCORE = 8
    if "nc" not in _CACHE:
        _CACHE["nc"] = Builder(TOK=TOK, D=D, E=E, INTER=INTER).build()
        _CACHE["masks"] = _masks(TOK)
    nc = _CACHE["nc"]
    mev, mod = _CACHE["masks"]

    idenb = np.eye(128, dtype=ml_dtypes.bfloat16)
    idenf = np.eye(128, dtype=np.float32)
    ioteps = np.broadcast_to((np.arange(E) * -1e-6).astype(np.float32),
                             (128, E)).copy()
    common = dict(
        wq=np.ascontiguousarray(L0_attn_Wq, np.float32),
        wk=np.ascontiguousarray(L0_attn_Wk, np.float32),
        wv=np.ascontiguousarray(L0_attn_Wv, np.float32),
        wo=np.ascontiguousarray(L0_attn_Wo, np.float32),
        wg=np.ascontiguousarray(L0_ffn_Wg, np.float32),
        w1=np.ascontiguousarray(L0_ffn_W1, np.float32),
        b1=np.ascontiguousarray(
            np.asarray(L0_ffn_b1, np.float32).reshape(E, INTER // 128, 128)
            .transpose(2, 0, 1)),
        w2=np.ascontiguousarray(L0_ffn_W2, np.float32),
        b2=np.ascontiguousarray(L0_ffn_b2, np.float32),
        idenb=idenb, idenf=idenf, ioteps=ioteps,
    )
    in_maps = []
    for c in range(NCORE):
        xm = x[c * TOK:(c + 1) * TOK]
        xh = (np.zeros((512, D), np.float32) if c % 2 == 0
              else x[c * TOK - 512: c * TOK])
        in_maps.append(dict(common, x_main=np.ascontiguousarray(xm),
                            x_halo=np.ascontiguousarray(xh),
                            mask=(mev if c % 2 == 0 else mod)))
    res = run_bass_kernel_spmd(nc, in_maps, core_ids=list(range(NCORE)))
    out = np.concatenate([r["yout"] for r in res.results], axis=0)
    return out.reshape(4, 4096, 1024).astype(np.float32)


# version 08
# speedup vs baseline: 1.3235x; 1.2811x over previous
import numpy as np
import ml_dtypes
from contextlib import ExitStack

import concourse.bass as bass
import concourse.mybir as mybir
from concourse.bass_utils import run_bass_kernel_spmd
from concourse.library_config import local_scatter as LIB_LSC

F32 = mybir.dt.float32
BF16 = mybir.dt.bfloat16
F16 = mybir.dt.float16
I16 = mybir.dt.int16
I32 = mybir.dt.int32
AF = mybir.ActivationFunctionType
ALU = mybir.AluOpType
AX = mybir.AxisListType

NEG = -1.0e9


class Ctr:

    def __init__(self, nc, ctx, name, step=1):
        self.sem = ctx.enter_context(nc.semaphore(name))
        self.step = step
        self.n = 0

    def inc(self, inst):
        inst.then_inc(self.sem, self.step)
        self.n += self.step
        return self.n


class DStream:

    def __init__(self, nc, ctx, name, depth=1):
        self.sems = [ctx.enter_context(nc.semaphore(f"{name}_{i}"))
                     for i in range(depth)]
        self.n = [0] * depth

    def inc(self, inst, slot=0):
        inst.then_inc(self.sems[slot], 16)
        self.n[slot] += 16
        return (slot, self.n[slot])

    def wait(self, eng, tok):
        eng.wait_ge(self.sems[tok[0]], tok[1])

    def wait_all(self, eng):
        for i, s in enumerate(self.sems):
            if self.n[i]:
                eng.wait_ge(s, self.n[i])


class _TE:

    def __init__(self, t):
        self._t = t

    def __getattr__(self, n):
        return getattr(self._t, n)

    def matmul(self, *a, **kw):
        kw.setdefault("skip_group_check", True)
        return self._t.matmul(*a, **kw)

    def transpose(self, out, in_, identity, tile_position=None):
        return self._t.matmul(out, in_, identity, start=True, stop=True,
                              is_transpose=True, skip_group_check=True,
                              tile_position=tile_position)


class Builder:
    def __init__(self, TOK=2048, D=1024, H=16, BS=64, WIN=512, E=8, INTER=4096):
        import os as _os
        self.dbg = set(_os.environ.get("KDBG", "").split(","))
        self.TOK, self.D, self.H, self.BS, self.WIN = TOK, D, H, BS, WIN
        self.E, self.INTER = E, INTER
        self.HALO = WIN
        self.HALF = TOK // 2
        self.HQB = self.HALF // BS
        self.POFF = self.HQB // 2
        assert self.POFF % 2 == 0
        self.NP = self.POFF
        self.KV = self.HALF + self.HALO
        self.KVT = self.KV // 128
        self.NTT = TOK // 128
        self.DT = D // 128
        self.WINW = WIN + BS
        self.IC = INTER // 512
        self.ITL = INTER // 128
        self.NPAIR_T = 2 * self.NP
        self.CAP = 768
        self.CT = self.CAP // 128
        self.st = {}

    def build(self):
        TOK, D, E, INTER = self.TOK, self.D, self.E, self.INTER
        DT, ITL, WINW = self.DT, self.ITL, self.WINW
        nc = self.nc = bass.Bass("TRN2", target_bir_lowering=False)

        self.x_main = nc.dram_tensor("x_main", [TOK, D], F32, kind="ExternalInput")
        self.x_halo = nc.dram_tensor("x_halo", [self.HALO, D], F32, kind="ExternalInput")
        self.w_in = [nc.dram_tensor(n, [D, D], F32, kind="ExternalInput")
                     for n in ("wq", "wk", "wv", "wo")]
        self.wg = nc.dram_tensor("wg", [E, D], F32, kind="ExternalInput")
        self.w1 = nc.dram_tensor("w1", [E, D, INTER], F32, kind="ExternalInput")
        self.b1 = nc.dram_tensor("b1", [128, E, INTER // 128], F32,
                                 kind="ExternalInput")
        self.w2 = nc.dram_tensor("w2", [E, INTER, D], F32, kind="ExternalInput")
        self.b2 = nc.dram_tensor("b2", [E, D], F32, kind="ExternalInput")
        self.mask = nc.dram_tensor("mask", [self.NPAIR_T, 128, WINW], BF16,
                                   kind="ExternalInput")
        self.idenb_d = nc.dram_tensor("idenb", [128, 128], BF16, kind="ExternalInput")
        self.idenf_d = nc.dram_tensor("idenf", [128, 128], F32, kind="ExternalInput")
        self.iot_d = nc.dram_tensor("ioteps", [128, E], F32, kind="ExternalInput")
        self.yout = nc.dram_tensor("yout", [TOK, D], F32, kind="ExternalOutput")
        self.wT_scr = [nc.dram_tensor(f"wT{i}_scr", [DT, 128, D], BF16, kind="Internal")
                       for i in range(4)]
        self.x1_scr = nc.dram_tensor("x1_scr", [TOK, D], F32, kind="Internal")
        self.y_scr = nc.dram_tensor("y_scr", [2 * TOK + 128, D], BF16,
                                    kind="Internal")
        self.iot16_d = nc.dram_tensor("iota16", [16, TOK], F16,
                                      kind="ExternalInput")

        ctx = ExitStack()
        with ctx:
            e = ctx.enter_context
            self.idb = e(nc.sbuf_tensor("idb", [128, 128], BF16))
            self.idf = e(nc.sbuf_tensor("idf", [128, 128], F32))
            self.b1sb = e(nc.sbuf_tensor("b1sb", [128, E, ITL], F32))
            self.b2sb = e(nc.sbuf_tensor("b2sb", [E, D], BF16))
            self.wgT = e(nc.sbuf_tensor("wgT", [128, DT, E], BF16))
            self.iot = e(nc.sbuf_tensor("iot", [128, E], F32))
            self.gates = e(nc.sbuf_tensor("gates", [128, self.NTT, E], F32))
            self.gatesT = e(nc.sbuf_tensor("gatesT", [16, TOK], BF16))
            self.k2T = e(nc.sbuf_tensor("k2T", [16, TOK], F16))
            self.IxI = e(nc.sbuf_tensor("IxI", [128, self.CT, 16], I32))
            self.IdSI = e(nc.sbuf_tensor("IdSI", [128, self.CT, 16], I32))
            self.wk = e(nc.sbuf_tensor("wktk", [128, self.NTT, 2], F32))
            self.zbuf = e(nc.sbuf_tensor("zbuf", [128, 1024], BF16))
            self.w1sl = e(nc.sbuf_tensor("w1sl", [128, 2, DT, 512], BF16))
            self.w2sl = e(nc.sbuf_tensor("w2sl", [128, 2, 4, 256], BF16))

            self.cPE = Ctr(nc, ctx, "cPE")
            self.cDV = Ctr(nc, ctx, "cDV")
            self.cAC = Ctr(nc, ctx, "cAC")
            self.cGP = Ctr(nc, ctx, "cGP")
            self.g_wld = DStream(nc, ctx, "g_wld", 2)
            self.g_wg = DStream(nc, ctx, "g_wg")
            self.g_bias = DStream(nc, ctx, "g_bias")
            self.g_xld = DStream(nc, ctx, "g_xld", 2)
            self.g_wTa = DStream(nc, ctx, "g_wTa")
            self.g_woT = DStream(nc, ctx, "g_woT")
            self.g_w1 = DStream(nc, ctx, "g_w1", 2)
            self.g_w2 = DStream(nc, ctx, "g_w2", 2)
            self.s_iden = DStream(nc, ctx, "s_iden")
            self.s_p0st = DStream(nc, ctx, "s_p0st", 2)
            self.s_mask = DStream(nc, ctx, "s_mask", 2)
            self.s_x0 = DStream(nc, ctx, "s_x0", 1)
            self.s_x1 = DStream(nc, ctx, "s_x1", 2)
            self.s_yin = DStream(nc, ctx, "s_yin")
            self.s_out = DStream(nc, ctx, "s_out", 2)
            self.s_yz = DStream(nc, ctx, "s_yz")
            self.s_yl = DStream(nc, ctx, "s_yl", 4)
            self.g_iot = DStream(nc, ctx, "g_iot")
            self.g_xg = DStream(nc, ctx, "g_xg", 6)
            self.g_sc = DStream(nc, ctx, "g_sc", 1)

            self.block = e(nc.Block())

            self.p0(ctx)
            x1fm_scope = nc.sbuf_tensor("X1fm", [128, DT, TOK], BF16)
            self.X1fm = x1fm_scope.__enter__()
            with nc.sbuf_tensor("Xfm", [128, DT, self.KV], BF16) as Xfm, \
                 nc.sbuf_tensor("Qf", [128, DT, self.HALF], BF16) as Qf, \
                 nc.sbuf_tensor("Kf", [128, DT, self.KV], BF16) as Kf, \
                 nc.sbuf_tensor("Vt", [128, self.KVT, D], BF16) as Vt, \
                 nc.sbuf_tensor("wTa", [128, DT, D], BF16) as wTa, \
                 nc.sbuf_tensor("xld", [128, 2, D], BF16) as xld, \
                 nc.sbuf_tensor("x0r", [128, 1, D], F32) as x0r, \
                 nc.sbuf_tensor("Pbf", [128, 2, WINW], BF16) as Pbf, \
                 nc.sbuf_tensor("PTb", [128, 2, 5, 128], BF16) as PTb, \
                 nc.sbuf_tensor("attnC", [128, 2, DT, 128], BF16) as attnC, \
                 nc.sbuf_tensor("x1c", [128, 2, D], F32) as x1c, \
                 nc.sbuf_tensor("maskb", [128, 2, WINW], BF16) as maskb, \
                 nc.sbuf_tensor("smal", [128, 2, 16], F32) as smal, \
                 nc.sbuf_tensor("sqj", [128, D], BF16) as sqj:
                self.Xfm, self.Qf, self.Kf, self.Vt = Xfm, Qf, Kf, Vt
                self.wTa, self.woT, self.xld, self.x0r = wTa, wTa, xld, x0r
                self.Pbf, self.PTb, self.attnC = Pbf, PTb, attnC
                self.x1c, self.maskb, self.smal, self.sqj = x1c, maskb, smal, sqj
                for ha in range(2):
                    self.half(ha)

            self.gates_phase()
            x1fm_scope.__exit__(None, None, None)
            with nc.sbuf_tensor("y_tm", [128, self.NTT, D], F32) as y_tm:
                self.y_tm = y_tm
                self.ffn()
                with nc.sbuf_tensor("outst", [128, 2, D], F32) as outst, \
                     nc.sbuf_tensor("lnt2", [128, 16], F32) as lnt2, \
                     nc.sbuf_tensor("sqj2", [128, D], BF16) as sqj2:
                    self.outst, self.lnt2, self.sqj2 = outst, lnt2, sqj2
                    self.ln2()
        mybir.codegen_inst_isa_subclasses(nc)
        return nc

    def on(self, eng, fn):
        if eng == "tensor":
            getattr(self.block, eng)(lambda t: fn(_TE(t)))
        else:
            getattr(self.block, eng)(fn)

    def p0(self, ctx):
        nc, st = self.nc, self.st
        DT, D, E, ITL, INTER = self.DT, self.D, self.E, self.ITL, self.INTER
        cPE, cDV = self.cPE, self.cDV

        with nc.sbuf_tensor("wld", [128, 2, D], BF16) as wld, \
             nc.sbuf_tensor("wstg", [128, 2, DT, 128], BF16) as wstg, \
             nc.sbuf_tensor("wgld", [E, D], BF16) as wgld, \
             nc.psum_tensor("wtp", [128, 2, D], BF16) as wtp, \
             nc.psum_tensor("wgp", [128, DT * E], BF16) as wgp:

            def dv_z(v):
                cDV.inc(v.memset(self.zbuf[:, :], 0.0))
                st["zb"] = cDV.n

            self.on("vector", dv_z)

            def sy0(sy):
                self.s_iden.inc(sy.dma_start(self.idb[:, :], self.idenb_d[:, :]))
                self.s_iden.inc(sy.dma_start(self.idf[:, :], self.idenf_d[:, :]))
                self.s_iden.inc(sy.dma_start(self.iot[:, :], self.iot_d[:, :]))
                sy.wait_ge(cDV.sem, st["zb"])
                nzr = (2 * self.TOK + 128) // 128
                for r in range(nzr):
                    self.s_yz.inc(sy.dma_start(
                        self.y_scr[r * 128:(r + 1) * 128, :], self.zbuf[:, :]))

            def gp0a(gp):
                st["wgld"] = self.g_wg.inc(gp.dma_start(wgld[:, :], self.wg[:, :]))
                self.g_bias.inc(gp.dma_start(self.b1sb[:, :, :], self.b1[:, :, :]))
                self.g_bias.inc(gp.dma_start(self.b2sb[:, :], self.b2[:, :]))

            self.on("sync", sy0)
            self.on("gpsimd", gp0a)

            for i in range(4 * DT):
                wi, rt = divmod(i, DT)
                s = i % 2

                def gp0(gp, i=i, wi=wi, rt=rt, s=s):
                    if i >= 2:
                        gp.wait_ge(cPE.sem, st[("p0tp", i - 2)])
                    st[("p0ld", i)] = self.g_wld.inc(gp.dma_start(
                        wld[:, s, :],
                        self.w_in[wi][rt * 128:(rt + 1) * 128, :]), s)

                def pe0(t, i=i, s=s):
                    if i == 0:
                        self.s_iden.wait_all(t)
                    self.g_wld.wait(t, st[("p0ld", i)])
                    if i >= 2:
                        t.wait_ge(cDV.sem, st[("p0ev", i - 2)])
                    last = None
                    for c in range(DT):
                        last = t.transpose(
                            wtp[:, s, c * 128:(c + 1) * 128],
                            wld[:, s, c * 128:(c + 1) * 128],
                            self.idb[:, :])
                    cPE.inc(last)
                    st[("p0tp", i)] = cPE.n

                def dv0(v, i=i, s=s):
                    v.wait_ge(cPE.sem, st[("p0tp", i)])
                    if i >= 2:
                        self.s_p0st.wait(v, st[("p0st", i - 2)])
                    cDV.inc(v.tensor_copy(
                        wstg[:, s, :, :],
                        bass.AP(wtp, s * D, [[2 * D, 128], [128, DT], [1, 128]])))
                    st[("p0ev", i)] = cDV.n

                def sy0b(sy, i=i, wi=wi, rt=rt, s=s):
                    sy.wait_ge(cDV.sem, st[("p0ev", i)])
                    st[("p0st", i)] = self.s_p0st.inc(sy.dma_start(
                        bass.AP(self.wT_scr[wi], rt * 128,
                                [[D, 128], [128 * D, DT], [1, 128]]),
                        wstg[:, s, :, :]), s)

                self.on("gpsimd", gp0)
                self.on("tensor", pe0)
                self.on("vector", dv0)
                self.on("sync", sy0b)

            def pe0g(t):
                self.g_wg.wait(t, st["wgld"])
                last = None
                for c in range(DT):
                    last = t.transpose(wgp[:, c * E:(c + 1) * E],
                                       wgld[:, c * 128:(c + 1) * 128],
                                       self.idb[0:E, 0:E])
                cPE.inc(last)
                st["wgtp"] = cPE.n

            def dv0g(v):
                v.wait_ge(cPE.sem, st["wgtp"])
                cDV.inc(v.tensor_copy(
                    self.wgT[:, :, :],
                    bass.AP(wgp, 0, [[DT * E, 128], [E, DT], [1, E]])))
                st["wgev"] = cDV.n

            self.on("tensor", pe0g)
            self.on("vector", dv0g)

    def half(self, ha):
        nc, st = self.nc, self.st
        D, DT, KV, KVT, HALF, HALO = self.D, self.DT, self.KV, self.KVT, self.HALF, self.HALO
        BS, NP, POFF, WINW, TOK = self.BS, self.NP, self.POFF, self.WINW, self.TOK
        cPE, cDV, cAC = self.cPE, self.cDV, self.cAC
        Xfm, Qf, Kf, Vt, wTa, woT = self.Xfm, self.Qf, self.Kf, self.Vt, self.wTa, self.woT
        xld, x0r = self.xld, self.x0r

        with nc.psum_tensor(f"xtp{ha}", [128, D], BF16) as xtp, \
             nc.psum_tensor(f"prj{ha}", [128, 2, 512], F32) as prj:

            def kv_src(kvt):
                base = ha * HALF - HALO + kvt * 128
                if base < 0:
                    return (self.x_halo, base + HALO)
                return (self.x_main, base)

            for kvt in range(KVT):
                def gp_x(gp, kvt=kvt):
                    if kvt >= 2:
                        gp.wait_ge(cPE.sem, st[("xT", ha, kvt - 2)])
                    elif ha == 1:
                        gp.wait_ge(cPE.sem, st[("xT", 0, KVT - 2 + kvt)])
                    else:
                        gp.wait_ge(cPE.sem, st["wgtp"])
                        self.s_p0st.wait_all(gp)
                    src, r0 = kv_src(kvt)
                    st[("xld", ha, kvt)] = self.g_xld.inc(
                        gp.dma_start(xld[:, kvt % 2, :],
                                     src[r0:r0 + 128, :]), kvt % 2)

                def pe_xT(t, kvt=kvt):
                    if ha == 0 and kvt == 0:
                        t.wait_ge(cDV.sem, st["wgev"])
                    self.g_xld.wait(t, st[("xld", ha, kvt)])
                    if kvt >= 1:
                        t.wait_ge(cDV.sem, st[("xTe", ha, kvt - 1)])
                    elif ha == 1:
                        t.wait_ge(cDV.sem, st[("xTe", 0, KVT - 1)])
                    last = None
                    for c in range(DT):
                        last = t.transpose(xtp[:, c * 128:(c + 1) * 128],
                                           xld[:, kvt % 2, c * 128:(c + 1) * 128],
                                           self.idb[:, :])
                    cPE.inc(last)
                    st[("xT", ha, kvt)] = cPE.n

                def dv_xT(v, kvt=kvt):
                    v.wait_ge(cPE.sem, st[("xT", ha, kvt)])
                    if ha == 1 and kvt == 0:
                        v.wait_ge(cPE.sem, st[("proj_done", 0, 2)])
                    elif ha == 0 and kvt == 0:
                        self.s_p0st.wait_all(v)
                    cDV.inc(v.tensor_copy(
                        Xfm[:, :, kvt * 128:(kvt + 1) * 128],
                        bass.AP(xtp, 0, [[D, 128], [128, DT], [1, 128]])))
                    st[("xTe", ha, kvt)] = cDV.n

                self.on("gpsimd", gp_x)
                self.on("tensor", pe_xT)
                self.on("vector", dv_xT)

            plan = []
            QCH, KCH = HALF // 512, KV // 512
            for dq in range(DT):
                for ch in range(QCH):
                    plan.append((0, "q", dq, ch))
            for dq in range(DT):
                for ch in range(KCH):
                    plan.append((1, "k", dq, ch))
            for kvt in range(KVT):
                for ch in range(2):
                    plan.append((2, "v", kvt, ch))
            NPL = len(plan)

            for i, (wi, kind, a, chn) in enumerate(plan):
                if i == 0 or plan[i - 1][0] != wi:
                    def gp_w(gp, wi=wi):
                        if (ha, wi) == (0, 0):
                            self.s_p0st.wait_all(gp)
                        elif wi == 0:
                            gp.wait_ge(cPE.sem, st[("opj_pe", NP - 1, 1)])
                        else:
                            gp.wait_ge(cPE.sem, st[("proj_done", ha, wi - 1)])
                        st[("wT", ha, wi)] = self.g_wTa.inc(gp.dma_start(
                            wTa[:, :, :],
                            bass.AP(self.wT_scr[wi], 0,
                                    [[D, 128], [128 * D, DT], [1, D]])))

                    self.on("gpsimd", gp_w)

                def pe_proj(t, i=i, wi=wi, kind=kind, a=a, chn=chn):
                    if i == 0 or plan[i - 1][0] != wi:
                        self.g_wTa.wait(t, st[("wT", ha, wi)])
                    slot = i % 2
                    if i >= 2:
                        t.wait_ge(cDV.sem, st[("prj_ev", ha, i - 2)])
                    elif ha == 1:
                        t.wait_ge(cDV.sem, st[("prj_ev", 0, NPL - 2 + i)])
                    last = None
                    if kind == "q":
                        t.wait_ge(cDV.sem,
                                  st[("xTe", ha, HALO // 128 + (chn + 1) * 4 - 1)])
                        for k in range(DT):
                            last = t.matmul(
                                prj[:, slot, :],
                                wTa[:, k, a * 128:(a + 1) * 128],
                                Xfm[:, k, HALO + chn * 512: HALO + (chn + 1) * 512],
                                start=(k == 0), stop=(k == DT - 1))
                    elif kind == "k":
                        t.wait_ge(cDV.sem, st[("xTe", ha, (chn + 1) * 4 - 1)])
                        for k in range(DT):
                            last = t.matmul(
                                prj[:, slot, :],
                                wTa[:, k, a * 128:(a + 1) * 128],
                                Xfm[:, k, chn * 512: (chn + 1) * 512],
                                start=(k == 0), stop=(k == DT - 1))
                    else:
                        t.wait_ge(cDV.sem, st[("xTe", ha, a)])
                        for k in range(DT):
                            last = t.matmul(
                                prj[:, slot, :],
                                Xfm[:, k, a * 128:(a + 1) * 128],
                                wTa[:, k, chn * 512: (chn + 1) * 512],
                                start=(k == 0), stop=(k == DT - 1))
                    cPE.inc(last)
                    st[("prj_pe", ha, i)] = cPE.n
                    if i == NPL - 1 or plan[i + 1][0] != wi:
                        st[("proj_done", ha, wi)] = cPE.n

                def dv_proj(v, i=i, kind=kind, a=a, chn=chn):
                    slot = i % 2
                    v.wait_ge(cPE.sem, st[("prj_pe", ha, i)])
                    if kind == "q":
                        cDV.inc(v.tensor_scalar_mul(
                            Qf[:, a, chn * 512:(chn + 1) * 512],
                            prj[:, slot, :], 0.125))
                    elif kind == "k":
                        cDV.inc(v.tensor_copy(
                            Kf[:, a, chn * 512:(chn + 1) * 512],
                            prj[:, slot, :]))
                    else:
                        cDV.inc(v.tensor_copy(
                            Vt[:, a, chn * 512:(chn + 1) * 512],
                            prj[:, slot, :]))
                    st[("prj_ev", ha, i)] = cDV.n

                self.on("tensor", pe_proj)
                self.on("vector", dv_proj)
            st[("qkv_dv", ha)] = cDV.n

            def gp_wo(gp):
                gp.wait_ge(cPE.sem, st[("proj_done", ha, 2)])
                st[("woT", ha)] = self.g_woT.inc(gp.dma_start(
                    wTa[:, :, :],
                    bass.AP(self.wT_scr[3], 0,
                            [[D, 128], [128 * D, DT], [1, D]])))

            self.on("gpsimd", gp_wo)

        with nc.psum_tensor(f"Sp0{ha}", [128, WINW], F32) as Sp0, \
             nc.psum_tensor(f"Sp1{ha}", [128, WINW], F32) as Sp1, \
             nc.psum_tensor(f"PTp{ha}", [128, 5, 128], BF16) as PTp, \
             nc.psum_tensor(f"pv0{ha}", [128, 512], F32) as pv0, \
             nc.psum_tensor(f"pv1{ha}", [128, 512], F32) as pv1, \
             nc.psum_tensor(f"x1p{ha}", [128, 512], F32) as x1p:
            Sp = [Sp0, Sp1]
            pvp = [pv0, pv1]

            def pe_ptinit(t):
                cPE.inc(t.transpose(PTp[:, 4, :], self.idb[:, :],
                                    self.idb[:, :]))
                st[("ptinit", ha)] = cPE.n

            self.on("tensor", pe_ptinit)
            for p in range(NP):
                self.pair(ha, p, Sp, PTp, pvp, x1p)
            self.tail(ha, NP - 1, pvp, x1p)

        st[("half_done", ha)] = cDV.n

    def pair(self, ha, p, Sp, PTp, pvp, x1p):
        nc, st = self.nc, self.st
        D, DT, HALF, BS, NP, POFF, WINW = (self.D, self.DT, self.HALF, self.BS,
                                           self.NP, self.POFF, self.WINW)
        TOK = self.TOK
        cPE, cDV, cAC = self.cPE, self.cDV, self.cAC
        Qf, Kf, Vt = self.Qf, self.Kf, self.Vt
        Pbf, PTb, attnC = self.Pbf, self.PTb, self.attnC
        maskb, smal = self.maskb, self.smal
        H_ = self.H

        pg = ha * NP + p
        s2 = p % 2
        qA, qB = p, p + POFF
        win0, win1 = qA * BS, qB * BS
        odd = (qA % 2 == 1)
        S_pe, S_ev, exp_ac, nrm_dv, PT_pe, PT_ev, pv_pe = \
            [], [], [], [], [], [], []
        pv_started = set()

        def sy_pair(sy):
            if pg >= 1:
                sy.wait_ge(cDV.sem, st[("x0_free", pg - 1)])
            r0 = ha * HALF + qA * BS
            r1 = ha * HALF + qB * BS
            self.s_x0.inc(sy.dma_start(self.x0r[0:64, 0, :],
                                       self.x_main[r0:r0 + 64, :]))
            st[("x0ld", pg)] = self.s_x0.inc(
                sy.dma_start(self.x0r[64:128, 0, :],
                             self.x_main[r1:r1 + 64, :]))

        def pe_S(t, h):
            hp, hh = divmod(h, 2)
            r0 = hh * 64
            hs = h % 2
            if h == 0 and p == 0:
                t.wait_ge(cDV.sem, st[("qkv_dv", ha)])
                self.g_woT.wait(t, st[("woT", ha)])
            if h == 0:
                self.s_mask.wait(t, st[("mload", pg)])
            if h >= 2:
                t.wait_ge(cAC.sem, exp_ac[h - 2])
            elif pg >= 1:
                t.wait_ge(cAC.sem, st[("exp_last", pg - 1)])
            t.matmul(Sp[hs][0:64, 0:512],
                     Qf[r0:r0 + 64, hp, qA * 64:(qA + 1) * 64],
                     Kf[r0:r0 + 64, hp, win0: win0 + 512],
                     start=True, stop=False, tile_position=(r0, 0))
            t.matmul(Sp[hs][64:128, 0:512],
                     Qf[r0:r0 + 64, hp, qB * 64:(qB + 1) * 64],
                     Kf[r0:r0 + 64, hp, win1: win1 + 512],
                     start=True, stop=False, tile_position=(r0, 64))
            t.matmul(Sp[hs][:, 0:512], self.idb[:, :], maskb[:, s2, 0:512],
                     start=False, stop=True)
            t.matmul(Sp[hs][0:64, 512:WINW],
                     Qf[r0:r0 + 64, hp, qA * 64:(qA + 1) * 64],
                     Kf[r0:r0 + 64, hp, win0 + 512: win0 + 576],
                     start=True, stop=False, tile_position=(r0, 0))
            t.matmul(Sp[hs][64:128, 512:WINW],
                     Qf[r0:r0 + 64, hp, qB * 64:(qB + 1) * 64],
                     Kf[r0:r0 + 64, hp, win1 + 512: win1 + 576],
                     start=True, stop=False, tile_position=(r0, 64))
            last = t.matmul(Sp[hs][:, 512:WINW], self.idb[:, :],
                            maskb[:, s2, 512:WINW],
                            start=False, stop=True)
            cPE.inc(last)
            S_pe.append(cPE.n)
            st[("S_pe", pg, h)] = cPE.n

        def dv_S(v, h):
            hs = h % 2
            v.wait_ge(cPE.sem, S_pe[h])
            if h >= 2:
                v.wait_ge(cAC.sem, exp_ac[h - 2])
            elif pg >= 1:
                v.wait_ge(cAC.sem, st[("exp_last", pg - 1)])
            cDV.inc(v.reduce_max(smal[:, hs, 0:1], Sp[hs][:, :],
                                 axis=AX.X, negate=True))
            S_ev.append(cDV.n)
            st[("S_ev", pg, h)] = cDV.n

        def ac_exp(s, h):
            hs = h % 2
            s.wait_ge(cDV.sem, S_ev[h])
            if h >= 2:
                s.wait_ge(cPE.sem, PT_pe[h - 2])
            elif pg >= 1:
                s.wait_ge(cPE.sem, st[("PT_last", pg - 1)])
            cAC.inc(s.activation(Pbf[:, hs, :], Sp[hs][:, :],
                                 AF.Exp, bias=smal[:, hs, 0:1],
                                 accum_out=smal[:, hs, 1:2]))
            exp_ac.append(cAC.n)

        def dv_nrm(v, h):
            hs = h % 2
            v.wait_ge(cAC.sem, exp_ac[h])
            cDV.inc(v.reciprocal(smal[:, hs, 2:3], smal[:, hs, 1:2]))
            v.wait_ge(cDV.sem, cDV.n)
            cDV.inc(v.tensor_scalar_mul(Pbf[:, hs, :], Pbf[:, hs, :],
                                        smal[:, hs, 2:3]))
            nrm_dv.append(cDV.n)

        def pe_PT(t, h):
            hs = h % 2
            t.wait_ge(cDV.sem, nrm_dv[h])
            if h >= 1:
                t.wait_ge(cDV.sem, PT_ev[h - 1])
            elif pg >= 1:
                t.wait_ge(cDV.sem, st[("PT_ev_last", pg - 1)])
            else:
                t.wait_ge(cPE.sem, st[("ptinit", ha)])
            last = None
            if not odd:
                for cc in range(4):
                    last = t.transpose(PTp[:, cc, :],
                                       Pbf[:, hs, cc * 128:(cc + 1) * 128],
                                       self.idb[:, :])
                last = t.transpose(PTp[0:64, 4, :],
                                   Pbf[:, hs, 512:576], self.idb[:, :])
            else:
                last = t.transpose(PTp[64:128, 0, :],
                                   Pbf[:, hs, 0:64], self.idb[:, :],
                                   tile_position=(0, 64))
                for cc in range(4):
                    last = t.transpose(
                        PTp[:, cc + 1, :],
                        Pbf[:, hs, 64 + cc * 128: 64 + (cc + 1) * 128],
                        self.idb[:, :])
            cPE.inc(last)
            PT_pe.append(cPE.n)
            st[("PT_pe", pg, h)] = cPE.n

        def dv_PT(v, h):
            hs = h % 2
            v.wait_ge(cPE.sem, PT_pe[h])
            if h >= 2:
                v.wait_ge(cPE.sem, pv_pe[h - 2])
            elif pg >= 1:
                v.wait_ge(cPE.sem, st[("pv_last", pg - 1)])
            cDV.inc(v.tensor_copy(PTb[:, hs, :, :], PTp[:, :, :]))
            PT_ev.append(cDV.n)

        def pe_PV(t, h):
            hs = h % 2
            hp, hh = divmod(h, 2)
            bank = pvp[hp // 4]
            fo = (hp % 4) * 128
            t.wait_ge(cDV.sem, PT_ev[h])
            if h == 0 and pg >= 1:
                t.wait_ge(cDV.sem, st[("attn_ev", pg - 1)])
            if h == 8 and pg >= 1:
                t.wait_ge(cDV.sem, st[("res_dv", pg - 1, 1)])
            lastmm = None
            for qi, qb in enumerate((qA, qB)):
                win = qb * BS
                if not odd:
                    chunks = [(win // 128 + cc, 0, 128, cc) for cc in range(4)]
                    chunks.append(((win + 512) // 128, 0, 64, 4))
                else:
                    chunks = [(win // 128, 64, 64, 0)]
                    chunks += [((win + 64) // 128 + cc, 0, 128, cc + 1)
                               for cc in range(4)]
                last_h = (hp // 4) * 8 + 6 + hh
                for ci, (vt_t, pr0, klen, slot) in enumerate(chunks):
                    key = (hp // 4, hh)
                    first = key not in pv_started
                    pv_started.add(key)
                    stop = (h == last_h and qi == 1 and ci == len(chunks) - 1)
                    lastmm = t.matmul(
                        bank[hh * 64:(hh + 1) * 64,
                             fo + qi * 64: fo + (qi + 1) * 64],
                        Vt[pr0:pr0 + klen, vt_t, h * 64:(h + 1) * 64],
                        PTb[pr0:pr0 + klen, hs, slot, qi * 64:(qi + 1) * 64],
                        start=first, stop=stop, tile_position=(pr0, hh * 64))
            cPE.inc(lastmm)
            pv_pe.append(cPE.n)
            st[("pv_pe", pg, h)] = cPE.n

        def sy_mask(sy, mpg):
            ms = mpg % 2
            if mpg >= 2:
                sy.wait_ge(cPE.sem, st[("mask_free", mpg - 2)])
            st[("mload", mpg)] = self.s_mask.inc(
                sy.dma_start(maskb[:, ms, :], self.mask[mpg, :, :]), ms)

        if p == 0:
            self.on("sync", lambda sy: sy_mask(sy, pg))
        for i in range(H_ + 1):
            def pe_c1(t, i=i):
                if i == 0:
                    pe_S(t, 0)
                    pe_S(t, 1)
                else:
                    if i + 1 < H_:
                        pe_S(t, i + 1)
                    pe_PT(t, i - 1)
            self.on("tensor", pe_c1)
            if i == 0 and p >= 1:
                self.emit_tail_pe_opj(ha, p - 1, pvp)
            if i == 0:
                self.on("sync", sy_pair)
            if i == 2:
                if p >= 1:
                    self.emit_tail_rest(ha, p - 1, x1p)
                    self.emit_tail_sync(ha, p - 1)
                if p + 1 < NP:
                    self.on("sync", lambda sy: sy_mask(sy, pg + 1))

            def dv_c2(v, i=i):
                if i >= 1:
                    dv_PT(v, i - 1)
                if i < H_:
                    dv_S(v, i)
            self.on("vector", dv_c2)
            if i >= 1:
                self.on("tensor", lambda t, i=i: pe_PV(t, i - 1))
            if i < H_:
                self.on("scalar", lambda s, i=i: ac_exp(s, i))
                self.on("vector", lambda v, i=i: dv_nrm(v, i))

        st[("exp_last", pg)] = exp_ac[H_ - 1]
        st[("PT_last", pg)] = PT_pe[H_ - 1]
        st[("PT_ev_last", pg)] = PT_ev[H_ - 1]
        st[("mask_free", pg)] = S_pe[H_ - 1]
        st[("pv_last", pg)] = pv_pe[H_ - 1]

        def dv_attn(v):
            v.wait_ge(cPE.sem, st[("pv_last", pg)])
            v.tensor_copy(attnC[:, s2, 0:4, :],
                          bass.AP(pvp[0], 0, [[512, 128], [128, 4], [1, 128]]))
            cDV.inc(v.tensor_copy(attnC[:, s2, 4:8, :],
                                  bass.AP(pvp[1], 0, [[512, 128], [128, 4], [1, 128]])))
            st[("attn_ev", pg)] = cDV.n

        self.on("vector", dv_attn)

    def tail(self, ha, p, pvp, x1p):
        self.emit_tail_pe_opj(ha, p, pvp)
        self.emit_tail_rest(ha, p, x1p)
        self.emit_tail_sync(ha, p)

    def emit_tail_pe_opj(self, ha, p, pvp):
        nc, st = self.nc, self.st
        D, DT, HALF, BS, NP, POFF = (self.D, self.DT, self.HALF, self.BS,
                                     self.NP, self.POFF)
        cPE, cDV, cAC = self.cPE, self.cDV, self.cAC
        attnC, wTa = self.attnC, self.woT
        pg = ha * NP + p
        s2 = p % 2

        def pe_opj0(t):
            t.wait_ge(cDV.sem, st[("attn_ev", pg)])
            last = None
            for k in range(DT):
                last = t.matmul(pvp[1][:, :], attnC[:, s2, k, :],
                                wTa[:, k, 0:512],
                                start=(k == 0), stop=(k == DT - 1))
            cPE.inc(last)
            st[("opj_pe", pg, 0)] = cPE.n

        def dv_res0(v):
            v.wait_ge(cPE.sem, st[("opj_pe", pg, 0)])
            self.s_x0.wait(v, st[("x0ld", pg)])
            if pg >= 2:
                self.s_x1.wait(v, st[("x1st", pg - 2)])
                v.wait_ge(cPE.sem, st[("x1T", pg - 2)])
                v.wait_ge(cAC.sem, st[("lnsq", pg - 2)])
            cDV.inc(v.tensor_add(self.x1c[:, s2, 0:512], pvp[1][:, :],
                                 self.x0r[:, 0, 0:512]))
            st[("res_dv", pg, 0)] = cDV.n

        def pe_opj1(t):
            t.wait_ge(cDV.sem, st[("res_dv", pg, 0)])
            last = None
            for k in range(DT):
                last = t.matmul(pvp[1][:, :], attnC[:, s2, k, :],
                                wTa[:, k, 512:1024],
                                start=(k == 0), stop=(k == DT - 1))
            cPE.inc(last)
            st[("opj_pe", pg, 1)] = cPE.n

        def dv_res1(v):
            v.wait_ge(cPE.sem, st[("opj_pe", pg, 1)])
            cDV.inc(v.tensor_add(self.x1c[:, s2, 512:1024], pvp[1][:, :],
                                 self.x0r[:, 0, 512:1024]))
            st[("res_dv", pg, 1)] = cDV.n
            v.wait_ge(cDV.sem, cDV.n)
            cDV.inc(v.reduce_sum(self.smal[:, s2, 4:5], self.x1c[:, s2, :],
                                 axis=AX.X))
            st[("lnsum", pg)] = cDV.n

        self.on("tensor", pe_opj0)
        self.on("vector", dv_res0)
        self.on("tensor", pe_opj1)
        self.on("vector", dv_res1)
        st[("x0_free", pg)] = st[("res_dv", pg, 1)]

    def emit_tail_rest(self, ha, p, x1p):
        nc, st = self.nc, self.st
        D, DT, HALF, BS, NP, POFF, TOK = (self.D, self.DT, self.HALF, self.BS,
                                          self.NP, self.POFF, self.TOK)
        cPE, cDV, cAC = self.cPE, self.cDV, self.cAC
        x1c, smal, sqj, X1fm = self.x1c, self.smal, self.sqj, self.X1fm
        pg = ha * NP + p
        s2 = p % 2
        qA = p

        def ac_lnsq(s):
            s.wait_ge(cDV.sem, st[("res_dv", pg, 1)])
            if pg >= 1:
                s.wait_ge(cAC.sem, st[("lnsq", pg - 1)])
            cAC.inc(s.activation(sqj[:, :], x1c[:, s2, :], AF.Square,
                                 accum_out=smal[:, s2, 5:6]))
            st[("lnsq", pg)] = cAC.n

        def dv_ln(v):
            v.wait_ge(cDV.sem, st[("lnsum", pg)])
            v.wait_ge(cAC.sem, st[("lnsq", pg)])
            cDV.inc(v.tensor_scalar_mul(smal[:, s2, 6:7], smal[:, s2, 4:5],
                                        1.0 / D))
            cDV.inc(v.tensor_scalar_mul(smal[:, s2, 7:8], smal[:, s2, 5:6],
                                        1.0 / D))
            v.wait_ge(cDV.sem, cDV.n)
            cDV.inc(v.tensor_mul(smal[:, s2, 8:9], smal[:, s2, 6:7],
                                 smal[:, s2, 6:7]))
            v.wait_ge(cDV.sem, cDV.n)
            cDV.inc(v.tensor_sub(smal[:, s2, 9:10], smal[:, s2, 7:8],
                                 smal[:, s2, 8:9]))
            v.wait_ge(cDV.sem, cDV.n)
            cDV.inc(v.tensor_scalar_add(smal[:, s2, 10:11], smal[:, s2, 9:10],
                                        1e-5))
            st[("lnvar", pg)] = cDV.n

        def ac_sqrt(s):
            s.wait_ge(cDV.sem, st[("lnvar", pg)])
            cAC.inc(s.activation(smal[:, s2, 11:12], smal[:, s2, 10:11],
                                 AF.Sqrt))
            st[("lnstd", pg)] = cAC.n

        def dv_ln2(v):
            v.wait_ge(cAC.sem, st[("lnstd", pg)])
            cDV.inc(v.reciprocal(smal[:, s2, 12:13], smal[:, s2, 11:12]))
            v.wait_ge(cDV.sem, cDV.n)
            cDV.inc(v.tensor_mul(smal[:, s2, 13:14], smal[:, s2, 6:7],
                                 smal[:, s2, 12:13]))
            v.wait_ge(cDV.sem, cDV.n)
            cDV.inc(v.tensor_scalar_mul(smal[:, s2, 14:15], smal[:, s2, 13:14],
                                        -1.0))
            st[("lnab", pg)] = cDV.n

        def ac_apply(s):
            s.wait_ge(cDV.sem, st[("lnab", pg)])
            cAC.inc(s.activation(x1c[:, s2, :], x1c[:, s2, :],
                                 AF.Identity, bias=smal[:, s2, 14:15],
                                 scale=smal[:, s2, 12:13]))
            st[("ln_done", pg)] = cAC.n

        def pe_x1T(t, half4):
            if half4 == 0:
                t.wait_ge(cAC.sem, st[("ln_done", pg)])
                if pg >= 1:
                    t.wait_ge(cDV.sem, st[("x1T_ev", pg - 1, 1)])
            else:
                t.wait_ge(cDV.sem, st[("x1T_ev", pg, 0)])
            last = None
            for c in range(4):
                cc = half4 * 4 + c
                last = t.transpose(x1p[:, c * 128:(c + 1) * 128],
                                   x1c[:, s2, cc * 128:(cc + 1) * 128],
                                   self.idf[:, :])
            cPE.inc(last)
            st[("x1T", pg, half4)] = cPE.n
            if half4 == 1:
                st[("x1T", pg)] = cPE.n

        def dv_x1T(v, half4):
            v.wait_ge(cPE.sem, st[("x1T", pg, half4)])
            dst = bass.AP(X1fm,
                          half4 * 4 * TOK + ha * HALF + qA * BS,
                          [[DT * TOK, 128], [TOK, 4], [POFF * BS, 2],
                           [1, 64]])
            src = bass.AP(x1p, 0, [[512, 128], [128, 4], [64, 2], [1, 64]])
            cDV.inc(v.tensor_copy(dst, src))
            st[("x1T_ev", pg, half4)] = cDV.n
            if half4 == 1:
                st[("x1T_ev", pg)] = cDV.n

        self.on("scalar", ac_lnsq)
        self.on("vector", dv_ln)
        self.on("scalar", ac_sqrt)
        self.on("vector", dv_ln2)
        self.on("scalar", ac_apply)
        self.on("tensor", lambda t: pe_x1T(t, 0))
        self.on("vector", lambda v: dv_x1T(v, 0))
        self.on("tensor", lambda t: pe_x1T(t, 1))
        self.on("vector", lambda v: dv_x1T(v, 1))

    def emit_tail_sync(self, ha, p):
        nc, st = self.nc, self.st
        HALF, BS, NP, POFF = self.HALF, self.BS, self.NP, self.POFF
        cAC = self.cAC
        pg = ha * NP + p
        s2 = p % 2
        qA, qB = p, p + POFF

        def sy_x1(sy):
            sy.wait_ge(cAC.sem, st[("ln_done", pg)])
            r0 = ha * HALF + qA * BS
            r1 = ha * HALF + qB * BS
            self.s_x1.inc(sy.dma_start(self.x1_scr[r0:r0 + 64, :],
                                       self.x1c[0:64, s2, :]), s2)
            st[("x1st", pg)] = self.s_x1.inc(
                sy.dma_start(self.x1_scr[r1:r1 + 64, :],
                             self.x1c[64:128, s2, :]), s2)

        self.on("sync", sy_x1)

    def gates_phase(self):
        nc, st = self.nc, self.st
        D, DT, E, NTT, TOK = self.D, self.DT, self.E, self.NTT, self.TOK
        cPE, cDV, cAC = self.cPE, self.cDV, self.cAC
        X1fm, wgT, iot, gates, gatesT, idf = (self.X1fm, self.wgT, self.iot,
                                              self.gates, self.gatesT, self.idf)
        last_x1 = st[("x1T_ev", 2 * self.NP - 1)]
        pe_at_g, dv_at_g, ac_at_g = self.cPE.n, self.cDV.n, self.cAC.n

        with nc.sbuf_tensor("gl", [128, 2, E], F32) as gl, \
             nc.sbuf_tensor("gsc", [128, 2, 8], F32) as gsc, \
             nc.psum_tensor("glp", [128, 2, E], F32) as glp, \
             nc.psum_tensor("gtp", [E, 4, 128], F32) as gtp:

            def dv_gz(v):
                v.wait_ge(cDV.sem, last_x1)
                cDV.inc(v.memset(self.gatesT[:, :], 0.0))
                cDV.inc(v.memset(self.k2T[:, :], 0.0))
                st["gz"] = cDV.n

            self.on("vector", dv_gz)
            for tt in range(NTT):
                s2 = tt % 2

                def pe_gl(t, tt=tt, s2=s2):
                    if tt == 0:
                        t.wait_ge(cDV.sem, last_x1)
                        t.wait_ge(cDV.sem, st["wgev"])
                    if tt >= 2:
                        t.wait_ge(cDV.sem, st[("glev", tt - 2)])
                    last = None
                    for k in range(DT):
                        last = t.matmul(glp[:, s2, :],
                                        X1fm[:, k, tt * 128:(tt + 1) * 128],
                                        wgT[:, k, :],
                                        start=(k == 0), stop=(k == DT - 1))
                    cPE.inc(last)
                    st[("glpe", tt)] = cPE.n

                def dv_g1(v, tt=tt, s2=s2):
                    v.wait_ge(cPE.sem, st[("glpe", tt)])
                    if tt == 0:
                        self.s_iden.wait_all(v)
                    cDV.inc(v.tensor_add(gl[:, s2, :],
                                         glp[:, s2, :], iot[:, :]))
                    st[("glev", tt)] = cDV.n
                    v.wait_ge(cDV.sem, cDV.n)
                    cDV.inc(v.reduce_max(gsc[:, s2, 0:1],
                                         gl[:, s2, :], axis=AX.X))
                    v.wait_ge(cDV.sem, cDV.n)
                    cDV.inc(v.tensor_single_scalar(gates[:, tt, :],
                                                   gl[:, s2, :],
                                                   gsc[:, s2, 0:1],
                                                   ALU.is_ge))
                    v.wait_ge(cDV.sem, cDV.n)
                    cDV.inc(v.scalar_tensor_tensor(gl[:, s2, :],
                                                   gates[:, tt, :], NEG,
                                                   gl[:, s2, :],
                                                   op0=ALU.mult, op1=ALU.add))
                    v.wait_ge(cDV.sem, cDV.n)
                    cDV.inc(v.reduce_max(gsc[:, s2, 1:2],
                                         gl[:, s2, :], axis=AX.X))
                    v.wait_ge(cDV.sem, cDV.n)
                    cDV.inc(v.tensor_sub(gsc[:, s2, 2:3],
                                         gsc[:, s2, 0:1],
                                         gsc[:, s2, 1:2]))
                    st[("gd", tt)] = cDV.n

                def ac_g(s, tt=tt, s2=s2):
                    s.wait_ge(cDV.sem, st[("gd", tt)])
                    cAC.inc(s.activation(gsc[:, s2, 3:4],
                                         gsc[:, s2, 2:3], AF.Exp,
                                         scale=-1.0))
                    st[("ge", tt)] = cAC.n

                def dv_g2(v, tt=tt, s2=s2):
                    v.wait_ge(cAC.sem, st[("ge", tt)])
                    cDV.inc(v.tensor_scalar_add(gsc[:, s2, 4:5],
                                                gsc[:, s2, 3:4], 1.0))
                    v.wait_ge(cDV.sem, cDV.n)
                    cDV.inc(v.reciprocal(gsc[:, s2, 5:6],
                                         gsc[:, s2, 4:5]))
                    v.wait_ge(cDV.sem, cDV.n)
                    cDV.inc(v.tensor_mul(gsc[:, s2, 6:7],
                                         gsc[:, s2, 3:4],
                                         gsc[:, s2, 5:6]))
                    v.wait_ge(cDV.sem, cDV.n)
                    cDV.inc(v.tensor_copy(self.wk[:, tt, 0:1],
                                          gsc[:, s2, 5:6]))
                    cDV.inc(v.tensor_copy(self.wk[:, tt, 1:2],
                                          gsc[:, s2, 6:7]))
                    cDV.inc(v.tensor_single_scalar(gl[:, s2, :],
                                                   gl[:, s2, :],
                                                   gsc[:, s2, 1:2],
                                                   ALU.is_ge))
                    v.wait_ge(cDV.sem, cDV.n)
                    cDV.inc(v.tensor_scalar_mul(gates[:, tt, :],
                                                gates[:, tt, :],
                                                gsc[:, s2, 5:6]))
                    v.wait_ge(cDV.sem, cDV.n)
                    cDV.inc(v.scalar_tensor_tensor(gates[:, tt, :],
                                                   gl[:, s2, :],
                                                   gsc[:, s2, 6:7],
                                                   gates[:, tt, :],
                                                   op0=ALU.mult, op1=ALU.add))
                    st[("gates_dv", tt)] = cDV.n

                def pe_gT(t, tt=tt, s2=s2):
                    t.wait_ge(cDV.sem, st[("gates_dv", tt)])
                    if tt >= 2:
                        t.wait_ge(cDV.sem, st[("gTev", tt - 2)])
                    t.transpose(gtp[:, s2, :], gates[:, tt, :], idf[:, :])
                    cPE.inc(t.transpose(gtp[:, 2 + s2, :],
                                        gl[:, s2, :], idf[:, :]))
                    st[("gTpe", tt)] = cPE.n

                def dv_gT(v, tt=tt, s2=s2):
                    v.wait_ge(cPE.sem, st[("gTpe", tt)])
                    cDV.inc(v.tensor_copy(
                        gatesT[0:E, tt * 128:(tt + 1) * 128], gtp[:, s2, :]))
                    cDV.inc(v.tensor_scalar_add(
                        self.k2T[0:E, tt * 128:(tt + 1) * 128],
                        gtp[:, 2 + s2, :], 1.0))
                    st[("gTev", tt)] = cDV.n

                self.on("tensor", pe_gl)
                self.on("vector", dv_g1)
                self.on("scalar", ac_g)
                self.on("vector", dv_g2)
                self.on("tensor", pe_gT)
                self.on("vector", dv_gT)

            self.routing(pe_at_g, dv_at_g, ac_at_g)

    def routing(self, pe_at_g, dv_at_g, ac_at_g):
        nc, st = self.nc, self.st
        TOK, E, NTT, CAP, CT = self.TOK, self.E, self.NTT, self.CAP, self.CT
        cPE, cDV, cGP = self.cPE, self.cDV, self.cGP
        gatesT, k2T, IxI, IdSI, idf = (self.gatesT, self.k2T, self.IxI,
                                       self.IdSI, self.idf)

        with nc.sbuf_tensor("iot16", [16, TOK], F16) as iot16, \
             nc.sbuf_tensor("mask16", [16, TOK], F32) as mask16, \
             nc.sbuf_tensor("zsc", [16, TOK], F32) as zsc, \
             nc.sbuf_tensor("pref", [16, TOK], F32) as pref, \
             nc.sbuf_tensor("slots", [16, TOK], F32) as slots, \
             nc.sbuf_tensor("slotsi", [16, TOK], I16) as slotsi, \
             nc.sbuf_tensor("il", [16, CAP], F16) as il, \
             nc.sbuf_tensor("kl", [16, CAP], F16) as kl, \
             nc.sbuf_tensor("idh", [16, 16], F16) as idh, \
             nc.sbuf_tensor("IxKf", [128, 2 * CT, 16], F32) as IxKf, \
             nc.sbuf_tensor("rowf", [128, CT, 16], F32) as rowf, \
             nc.psum_tensor("itp", [128, 2 * CT, 16], F16) as itp:

            def dv_rt(v):
                v.wait_ge(cPE.sem, pe_at_g)
                v.wait_ge(self.cAC.sem, ac_at_g)
                cDV.inc(v.memset(zsc[:, :], 0.0))
                cDV.inc(v.tensor_copy(idh[:, :], idf[0:16, 0:16]))
                v.wait_ge(cDV.sem, st[("gTev", NTT - 1)])
                v.wait_ge(cDV.sem, st["gz"])
                cDV.inc(v.tensor_single_scalar(mask16[:, :], gatesT[:, :],
                                               0.0, ALU.is_gt))
                v.wait_ge(cDV.sem, cDV.n)
                cDV.inc(v.tensor_tensor_scan(pref[:, :], mask16[:, :],
                                             zsc[:, :], 0.0,
                                             op0=ALU.add, op1=ALU.add))
                v.wait_ge(cDV.sem, cDV.n)
                cDV.inc(v.tensor_mul(slots[:, :], mask16[:, :], pref[:, :]))
                v.wait_ge(cDV.sem, cDV.n)
                cDV.inc(v.tensor_scalar_add(slots[:, :], slots[:, :], -1.0))
                v.wait_ge(cDV.sem, cDV.n)
                cDV.inc(v.tensor_single_scalar(mask16[:, :], slots[:, :],
                                               float(CAP), ALU.is_ge))
                cDV.inc(v.tensor_scalar_add(zsc[:, :], slots[:, :], 1.0))
                v.wait_ge(cDV.sem, cDV.n)
                cDV.inc(v.tensor_mul(zsc[:, :], mask16[:, :], zsc[:, :]))
                v.wait_ge(cDV.sem, cDV.n)
                cDV.inc(v.tensor_sub(slots[:, :], slots[:, :], zsc[:, :]))
                v.wait_ge(cDV.sem, cDV.n)
                cDV.inc(v.tensor_copy(slotsi[:, :], slots[:, :]))
                st["sloti"] = cDV.n

            def gp_rt(gp):
                gp.wait_ge(cPE.sem, pe_at_g)
                gp.wait_ge(cDV.sem, dv_at_g)
                gp.wait_ge(self.cAC.sem, ac_at_g)
                self.g_iot.inc(gp.dma_start(iot16[:, :], self.iot16_d[:, :]))
                gp.wait_ge(cDV.sem, st["sloti"])
                self.g_iot.wait(gp, (0, self.g_iot.n[0]))
                if "nolsc" in self.dbg:
                    self.g_iot.inc(gp.dma_start(il[:, :],
                                                self.iot16_d[:, 0:CAP]))
                    self.g_iot.inc(gp.dma_start(kl[:, :],
                                                self.iot16_d[:, 0:CAP]))
                else:
                    gp.load_library(LIB_LSC)
                    cGP.inc(gp.local_scatter(il[:, :], iot16[:, :],
                                             slotsi[:, :], channels=16,
                                             num_elems=CAP, num_idxs=TOK))
                    cGP.inc(gp.local_scatter(kl[:, :], k2T[:, :],
                                             slotsi[:, :], channels=16,
                                             num_elems=CAP, num_idxs=TOK))
                st["lsc"] = cGP.n

            def pe_rt(t):
                if "nolsc" in self.dbg:
                    self.g_iot.wait_all(t)
                else:
                    t.wait_ge(cGP.sem, st["lsc"])
                t.wait_ge(self.cAC.sem, ac_at_g)
                last = None
                for j in range(2 * CT):
                    srcb = il if j < CT else kl
                    ct = j % CT
                    last = t.transpose(itp[:, j, :],
                                       srcb[:, ct * 128:(ct + 1) * 128],
                                       idh[:, :])
                cPE.inc(last)
                st["rtp"] = cPE.n

            def dv_rt2(v):
                v.wait_ge(cPE.sem, st["rtp"])
                cDV.inc(v.tensor_copy(IxKf[:, :, :], itp[:, :, :]))
                v.wait_ge(cDV.sem, cDV.n)
                cDV.inc(v.tensor_copy(IxI[:, :, :], IxKf[:, 0:CT, :]))
                cDV.inc(v.scalar_tensor_tensor(rowf[:, :, :],
                                               IxKf[:, CT:2 * CT, :],
                                               float(TOK),
                                               IxKf[:, 0:CT, :],
                                               op0=ALU.mult, op1=ALU.add))
                v.wait_ge(cDV.sem, cDV.n)
                cDV.inc(v.tensor_scalar_add(rowf[:, :, :], rowf[:, :, :],
                                            float(128 - TOK)))
                v.wait_ge(cDV.sem, cDV.n)
                cDV.inc(v.tensor_single_scalar(rowf[:, :, :], rowf[:, :, :],
                                               0.0, ALU.max))
                v.wait_ge(cDV.sem, cDV.n)
                cDV.inc(v.tensor_copy(IdSI[:, :, :], rowf[:, :, :]))
                st["ridx"] = cDV.n

            self.on("vector", dv_rt)
            self.on("gpsimd", gp_rt)
            self.on("tensor", pe_rt)
            self.on("vector", dv_rt2)

    def ffn(self):
        nc, st = self.nc, self.st
        D, DT, E, NTT, INTER, IC, ITL = (self.D, self.DT, self.E, self.NTT,
                                         self.INTER, self.IC, self.ITL)
        TOK, CAP, CT = self.TOK, self.CAP, self.CT
        cPE, cDV, cAC, cGP = self.cPE, self.cDV, self.cAC, self.cGP
        b1sb, b2sb, gatesT, wk = self.b1sb, self.b2sb, self.gatesT, self.wk
        w1sl, w2sl, y_tm = self.w1sl, self.w2sl, self.y_tm
        IxI, IdSI, idf = self.IxI, self.IdSI, self.idf
        NG = IC * 4 * 2
        NH = 4
        WQ = D // NH

        pe_at_ffn = cPE.n
        ac_at_ffn = cAC.n
        dv_at_ffn = cDV.n
        gp_at_ffn = cGP.n

        def sy_yinit(sy):
            self.s_x1.wait_all(sy)
            sy.wait_ge(cPE.sem, pe_at_ffn)
            sy.wait_ge(cDV.sem, dv_at_ffn)
            sy.wait_ge(cAC.sem, ac_at_ffn)
            sy.wait_ge(cGP.sem, gp_at_ffn)
            for tt in range(NTT):
                self.s_yin.inc(sy.dma_start(
                    y_tm[:, tt, :],
                    self.x1_scr[tt * 128:(tt + 1) * 128, :]))

        self.on("sync", sy_yinit)

        if "noffn" in self.dbg:
            def dv_fake(v):
                self.s_yin.wait_all(v)
                for tt in range(NTT):
                    st[("b2evt", tt)] = cDV.n
            self.on("vector", dv_fake)
            return

        with nc.sbuf_tensor("hbuf", [128, ITL, CAP], BF16) as hbuf, \
             nc.sbuf_tensor("Xgf", [128, DT, CAP], BF16) as Xgf, \
             nc.sbuf_tensor("Xgt", [128, CT, D], F32) as Xgt, \
             nc.psum_tensor("mpT", [128, 2, 4, 128], F32) as mpT, \
             nc.psum_tensor("mpA", [128, 2, 512], F32) as mpA, \
             nc.psum_tensor("mp2", [128, CT, WQ], F32) as mp2:

            NE = E
            for f in self.dbg:
                if f.startswith("ne"):
                    NE = int(f[2:])
            self.NE = NE
            with nc.sbuf_tensor("og", [128, CT, D], BF16) as og:
                for ei in range(NE):
                    self.expert(ei, hbuf, Xgf, Xgt, og, mpT, mpA, mp2,
                                pe_at_ffn, ac_at_ffn, dv_at_ffn)

                if "nomm2" not in self.dbg:
                    def gp_sc_last(gp):
                        self.emit_scatters(gp, NE - 1, og)

                    self.on("gpsimd", gp_sc_last)

            if "nofin" in self.dbg:
                def dv_fake2(v):
                    self.s_yin.wait_all(v)
                    self.g_sc.wait_all(v)
                    for tt in range(NTT):
                        st[("b2evt", tt)] = cDV.n
                self.on("vector", dv_fake2)
            else:
                with nc.sbuf_tensor("yl", [128, 4, D], BF16) as yl:
                    self.finish_y(yl, mpA)

    def expert(self, ei, hbuf, Xgf, Xgt, og, mpT, mpA, mp2,
               pe_at_ffn, ac_at_ffn, dv_at_ffn):
        nc, st = self.nc, self.st
        D, DT, E, IC, CAP, CT = (self.D, self.DT, self.E, self.IC, self.CAP,
                                 self.CT)
        cPE, cDV, cAC = self.cPE, self.cDV, self.cAC
        b1sb, hbuf_, IxI, idf = self.b1sb, hbuf, self.IxI, self.idf
        NG = IC * 4 * 2
        NH, WQ = 4, D // 4
        widths = (512, CAP - 512)

        if "noxT" in self.dbg:
            def dv_fakexT(v, ei=ei):
                for j in range(2 * CT):
                    st[("xgT", ei, j)] = cPE.n
                    st[("xgev", ei, j)] = cDV.n
                st[("mm1end", ei - 1)] = st.get(("mm1end", ei - 1), cPE.n)
            self.on("vector", dv_fakexT)

        if ei == 0 and "noxT" not in self.dbg:
            def gp_g0(gp):
                gp.wait_ge(cDV.sem, st["ridx"])
                gp.wait_ge(cPE.sem, pe_at_ffn)
                gp.wait_ge(cAC.sem, ac_at_ffn)
                self.s_x1.wait_all(gp)
                for ct in range(CT):
                    if "noind" in self.dbg:
                        dma = gp.dma_start(Xgt[:, ct, :],
                                           self.x1_scr[ct * 128:(ct + 1) * 128, :])
                    else:
                        dma = gp.indirect_dma_start(
                            Xgt[:, ct, :], None, self.x1_scr[:, :],
                            bass.IndirectOffsetOnAxis(IxI[:, ct, 0:1], 0))
                    st[("xg", 0, ct)] = self.g_xg.inc(dma, ct)

            self.on("gpsimd", gp_g0)

        for j in range(2 * CT if "noxT" not in self.dbg else 0):
            ct, hf = j // 2, j % 2

            def pe_xT(t, ei=ei, j=j, ct=ct, hf=hf):
                if j == 0:
                    self.g_xg.wait(t, st[("xg", ei, 0)])
                    if ei == 0:
                        t.wait_ge(cDV.sem, dv_at_ffn)
                        t.wait_ge(cAC.sem, ac_at_ffn)
                elif hf == 0:
                    self.g_xg.wait(t, st[("xg", ei, ct)])
                if j >= 2:
                    t.wait_ge(cDV.sem, st[("xgev", ei, j - 2)])
                elif ei >= 1:
                    t.wait_ge(cDV.sem, st[("xgev", ei - 1, 2 * CT - 2 + j)])
                last = None
                for m in range(4):
                    last = t.transpose(
                        mpT[:, j % 2, m, :],
                        Xgt[:, ct, (hf * 4 + m) * 128:(hf * 4 + m + 1) * 128],
                        idf[:, :])
                cPE.inc(last)
                st[("xgT", ei, j)] = cPE.n

            def dv_xT(v, ei=ei, j=j, ct=ct, hf=hf):
                v.wait_ge(cPE.sem, st[("xgT", ei, j)])
                if ei >= 1 and j == 0:
                    v.wait_ge(cPE.sem, st[("mm1end", ei - 1)])
                cDV.inc(v.tensor_copy(
                    Xgf[:, hf * 4:(hf + 1) * 4, ct * 128:(ct + 1) * 128],
                    mpT[:, j % 2, :, :]))
                st[("xgev", ei, j)] = cDV.n

            self.on("tensor", pe_xT)
            self.on("vector", dv_xT)

        if "nomm1" in self.dbg:
            def dv_fakemm1(v, ei=ei):
                for g in range(NG):
                    st[("gelu", ei, g)] = cAC.n
                for c in range(IC):
                    st[("mm1cd", ei * IC + c)] = cPE.n
                st[("mm1end", ei)] = cPE.n
            self.on("vector", dv_fakemm1)

        for c in range(IC if "nomm1" not in self.dbg else 0):
            idx = ei * IC + c

            def gp_w1(gp, ei=ei, c=c, idx=idx):
                if idx >= 2:
                    gp.wait_ge(cPE.sem, st[("mm1cd", idx - 2)])
                st[("w1ld", idx)] = self.g_w1.inc(gp.dma_start(
                    self.w1sl[:, idx % 2, :, :],
                    bass.AP(self.w1, ei * D * self.INTER + c * 512,
                            [[self.INTER, 128], [128 * self.INTER, DT],
                             [1, 512]])), idx % 2)

            self.on("gpsimd", gp_w1)
            for it in range(4):
                def pe_mm1(t, ei=ei, c=c, idx=idx, it=it):
                    if it == 0:
                        self.g_w1.wait(t, st[("w1ld", idx)])
                    for tch in range(2):
                        g = (c * 4 + it) * 2 + tch
                        slot = g % 2
                        w = widths[tch]
                        if g >= 2:
                            t.wait_ge(cAC.sem, st[("gelu", ei, g - 2)])
                        elif ei >= 1:
                            t.wait_ge(cAC.sem,
                                      st[("gelu", ei - 1, NG - 2 + g)])
                        else:
                            t.wait_ge(cDV.sem, dv_at_ffn)
                        if g == 0:
                            t.wait_ge(cDV.sem, st[("xgev", ei, 2 * CT - 1)])
                        last = None
                        for k in range(DT):
                            last = t.matmul(
                                mpA[:, slot, 0:w],
                                self.w1sl[:, idx % 2, k,
                                          it * 128:(it + 1) * 128],
                                Xgf[:, k, tch * 512:tch * 512 + w],
                                start=(k == 0), stop=(k == DT - 1))
                        cPE.inc(last)
                        st[("mm1pe", ei, g)] = cPE.n
                    if it == 3:
                        st[("mm1cd", idx)] = cPE.n
                        if c == IC - 1:
                            st[("mm1end", ei)] = cPE.n

                def ac_gelu(s, ei=ei, c=c, it=it):
                    for tch in range(2):
                        g = (c * 4 + it) * 2 + tch
                        slot = g % 2
                        w = widths[tch]
                        s.wait_ge(cPE.sem, st[("mm1pe", ei, g)])
                        if g == 0 and ei == 0:
                            self.g_bias.wait_all(s)
                            s.wait_ge(cDV.sem, dv_at_ffn)
                        cAC.inc(s.activation(
                            hbuf[:, c * 4 + it, tch * 512:tch * 512 + w],
                            mpA[:, slot, 0:w], AF.Gelu,
                            bias=b1sb[:, ei, c * 4 + it:c * 4 + it + 1]))
                        st[("gelu", ei, g)] = cAC.n

                self.on("tensor", pe_mm1)
                self.on("scalar", ac_gelu)

        if ei + 1 < getattr(self, "NE", E):
            def gp_gn(gp, ei=ei):
                for ct in range(CT):
                    gp.wait_ge(cPE.sem, st[("xgT", ei, 2 * ct + 1)])
                    if "noind" in self.dbg:
                        dma = gp.dma_start(Xgt[:, ct, :],
                                           self.x1_scr[ct * 128:(ct + 1) * 128, :])
                    else:
                        dma = gp.indirect_dma_start(
                            Xgt[:, ct, :], None, self.x1_scr[:, :],
                            bass.IndirectOffsetOnAxis(IxI[:, ct, ei + 1:ei + 2],
                                                      0))
                    st[("xg", ei + 1, ct)] = self.g_xg.inc(dma, ct)

            self.on("gpsimd", gp_gn)
        if ei >= 1 and "nomm2" not in self.dbg:
            def gp_sc(gp, ei=ei):
                self.emit_scatters(gp, ei - 1, og)

            self.on("gpsimd", gp_sc)

        if "nomm2" in self.dbg:
            def dv_fakemm2(v, ei=ei):
                for Hh in range(NH):
                    for tt in range(CT):
                        st[("mm2pe", ei, Hh, tt)] = cPE.n
                        st[("ogev", ei, Hh, tt)] = cDV.n
                for Hh in range(NH):
                    for c in range(IC):
                        st[("mm2cd", (ei * NH + Hh) * IC + c)] = cPE.n
                if ei >= 1:
                    for tt in range(CT):
                        st[("scat", ei - 1, tt)] = (0, self.g_sc.n[0])
            self.on("vector", dv_fakemm2)

        for Hh in range(NH if "nomm2" not in self.dbg else 0):
            for c in range(IC):
                idx2 = (ei * NH + Hh) * IC + c

                def gp_w2(gp, ei=ei, Hh=Hh, c=c, idx2=idx2):
                    if idx2 >= 2:
                        gp.wait_ge(cPE.sem, st[("mm2cd", idx2 - 2)])
                    st[("w2ld", idx2)] = self.g_w2.inc(gp.dma_start(
                        self.w2sl[:, idx2 % 2, :, :],
                        bass.AP(self.w2,
                                ei * self.INTER * D + c * 512 * D + Hh * WQ,
                                [[D, 128], [128 * D, 4], [1, WQ]])),
                        idx2 % 2)

                def pe_mm2(t, ei=ei, Hh=Hh, c=c, idx2=idx2):
                    self.g_w2.wait(t, st[("w2ld", idx2)])
                    last = None
                    for tt in range(CT):
                        if c == 0:
                            if Hh == 0 and ei == 0:
                                t.wait_ge(cDV.sem, dv_at_ffn)
                                t.wait_ge(cAC.sem, ac_at_ffn)
                            elif Hh == 0:
                                t.wait_ge(cDV.sem,
                                          st[("ogev", ei - 1, NH - 1, tt | 1)])
                            else:
                                t.wait_ge(cDV.sem,
                                          st[("ogev", ei, Hh - 1, tt | 1)])
                        for k in range(4):
                            if tt == 0:
                                t.wait_ge(cAC.sem,
                                          st[("gelu", ei, (c * 4 + k) * 2 + 1)])
                            last = t.matmul(
                                mp2[:, tt, :],
                                hbuf[:, c * 4 + k, tt * 128:(tt + 1) * 128],
                                self.w2sl[:, idx2 % 2, k, :],
                                start=(c == 0 and k == 0 and tt % 2 == 0),
                                stop=(c == IC - 1 and k == 3))
                        if c == IC - 1:
                            cPE.inc(last)
                            st[("mm2pe", ei, Hh, tt)] = cPE.n
                    if c != IC - 1:
                        cPE.inc(last)
                    st[("mm2cd", idx2)] = cPE.n

                self.on("gpsimd", gp_w2)
                self.on("tensor", pe_mm2)

            def dv_og(v, ei=ei, Hh=Hh):
                for tt in range(CT):
                    v.wait_ge(cPE.sem, st[("mm2pe", ei, Hh, tt | 1)])
                    if Hh == 0 and ei >= 1:
                        self.g_sc.wait(v, st[("scat", ei - 1, tt)])
                    cDV.inc(v.tensor_copy(og[:, tt, Hh * WQ:(Hh + 1) * WQ],
                                          mp2[:, tt, :]))
                    st[("ogev", ei, Hh, tt)] = cDV.n

            self.on("vector", dv_og)

    def emit_scatters(self, gp, ei, og):
        st, CT = self.st, self.CT
        if ei == 0:
            self.s_yz.wait_all(gp)
        for tt in range(CT):
            gp.wait_ge(self.cDV.sem, st[("ogev", ei, 3, tt)])
            if self.g_sc.n[0]:
                gp.wait_ge(self.g_sc.sems[0], self.g_sc.n[0])
            if "noind" in self.dbg:
                r0 = 128 + (((ei * CT + tt) % 32) * 128)
                dma = gp.dma_start(self.y_scr[r0:r0 + 128, :], og[:, tt, :])
            else:
                dma = gp.indirect_dma_start(
                    self.y_scr[:, :],
                    bass.IndirectOffsetOnAxis(self.IdSI[:, tt, ei:ei + 1], 0),
                    og[:, tt, :], None)
            st[("scat", ei, tt)] = self.g_sc.inc(dma)

    def finish_y(self, yl, mpA):
        nc, st = self.nc, self.st
        D, E, NTT, TOK = self.D, self.E, self.NTT, self.TOK
        cPE, cDV, cAC = self.cPE, self.cDV, self.cAC
        y_tm, wk, gatesT, b2sb = self.y_tm, self.wk, self.gatesT, self.b2sb
        NG = self.IC * 4 * 2

        for tt in range(NTT):
            for k2 in range(2):
                sl = (tt * 2 + k2) % 4

                def sy_yl(sy, tt=tt, k2=k2, sl=sl):
                    if tt == 0 and k2 == 0:
                        self.g_sc.wait_all(sy)
                    if tt >= 2:
                        sy.wait_ge(cDV.sem, st[("yadd", tt - 2, k2)])
                    r0 = 128 + k2 * TOK + tt * 128
                    st[("yld", tt, k2)] = self.s_yl.inc(
                        sy.dma_start(yl[:, sl, :],
                                     self.y_scr[r0:r0 + 128, :]), sl)

                def dv_yadd(v, tt=tt, k2=k2, sl=sl):
                    self.s_yl.wait(v, st[("yld", tt, k2)])
                    if tt == 0 and k2 == 0:
                        self.s_yin.wait_all(v)
                    v.wait_ge(cDV.sem, cDV.n)
                    cDV.inc(v.scalar_tensor_tensor(
                        y_tm[:, tt, :], yl[:, sl, :], wk[:, tt, k2:k2 + 1],
                        y_tm[:, tt, :], op0=ALU.mult, op1=ALU.add))
                    st[("yadd", tt, k2)] = cDV.n

                self.on("sync", sy_yl)
                self.on("vector", dv_yadd)

        for tt in range(NTT):
            for ch in range(2):
                fi = tt * 2 + ch
                slot = fi % 2

                def pe_b2(t, tt=tt, ch=ch, fi=fi, slot=slot):
                    if fi == 0:
                        t.wait_ge(cDV.sem, st[("gTev", NTT - 1)])
                        self.g_bias.wait_all(t)
                    if fi >= 2:
                        t.wait_ge(cDV.sem, st[("b2ev", fi - 2)])
                    else:
                        t.wait_ge(cAC.sem, st[("gelu", E - 1, NG - 2 + slot)])
                    cPE.inc(t.matmul(
                        mpA[:, slot, :],
                        gatesT[0:E, tt * 128:(tt + 1) * 128],
                        b2sb[:, ch * 512:(ch + 1) * 512],
                        start=True, stop=True))
                    st[("b2pe", fi)] = cPE.n

                def dv_b2(v, tt=tt, ch=ch, fi=fi, slot=slot):
                    v.wait_ge(cPE.sem, st[("b2pe", fi)])
                    v.wait_ge(cDV.sem, st[("yadd", tt, 1)])
                    cDV.inc(v.tensor_add(
                        y_tm[:, tt, ch * 512:(ch + 1) * 512],
                        y_tm[:, tt, ch * 512:(ch + 1) * 512],
                        mpA[:, slot, :]))
                    st[("b2ev", fi)] = cDV.n
                    if ch == 1:
                        st[("b2evt", tt)] = cDV.n

                self.on("tensor", pe_b2)
                self.on("vector", dv_b2)

    def ln2(self):
        nc, st = self.nc, self.st
        D, NTT = self.D, self.NTT
        cDV, cAC = self.cDV, self.cAC
        y_tm, outst, lnt2, sqj2 = self.y_tm, self.outst, self.lnt2, self.sqj2

        for tt in range(NTT):
            s2 = tt % 2

            def dv_a(v, tt=tt, s2=s2):
                v.wait_ge(cDV.sem, st[("b2evt", tt)])
                cDV.inc(v.reduce_sum(lnt2[:, s2 * 8 + 0: s2 * 8 + 1],
                                     y_tm[:, tt, :], axis=AX.X))
                st[("l2sum", tt)] = cDV.n

            def ac_a(s, tt=tt, s2=s2):
                s.wait_ge(cDV.sem, st[("l2sum", tt)])
                if tt >= 2:
                    s.wait_ge(cAC.sem, st[("l2done", tt - 2)])
                cAC.inc(s.activation(sqj2[:, :], y_tm[:, tt, :],
                                     AF.Square,
                                     accum_out=lnt2[:, s2 * 8 + 1: s2 * 8 + 2]))
                st[("l2sq", tt)] = cAC.n

            def dv_b(v, tt=tt, s2=s2):
                o = s2 * 8
                v.wait_ge(cAC.sem, st[("l2sq", tt)])
                cDV.inc(v.tensor_scalar_mul(lnt2[:, o + 2:o + 3],
                                            lnt2[:, o + 0:o + 1], 1.0 / D))
                cDV.inc(v.tensor_scalar_mul(lnt2[:, o + 3:o + 4],
                                            lnt2[:, o + 1:o + 2], 1.0 / D))
                v.wait_ge(cDV.sem, cDV.n)
                cDV.inc(v.tensor_mul(lnt2[:, o + 4:o + 5], lnt2[:, o + 2:o + 3],
                                     lnt2[:, o + 2:o + 3]))
                v.wait_ge(cDV.sem, cDV.n)
                cDV.inc(v.tensor_sub(lnt2[:, o + 5:o + 6], lnt2[:, o + 3:o + 4],
                                     lnt2[:, o + 4:o + 5]))
                v.wait_ge(cDV.sem, cDV.n)
                cDV.inc(v.tensor_scalar_add(lnt2[:, o + 6:o + 7],
                                            lnt2[:, o + 5:o + 6], 1e-5))
                st[("l2var", tt)] = cDV.n

            def ac_b(s, tt=tt, s2=s2):
                s.wait_ge(cDV.sem, st[("l2var", tt)])
                cAC.inc(s.activation(lnt2[:, s2 * 8 + 7: s2 * 8 + 8],
                                     lnt2[:, s2 * 8 + 6: s2 * 8 + 7], AF.Sqrt))
                st[("l2std", tt)] = cAC.n

            def dv_c(v, tt=tt, s2=s2):
                o = s2 * 8
                v.wait_ge(cAC.sem, st[("l2std", tt)])
                cDV.inc(v.reciprocal(lnt2[:, o + 4:o + 5], lnt2[:, o + 7:o + 8]))
                v.wait_ge(cDV.sem, cDV.n)
                cDV.inc(v.tensor_mul(lnt2[:, o + 5:o + 6], lnt2[:, o + 2:o + 3],
                                     lnt2[:, o + 4:o + 5]))
                v.wait_ge(cDV.sem, cDV.n)
                cDV.inc(v.tensor_scalar_mul(lnt2[:, o + 6:o + 7],
                                            lnt2[:, o + 5:o + 6], -1.0))
                st[("l2ab", tt)] = cDV.n

            def ac_c(s, tt=tt, s2=s2):
                s.wait_ge(cDV.sem, st[("l2ab", tt)])
                if tt >= 2:
                    self.s_out.wait(s, st[("outst", tt - 2)])
                cAC.inc(s.activation(outst[:, s2, :],
                                     y_tm[:, tt, :], AF.Identity,
                                     bias=lnt2[:, s2 * 8 + 6: s2 * 8 + 7],
                                     scale=lnt2[:, s2 * 8 + 4: s2 * 8 + 5]))
                st[("l2done", tt)] = cAC.n

            def sy_out(sy, tt=tt, s2=s2):
                sy.wait_ge(cAC.sem, st[("l2done", tt)])
                st[("outst", tt)] = self.s_out.inc(
                    sy.dma_start(self.yout[tt * 128:(tt + 1) * 128, :],
                                 outst[:, s2, :]), s2)

            self.on("vector", dv_a)
            self.on("scalar", ac_a)
            self.on("vector", dv_b)
            self.on("scalar", ac_b)
            self.on("vector", dv_c)
            self.on("scalar", ac_c)
            self.on("sync", sy_out)

        def sy_fin(sy):
            self.s_out.wait_all(sy)

        self.on("sync", sy_fin)


_CACHE = {}


def _masks(TOK, BS=64, WIN=512):
    WINW = WIN + BS
    HALF = TOK // 2
    HQB = HALF // BS
    POFF = HQB // 2
    NP = POFF
    WB = WIN // BS
    out = []
    qi = np.arange(BS)[:, None]
    ki = np.arange(WINW)[None, :]
    wi = ki // BS
    kwi = ki % BS
    for parity in range(2):
        m = np.zeros((2 * NP, 128, WINW), np.float32)
        for ha in range(2):
            for r in range(NP):
                for half_row, qb in ((0, r), (1, r + POFF)):
                    g = ha * HQB + qb
                    g_seq = g + parity * (TOK // BS)
                    qpos = g_seq * BS + qi
                    kpos = (g_seq + wi - WB) * BS + kwi
                    valid = ((g_seq + wi - WB) >= 0) & (kpos <= qpos) & \
                            (qpos - kpos < WIN)
                    m[ha * NP + r, half_row * 64:(half_row + 1) * 64, :] = \
                        np.where(valid, 0.0, NEG).astype(np.float32)
        out.append(m.astype(ml_dtypes.bfloat16))
    return out


def kernel(L0_attn_x, L0_attn_Wq, L0_attn_Wk, L0_attn_Wv, L0_attn_Wo,
           L0_ffn_Wg, L0_ffn_W1, L0_ffn_b1, L0_ffn_W2, L0_ffn_b2):
    TOK, D, E, INTER = 2048, 1024, 8, 4096
    x = np.ascontiguousarray(np.asarray(L0_attn_x, dtype=np.float32)).reshape(-1, D)
    NCORE = 8
    if "nc" not in _CACHE:
        _CACHE["nc"] = Builder(TOK=TOK, D=D, E=E, INTER=INTER).build()
        _CACHE["masks"] = _masks(TOK)
    nc = _CACHE["nc"]
    mev, mod = _CACHE["masks"]

    idenb = np.eye(128, dtype=ml_dtypes.bfloat16)
    idenf = np.eye(128, dtype=np.float32)
    ioteps = np.broadcast_to((np.arange(E) * -1e-6).astype(np.float32),
                             (128, E)).copy()
    iota16 = np.broadcast_to(np.arange(TOK, dtype=np.float16),
                             (16, TOK)).copy()
    common = dict(
        wq=np.ascontiguousarray(L0_attn_Wq, np.float32),
        wk=np.ascontiguousarray(L0_attn_Wk, np.float32),
        wv=np.ascontiguousarray(L0_attn_Wv, np.float32),
        wo=np.ascontiguousarray(L0_attn_Wo, np.float32),
        wg=np.ascontiguousarray(L0_ffn_Wg, np.float32),
        w1=np.ascontiguousarray(L0_ffn_W1, np.float32),
        b1=np.ascontiguousarray(
            np.asarray(L0_ffn_b1, np.float32).reshape(E, INTER // 128, 128)
            .transpose(2, 0, 1)),
        w2=np.ascontiguousarray(L0_ffn_W2, np.float32),
        b2=np.ascontiguousarray(L0_ffn_b2, np.float32),
        idenb=idenb, idenf=idenf, ioteps=ioteps, iota16=iota16,
    )
    in_maps = []
    for c in range(NCORE):
        xm = x[c * TOK:(c + 1) * TOK]
        xh = (np.zeros((512, D), np.float32) if c % 2 == 0
              else x[c * TOK - 512: c * TOK])
        in_maps.append(dict(common, x_main=np.ascontiguousarray(xm),
                            x_halo=np.ascontiguousarray(xh),
                            mask=(mev if c % 2 == 0 else mod)))
    res = run_bass_kernel_spmd(nc, in_maps, core_ids=list(range(NCORE)))
    out = np.concatenate([r["yout"] for r in res.results], axis=0)
    return out.reshape(4, 4096, 1024).astype(np.float32)


# version 10
# speedup vs baseline: 1.3761x; 1.0397x over previous
import numpy as np
import ml_dtypes
from contextlib import ExitStack

import concourse.bass as bass
import concourse.mybir as mybir
from concourse.bass_utils import run_bass_kernel_spmd
from concourse.library_config import local_scatter as LIB_LSC

F32 = mybir.dt.float32
BF16 = mybir.dt.bfloat16
F16 = mybir.dt.float16
I16 = mybir.dt.int16
I32 = mybir.dt.int32
AF = mybir.ActivationFunctionType
ALU = mybir.AluOpType
AX = mybir.AxisListType

NEG = -1.0e9


class Ctr:

    def __init__(self, nc, ctx, name, step=1):
        self.sem = ctx.enter_context(nc.semaphore(name))
        self.step = step
        self.n = 0

    def inc(self, inst):
        inst.then_inc(self.sem, self.step)
        self.n += self.step
        return self.n


class DStream:

    def __init__(self, nc, ctx, name, depth=1):
        self.sems = [ctx.enter_context(nc.semaphore(f"{name}_{i}"))
                     for i in range(depth)]
        self.n = [0] * depth

    def inc(self, inst, slot=0):
        inst.then_inc(self.sems[slot], 16)
        self.n[slot] += 16
        return (slot, self.n[slot])

    def wait(self, eng, tok):
        eng.wait_ge(self.sems[tok[0]], tok[1])

    def wait_all(self, eng):
        for i, s in enumerate(self.sems):
            if self.n[i]:
                eng.wait_ge(s, self.n[i])


class _TE:

    def __init__(self, t):
        self._t = t

    def __getattr__(self, n):
        return getattr(self._t, n)

    def matmul(self, *a, **kw):
        kw.setdefault("skip_group_check", True)
        return self._t.matmul(*a, **kw)

    def transpose(self, out, in_, identity, tile_position=None):
        return self._t.matmul(out, in_, identity, start=True, stop=True,
                              is_transpose=True, skip_group_check=True,
                              tile_position=tile_position)


class Builder:
    def __init__(self, TOK=2048, D=1024, H=16, BS=64, WIN=512, E=8, INTER=4096):
        import os as _os
        self.dbg = set(_os.environ.get("KDBG", "").split(","))
        self.TOK, self.D, self.H, self.BS, self.WIN = TOK, D, H, BS, WIN
        self.E, self.INTER = E, INTER
        self.HALO = WIN
        self.HALF = TOK // 2
        self.HQB = self.HALF // BS
        self.POFF = self.HQB // 2
        assert self.POFF % 2 == 0
        self.NP = self.POFF
        self.KV = self.HALF + self.HALO
        self.KVT = self.KV // 128
        self.NTT = TOK // 128
        self.DT = D // 128
        self.WINW = WIN + BS
        self.IC = INTER // 512
        self.ITL = INTER // 128
        self.NPAIR_T = 2 * self.NP
        self.CAP = 768
        self.CT = self.CAP // 128
        self.st = {}

    def build(self):
        TOK, D, E, INTER = self.TOK, self.D, self.E, self.INTER
        DT, ITL, WINW = self.DT, self.ITL, self.WINW
        nc = self.nc = bass.Bass("TRN2", target_bir_lowering=False)

        self.x_main = nc.dram_tensor("x_main", [TOK, D], F32, kind="ExternalInput")
        self.x_halo = nc.dram_tensor("x_halo", [self.HALO, D], F32, kind="ExternalInput")
        self.w_in = [nc.dram_tensor(n, [D, D], F32, kind="ExternalInput")
                     for n in ("wq", "wk", "wv", "wo")]
        self.wg = nc.dram_tensor("wg", [E, D], F32, kind="ExternalInput")
        self.w1 = nc.dram_tensor("w1", [E, D, INTER], F32, kind="ExternalInput")
        self.b1 = nc.dram_tensor("b1", [128, E, INTER // 128], F32,
                                 kind="ExternalInput")
        self.w2 = nc.dram_tensor("w2", [E, INTER, D], F32, kind="ExternalInput")
        self.b2 = nc.dram_tensor("b2", [E, D], F32, kind="ExternalInput")
        self.mask = nc.dram_tensor("mask", [self.NPAIR_T, 128, WINW], BF16,
                                   kind="ExternalInput")
        self.idenb_d = nc.dram_tensor("idenb", [128, 128], BF16, kind="ExternalInput")
        self.idenf_d = nc.dram_tensor("idenf", [128, 128], F32, kind="ExternalInput")
        self.iot_d = nc.dram_tensor("ioteps", [128, E], F32, kind="ExternalInput")
        self.yout = nc.dram_tensor("yout", [TOK, D], F32, kind="ExternalOutput")
        self.wT_scr = [nc.dram_tensor(f"wT{i}_scr", [DT, 128, D], BF16, kind="Internal")
                       for i in range(4)]
        self.x1_scr = nc.dram_tensor("x1_scr", [TOK, D], F32, kind="Internal")
        self.y_scr = nc.dram_tensor("y_scr", [2 * TOK + 128, D], BF16,
                                    kind="Internal")
        self.iot16_d = nc.dram_tensor("iota16", [16, TOK], F16,
                                      kind="ExternalInput")

        ctx = ExitStack()
        with ctx:
            e = ctx.enter_context
            self.idb = e(nc.sbuf_tensor("idb", [128, 128], BF16))
            self.idf = e(nc.sbuf_tensor("idf", [128, 128], F32))
            self.b1sb = e(nc.sbuf_tensor("b1sb", [128, E, ITL], F32))
            self.b2sb = e(nc.sbuf_tensor("b2sb", [E, D], BF16))
            self.wgT = e(nc.sbuf_tensor("wgT", [128, DT, E], BF16))
            self.iot = e(nc.sbuf_tensor("iot", [128, E], F32))
            self.gates = e(nc.sbuf_tensor("gates", [128, self.NTT, E], F32))
            self.gatesT = e(nc.sbuf_tensor("gatesT", [16, TOK], BF16))
            self.k2T = e(nc.sbuf_tensor("k2T", [16, TOK], F16))
            self.IxI = e(nc.sbuf_tensor("IxI", [128, self.CT, 16], I32))
            self.IdSI = e(nc.sbuf_tensor("IdSI", [128, self.CT, 16], I32))
            self.wk = e(nc.sbuf_tensor("wktk", [128, self.NTT, 2], F32))
            self.zbuf = e(nc.sbuf_tensor("zbuf", [128, 1024], BF16))
            self.w1sl = e(nc.sbuf_tensor("w1sl", [128, 2, DT, 512], BF16))
            self.w2sl = e(nc.sbuf_tensor("w2sl", [128, 2, 4, 256], BF16))

            self.cPE = Ctr(nc, ctx, "cPE")
            self.cDV = Ctr(nc, ctx, "cDV")
            self.cAC = Ctr(nc, ctx, "cAC")
            self.cGP = Ctr(nc, ctx, "cGP")
            self.g_wld = DStream(nc, ctx, "g_wld", 2)
            self.g_wg = DStream(nc, ctx, "g_wg")
            self.g_bias = DStream(nc, ctx, "g_bias")
            self.g_xld = DStream(nc, ctx, "g_xld", 2)
            self.g_wTa = DStream(nc, ctx, "g_wTa")
            self.g_woT = DStream(nc, ctx, "g_woT")
            self.g_w1 = DStream(nc, ctx, "g_w1", 2)
            self.g_w2 = DStream(nc, ctx, "g_w2", 2)
            self.s_iden = DStream(nc, ctx, "s_iden")
            self.s_p0st = DStream(nc, ctx, "s_p0st", 2)
            self.s_mask = DStream(nc, ctx, "s_mask", 2)
            self.s_x0 = DStream(nc, ctx, "s_x0", 1)
            self.s_x1 = DStream(nc, ctx, "s_x1", 2)
            self.s_yin = DStream(nc, ctx, "s_yin")
            self.s_out = DStream(nc, ctx, "s_out", 2)
            self.s_yz = DStream(nc, ctx, "s_yz")
            self.s_yl = DStream(nc, ctx, "s_yl", 4)
            self.g_iot = DStream(nc, ctx, "g_iot")
            self.g_xg = DStream(nc, ctx, "g_xg", 6)
            self.g_sc = DStream(nc, ctx, "g_sc", 1)

            self.block = e(nc.Block())

            self.p0(ctx)
            x1fm_scope = nc.sbuf_tensor("X1fm", [128, DT, TOK], BF16)
            self.X1fm = x1fm_scope.__enter__()
            with nc.sbuf_tensor("Xfm", [128, DT, self.KV], BF16) as Xfm, \
                 nc.sbuf_tensor("Qf", [128, DT, self.HALF], BF16) as Qf, \
                 nc.sbuf_tensor("Kf", [128, DT, self.KV], BF16) as Kf, \
                 nc.sbuf_tensor("Vt", [128, self.KVT, D], BF16) as Vt, \
                 nc.sbuf_tensor("wTa", [128, DT, D], BF16) as wTa, \
                 nc.sbuf_tensor("xld", [128, 2, D], BF16) as xld, \
                 nc.sbuf_tensor("x0r", [128, 1, D], F32) as x0r, \
                 nc.sbuf_tensor("Pbf", [128, 2, WINW], BF16) as Pbf, \
                 nc.sbuf_tensor("PTb", [128, 2, 5, 128], BF16) as PTb, \
                 nc.sbuf_tensor("attnC", [128, 2, DT, 128], BF16) as attnC, \
                 nc.sbuf_tensor("x1c", [128, 2, D], F32) as x1c, \
                 nc.sbuf_tensor("maskb", [128, 2, WINW], BF16) as maskb, \
                 nc.sbuf_tensor("smal", [128, 2, 16], F32) as smal, \
                 nc.sbuf_tensor("sqj", [128, D], BF16) as sqj:
                self.Xfm, self.Qf, self.Kf, self.Vt = Xfm, Qf, Kf, Vt
                self.wTa, self.woT, self.xld, self.x0r = wTa, wTa, xld, x0r
                self.Pbf, self.PTb, self.attnC = Pbf, PTb, attnC
                self.x1c, self.maskb, self.smal, self.sqj = x1c, maskb, smal, sqj
                for ha in range(2):
                    self.half(ha)

            self.gates_phase()
            x1fm_scope.__exit__(None, None, None)
            with nc.sbuf_tensor("y_tm", [128, self.NTT, D], F32) as y_tm:
                self.y_tm = y_tm
                self.ffn()
                with nc.sbuf_tensor("outst", [128, 2, D], F32) as outst, \
                     nc.sbuf_tensor("lnt2", [128, 16], F32) as lnt2, \
                     nc.sbuf_tensor("sqj2", [128, D], BF16) as sqj2:
                    self.outst, self.lnt2, self.sqj2 = outst, lnt2, sqj2
                    self.ln2()
        mybir.codegen_inst_isa_subclasses(nc)
        return nc

    def on(self, eng, fn):
        if eng == "tensor":
            getattr(self.block, eng)(lambda t: fn(_TE(t)))
        else:
            getattr(self.block, eng)(fn)

    def p0(self, ctx):
        nc, st = self.nc, self.st
        DT, D, E, ITL, INTER = self.DT, self.D, self.E, self.ITL, self.INTER
        cPE, cDV = self.cPE, self.cDV

        with nc.sbuf_tensor("wld", [128, 2, D], BF16) as wld, \
             nc.sbuf_tensor("wstg", [128, 2, DT, 128], BF16) as wstg, \
             nc.sbuf_tensor("wgld", [E, D], BF16) as wgld, \
             nc.psum_tensor("wtp", [128, 2, D], BF16) as wtp, \
             nc.psum_tensor("wgp", [128, DT * E], BF16) as wgp:

            def dv_z(v):
                cDV.inc(v.memset(self.zbuf[:, :], 0.0))
                st["zb"] = cDV.n

            self.on("vector", dv_z)

            def sy0(sy):
                self.s_iden.inc(sy.dma_start(self.idb[:, :], self.idenb_d[:, :]))
                self.s_iden.inc(sy.dma_start(self.idf[:, :], self.idenf_d[:, :]))
                self.s_iden.inc(sy.dma_start(self.iot[:, :], self.iot_d[:, :]))
                sy.wait_ge(cDV.sem, st["zb"])
                nzr = (2 * self.TOK + 128) // 128
                for r in range(nzr):
                    self.s_yz.inc(sy.dma_start(
                        self.y_scr[r * 128:(r + 1) * 128, :], self.zbuf[:, :]))

            def gp0a(gp):
                st["wgld"] = self.g_wg.inc(gp.dma_start(wgld[:, :], self.wg[:, :]))
                self.g_bias.inc(gp.dma_start(self.b1sb[:, :, :], self.b1[:, :, :]))
                self.g_bias.inc(gp.dma_start(self.b2sb[:, :], self.b2[:, :]))

            self.on("sync", sy0)
            self.on("gpsimd", gp0a)

            for i in range(4 * DT):
                wi, rt = divmod(i, DT)
                s = i % 2

                def gp0(gp, i=i, wi=wi, rt=rt, s=s):
                    if i >= 2:
                        gp.wait_ge(cPE.sem, st[("p0tp", i - 2)])
                    st[("p0ld", i)] = self.g_wld.inc(gp.dma_start(
                        wld[:, s, :],
                        self.w_in[wi][rt * 128:(rt + 1) * 128, :]), s)

                def pe0(t, i=i, s=s):
                    if i == 0:
                        self.s_iden.wait_all(t)
                    self.g_wld.wait(t, st[("p0ld", i)])
                    if i >= 2:
                        t.wait_ge(cDV.sem, st[("p0ev", i - 2)])
                    last = None
                    for c in range(DT):
                        last = t.transpose(
                            wtp[:, s, c * 128:(c + 1) * 128],
                            wld[:, s, c * 128:(c + 1) * 128],
                            self.idb[:, :])
                    cPE.inc(last)
                    st[("p0tp", i)] = cPE.n

                def dv0(v, i=i, s=s):
                    v.wait_ge(cPE.sem, st[("p0tp", i)])
                    if i >= 2:
                        self.s_p0st.wait(v, st[("p0st", i - 2)])
                    cDV.inc(v.tensor_copy(
                        wstg[:, s, :, :],
                        bass.AP(wtp, s * D, [[2 * D, 128], [128, DT], [1, 128]])))
                    st[("p0ev", i)] = cDV.n

                def sy0b(sy, i=i, wi=wi, rt=rt, s=s):
                    sy.wait_ge(cDV.sem, st[("p0ev", i)])
                    st[("p0st", i)] = self.s_p0st.inc(sy.dma_start(
                        bass.AP(self.wT_scr[wi], rt * 128,
                                [[D, 128], [128 * D, DT], [1, 128]]),
                        wstg[:, s, :, :]), s)

                self.on("gpsimd", gp0)
                self.on("tensor", pe0)
                self.on("vector", dv0)
                self.on("sync", sy0b)

            def pe0g(t):
                self.g_wg.wait(t, st["wgld"])
                last = None
                for c in range(DT):
                    last = t.transpose(wgp[:, c * E:(c + 1) * E],
                                       wgld[:, c * 128:(c + 1) * 128],
                                       self.idb[0:E, 0:E])
                cPE.inc(last)
                st["wgtp"] = cPE.n

            def dv0g(v):
                v.wait_ge(cPE.sem, st["wgtp"])
                cDV.inc(v.tensor_copy(
                    self.wgT[:, :, :],
                    bass.AP(wgp, 0, [[DT * E, 128], [E, DT], [1, E]])))
                st["wgev"] = cDV.n

            self.on("tensor", pe0g)
            self.on("vector", dv0g)

    def half(self, ha):
        nc, st = self.nc, self.st
        D, DT, KV, KVT, HALF, HALO = self.D, self.DT, self.KV, self.KVT, self.HALF, self.HALO
        BS, NP, POFF, WINW, TOK = self.BS, self.NP, self.POFF, self.WINW, self.TOK
        cPE, cDV, cAC = self.cPE, self.cDV, self.cAC
        Xfm, Qf, Kf, Vt, wTa, woT = self.Xfm, self.Qf, self.Kf, self.Vt, self.wTa, self.woT
        xld, x0r = self.xld, self.x0r

        with nc.psum_tensor(f"xtp{ha}", [128, D], BF16) as xtp, \
             nc.psum_tensor(f"prj{ha}", [128, 2, 512], F32) as prj:

            def kv_src(kvt):
                base = ha * HALF - HALO + kvt * 128
                if base < 0:
                    return (self.x_halo, base + HALO)
                return (self.x_main, base)

            for kvt in range(KVT):
                def gp_x(gp, kvt=kvt):
                    if kvt >= 2:
                        gp.wait_ge(cPE.sem, st[("xT", ha, kvt - 2)])
                    elif ha == 1:
                        gp.wait_ge(cPE.sem, st[("xT", 0, KVT - 2 + kvt)])
                    else:
                        gp.wait_ge(cPE.sem, st["wgtp"])
                        self.s_p0st.wait_all(gp)
                    src, r0 = kv_src(kvt)
                    st[("xld", ha, kvt)] = self.g_xld.inc(
                        gp.dma_start(xld[:, kvt % 2, :],
                                     src[r0:r0 + 128, :]), kvt % 2)

                def pe_xT(t, kvt=kvt):
                    if ha == 0 and kvt == 0:
                        t.wait_ge(cDV.sem, st["wgev"])
                    self.g_xld.wait(t, st[("xld", ha, kvt)])
                    if kvt >= 1:
                        t.wait_ge(cDV.sem, st[("xTe", ha, kvt - 1)])
                    elif ha == 1:
                        t.wait_ge(cDV.sem, st[("xTe", 0, KVT - 1)])
                    last = None
                    for c in range(DT):
                        last = t.transpose(xtp[:, c * 128:(c + 1) * 128],
                                           xld[:, kvt % 2, c * 128:(c + 1) * 128],
                                           self.idb[:, :])
                    cPE.inc(last)
                    st[("xT", ha, kvt)] = cPE.n

                def dv_xT(v, kvt=kvt):
                    v.wait_ge(cPE.sem, st[("xT", ha, kvt)])
                    if ha == 1 and kvt == 0:
                        v.wait_ge(cPE.sem, st[("proj_done", 0, 2)])
                    elif ha == 0 and kvt == 0:
                        self.s_p0st.wait_all(v)
                    cDV.inc(v.tensor_copy(
                        Xfm[:, :, kvt * 128:(kvt + 1) * 128],
                        bass.AP(xtp, 0, [[D, 128], [128, DT], [1, 128]])))
                    st[("xTe", ha, kvt)] = cDV.n

                self.on("gpsimd", gp_x)
                self.on("tensor", pe_xT)
                self.on("vector", dv_xT)

            plan = []
            QCH, KCH = HALF // 512, KV // 512
            for dq in range(DT):
                for ch in range(QCH):
                    plan.append((0, "q", dq, ch))
            for dq in range(DT):
                for ch in range(KCH):
                    plan.append((1, "k", dq, ch))
            for kvt in range(KVT):
                for ch in range(2):
                    plan.append((2, "v", kvt, ch))
            NPL = len(plan)

            for i, (wi, kind, a, chn) in enumerate(plan):
                if i == 0 or plan[i - 1][0] != wi:
                    def gp_w(gp, wi=wi):
                        if (ha, wi) == (0, 0):
                            self.s_p0st.wait_all(gp)
                        elif wi == 0:
                            gp.wait_ge(cPE.sem, st[("opj_pe", NP - 1, 1)])
                        else:
                            gp.wait_ge(cPE.sem, st[("proj_done", ha, wi - 1)])
                        st[("wT", ha, wi)] = self.g_wTa.inc(gp.dma_start(
                            wTa[:, :, :],
                            bass.AP(self.wT_scr[wi], 0,
                                    [[D, 128], [128 * D, DT], [1, D]])))

                    self.on("gpsimd", gp_w)

                def pe_proj(t, i=i, wi=wi, kind=kind, a=a, chn=chn):
                    if i == 0 or plan[i - 1][0] != wi:
                        self.g_wTa.wait(t, st[("wT", ha, wi)])
                    slot = i % 2
                    if i >= 2:
                        t.wait_ge(cDV.sem, st[("prj_ev", ha, i - 2)])
                    elif ha == 1:
                        t.wait_ge(cDV.sem, st[("prj_ev", 0, NPL - 2 + i)])
                    last = None
                    if kind == "q":
                        t.wait_ge(cDV.sem,
                                  st[("xTe", ha, HALO // 128 + (chn + 1) * 4 - 1)])
                        for k in range(DT):
                            last = t.matmul(
                                prj[:, slot, :],
                                wTa[:, k, a * 128:(a + 1) * 128],
                                Xfm[:, k, HALO + chn * 512: HALO + (chn + 1) * 512],
                                start=(k == 0), stop=(k == DT - 1))
                    elif kind == "k":
                        t.wait_ge(cDV.sem, st[("xTe", ha, (chn + 1) * 4 - 1)])
                        for k in range(DT):
                            last = t.matmul(
                                prj[:, slot, :],
                                wTa[:, k, a * 128:(a + 1) * 128],
                                Xfm[:, k, chn * 512: (chn + 1) * 512],
                                start=(k == 0), stop=(k == DT - 1))
                    else:
                        t.wait_ge(cDV.sem, st[("xTe", ha, a)])
                        for k in range(DT):
                            last = t.matmul(
                                prj[:, slot, :],
                                Xfm[:, k, a * 128:(a + 1) * 128],
                                wTa[:, k, chn * 512: (chn + 1) * 512],
                                start=(k == 0), stop=(k == DT - 1))
                    cPE.inc(last)
                    st[("prj_pe", ha, i)] = cPE.n
                    if i == NPL - 1 or plan[i + 1][0] != wi:
                        st[("proj_done", ha, wi)] = cPE.n

                def dv_proj(v, i=i, kind=kind, a=a, chn=chn):
                    slot = i % 2
                    v.wait_ge(cPE.sem, st[("prj_pe", ha, i)])
                    if kind == "q":
                        cDV.inc(v.tensor_scalar_mul(
                            Qf[:, a, chn * 512:(chn + 1) * 512],
                            prj[:, slot, :], 0.125))
                    elif kind == "k":
                        cDV.inc(v.tensor_copy(
                            Kf[:, a, chn * 512:(chn + 1) * 512],
                            prj[:, slot, :]))
                    else:
                        cDV.inc(v.tensor_copy(
                            Vt[:, a, chn * 512:(chn + 1) * 512],
                            prj[:, slot, :]))
                    st[("prj_ev", ha, i)] = cDV.n

                self.on("tensor", pe_proj)
                self.on("vector", dv_proj)
            st[("qkv_dv", ha)] = cDV.n

            def gp_wo(gp):
                gp.wait_ge(cPE.sem, st[("proj_done", ha, 2)])
                st[("woT", ha)] = self.g_woT.inc(gp.dma_start(
                    wTa[:, :, :],
                    bass.AP(self.wT_scr[3], 0,
                            [[D, 128], [128 * D, DT], [1, D]])))

            self.on("gpsimd", gp_wo)

        with nc.psum_tensor(f"Sp0{ha}", [128, WINW], F32) as Sp0, \
             nc.psum_tensor(f"Sp1{ha}", [128, WINW], F32) as Sp1, \
             nc.psum_tensor(f"PTp{ha}", [128, 5, 128], BF16) as PTp, \
             nc.psum_tensor(f"pv0{ha}", [128, 512], F32) as pv0, \
             nc.psum_tensor(f"pv1{ha}", [128, 512], F32) as pv1, \
             nc.psum_tensor(f"x1p{ha}", [128, 512], F32) as x1p:
            Sp = [Sp0, Sp1]
            pvp = [pv0, pv1]

            def pe_ptinit(t):
                cPE.inc(t.transpose(PTp[:, 4, :], self.idb[:, :],
                                    self.idb[:, :]))
                st[("ptinit", ha)] = cPE.n

            self.on("tensor", pe_ptinit)
            for p in range(NP):
                self.pair(ha, p, Sp, PTp, pvp, x1p)
            self.tail(ha, NP - 1, pvp, x1p)

        st[("half_done", ha)] = cDV.n

    def pair(self, ha, p, Sp, PTp, pvp, x1p):
        nc, st = self.nc, self.st
        D, DT, HALF, BS, NP, POFF, WINW = (self.D, self.DT, self.HALF, self.BS,
                                           self.NP, self.POFF, self.WINW)
        TOK = self.TOK
        cPE, cDV, cAC = self.cPE, self.cDV, self.cAC
        Qf, Kf, Vt = self.Qf, self.Kf, self.Vt
        Pbf, PTb, attnC = self.Pbf, self.PTb, self.attnC
        maskb, smal = self.maskb, self.smal
        H_ = self.H

        pg = ha * NP + p
        s2 = p % 2
        qA, qB = p, p + POFF
        win0, win1 = qA * BS, qB * BS
        odd = (qA % 2 == 1)
        S_pe, S_ev, exp_ac, nrm_dv, PT_pe, PT_ev, pv_pe = \
            [], [], [], [], [], [], []
        pv_started = set()

        def sy_pair(sy):
            if pg >= 1:
                sy.wait_ge(cDV.sem, st[("x0_free", pg - 1)])
            r0 = ha * HALF + qA * BS
            r1 = ha * HALF + qB * BS
            self.s_x0.inc(sy.dma_start(self.x0r[0:64, 0, :],
                                       self.x_main[r0:r0 + 64, :]))
            st[("x0ld", pg)] = self.s_x0.inc(
                sy.dma_start(self.x0r[64:128, 0, :],
                             self.x_main[r1:r1 + 64, :]))

        def pe_S(t, h):
            hp, hh = divmod(h, 2)
            r0 = hh * 64
            hs = h % 2
            if h == 0 and p == 0:
                t.wait_ge(cDV.sem, st[("qkv_dv", ha)])
                self.g_woT.wait(t, st[("woT", ha)])
            if h == 0:
                self.s_mask.wait(t, st[("mload", pg)])
            if h >= 2:
                t.wait_ge(cAC.sem, exp_ac[h - 2])
            elif pg >= 1:
                t.wait_ge(cAC.sem, st[("exp_last", pg - 1)])
            t.matmul(Sp[hs][0:64, 0:512],
                     Qf[r0:r0 + 64, hp, qA * 64:(qA + 1) * 64],
                     Kf[r0:r0 + 64, hp, win0: win0 + 512],
                     start=True, stop=False, tile_position=(r0, 0))
            t.matmul(Sp[hs][64:128, 0:512],
                     Qf[r0:r0 + 64, hp, qB * 64:(qB + 1) * 64],
                     Kf[r0:r0 + 64, hp, win1: win1 + 512],
                     start=True, stop=False, tile_position=(r0, 64))
            t.matmul(Sp[hs][:, 0:512], self.idb[:, :], maskb[:, s2, 0:512],
                     start=False, stop=True)
            t.matmul(Sp[hs][0:64, 512:WINW],
                     Qf[r0:r0 + 64, hp, qA * 64:(qA + 1) * 64],
                     Kf[r0:r0 + 64, hp, win0 + 512: win0 + 576],
                     start=True, stop=False, tile_position=(r0, 0))
            t.matmul(Sp[hs][64:128, 512:WINW],
                     Qf[r0:r0 + 64, hp, qB * 64:(qB + 1) * 64],
                     Kf[r0:r0 + 64, hp, win1 + 512: win1 + 576],
                     start=True, stop=False, tile_position=(r0, 64))
            last = t.matmul(Sp[hs][:, 512:WINW], self.idb[:, :],
                            maskb[:, s2, 512:WINW],
                            start=False, stop=True)
            cPE.inc(last)
            S_pe.append(cPE.n)
            st[("S_pe", pg, h)] = cPE.n

        def dv_S(v, h):
            hs = h % 2
            v.wait_ge(cPE.sem, S_pe[h])
            if h >= 2:
                v.wait_ge(cAC.sem, exp_ac[h - 2])
            elif pg >= 1:
                v.wait_ge(cAC.sem, st[("exp_last", pg - 1)])
            cDV.inc(v.reduce_max(smal[:, hs, 0:1], Sp[hs][:, :],
                                 axis=AX.X, negate=True))
            S_ev.append(cDV.n)
            st[("S_ev", pg, h)] = cDV.n

        def ac_exp(s, h):
            hs = h % 2
            s.wait_ge(cDV.sem, S_ev[h])
            if h >= 2:
                s.wait_ge(cPE.sem, PT_pe[h - 2])
            elif pg >= 1:
                s.wait_ge(cPE.sem, st[("PT_last", pg - 1)])
            cAC.inc(s.activation(Pbf[:, hs, :], Sp[hs][:, :],
                                 AF.Exp, bias=smal[:, hs, 0:1],
                                 accum_out=smal[:, hs, 1:2]))
            exp_ac.append(cAC.n)

        def dv_nrm(v, h):
            hs = h % 2
            v.wait_ge(cAC.sem, exp_ac[h])
            cDV.inc(v.reciprocal(smal[:, hs, 2:3], smal[:, hs, 1:2]))
            v.wait_ge(cDV.sem, cDV.n)
            cDV.inc(v.tensor_scalar_mul(Pbf[:, hs, :], Pbf[:, hs, :],
                                        smal[:, hs, 2:3]))
            nrm_dv.append(cDV.n)

        def pe_PT(t, h):
            hs = h % 2
            t.wait_ge(cDV.sem, nrm_dv[h])
            if h >= 1:
                t.wait_ge(cDV.sem, PT_ev[h - 1])
            elif pg >= 1:
                t.wait_ge(cDV.sem, st[("PT_ev_last", pg - 1)])
            else:
                t.wait_ge(cPE.sem, st[("ptinit", ha)])
            last = None
            if not odd:
                for cc in range(4):
                    last = t.transpose(PTp[:, cc, :],
                                       Pbf[:, hs, cc * 128:(cc + 1) * 128],
                                       self.idb[:, :])
                last = t.transpose(PTp[0:64, 4, :],
                                   Pbf[:, hs, 512:576], self.idb[:, :])
            else:
                last = t.transpose(PTp[64:128, 0, :],
                                   Pbf[:, hs, 0:64], self.idb[:, :],
                                   tile_position=(0, 64))
                for cc in range(4):
                    last = t.transpose(
                        PTp[:, cc + 1, :],
                        Pbf[:, hs, 64 + cc * 128: 64 + (cc + 1) * 128],
                        self.idb[:, :])
            cPE.inc(last)
            PT_pe.append(cPE.n)
            st[("PT_pe", pg, h)] = cPE.n

        def dv_PT(v, h):
            hs = h % 2
            v.wait_ge(cPE.sem, PT_pe[h])
            if h >= 2:
                v.wait_ge(cPE.sem, pv_pe[h - 2])
            elif pg >= 1:
                v.wait_ge(cPE.sem, st[("pv_last", pg - 1)])
            cDV.inc(v.tensor_copy(PTb[:, hs, :, :], PTp[:, :, :]))
            PT_ev.append(cDV.n)

        def pe_PV(t, h):
            hs = h % 2
            hp, hh = divmod(h, 2)
            bank = pvp[hp // 4]
            fo = (hp % 4) * 128
            t.wait_ge(cDV.sem, PT_ev[h])
            if h == 0 and pg >= 1:
                t.wait_ge(cDV.sem, st[("attn_ev", pg - 1)])
            if h == 8 and pg >= 1:
                t.wait_ge(cDV.sem, st[("res_dv", pg - 1, 1)])
            lastmm = None
            for qi, qb in enumerate((qA, qB)):
                win = qb * BS
                if not odd:
                    chunks = [(win // 128 + cc, 0, 128, cc) for cc in range(4)]
                    chunks.append(((win + 512) // 128, 0, 64, 4))
                else:
                    chunks = [(win // 128, 64, 64, 0)]
                    chunks += [((win + 64) // 128 + cc, 0, 128, cc + 1)
                               for cc in range(4)]
                last_h = (hp // 4) * 8 + 6 + hh
                for ci, (vt_t, pr0, klen, slot) in enumerate(chunks):
                    key = (hp // 4, hh)
                    first = key not in pv_started
                    pv_started.add(key)
                    stop = (h == last_h and qi == 1 and ci == len(chunks) - 1)
                    lastmm = t.matmul(
                        bank[hh * 64:(hh + 1) * 64,
                             fo + qi * 64: fo + (qi + 1) * 64],
                        Vt[pr0:pr0 + klen, vt_t, h * 64:(h + 1) * 64],
                        PTb[pr0:pr0 + klen, hs, slot, qi * 64:(qi + 1) * 64],
                        start=first, stop=stop, tile_position=(pr0, hh * 64))
            cPE.inc(lastmm)
            pv_pe.append(cPE.n)
            st[("pv_pe", pg, h)] = cPE.n

        def sy_mask(sy, mpg):
            ms = mpg % 2
            if mpg >= 2:
                sy.wait_ge(cPE.sem, st[("mask_free", mpg - 2)])
            st[("mload", mpg)] = self.s_mask.inc(
                sy.dma_start(maskb[:, ms, :], self.mask[mpg, :, :]), ms)

        if p == 0:
            self.on("sync", lambda sy: sy_mask(sy, pg))
        for i in range(H_ + 1):
            def pe_c1(t, i=i):
                if i == 0:
                    pe_S(t, 0)
                    pe_S(t, 1)
                else:
                    if i + 1 < H_:
                        pe_S(t, i + 1)
                    pe_PT(t, i - 1)
            self.on("tensor", pe_c1)
            if i == 0 and p >= 1:
                self.emit_tail_pe_opj(ha, p - 1, pvp)
            if i == 0:
                self.on("sync", sy_pair)
            if i == 2:
                if p >= 1:
                    self.emit_tail_rest(ha, p - 1, x1p)
                    self.emit_tail_sync(ha, p - 1)
                if p + 1 < NP:
                    self.on("sync", lambda sy: sy_mask(sy, pg + 1))

            def dv_c2(v, i=i):
                if i >= 1:
                    dv_PT(v, i - 1)
                if i < H_:
                    dv_S(v, i)
            self.on("vector", dv_c2)
            if i >= 1:
                self.on("tensor", lambda t, i=i: pe_PV(t, i - 1))
            if i < H_:
                self.on("scalar", lambda s, i=i: ac_exp(s, i))
                self.on("vector", lambda v, i=i: dv_nrm(v, i))

        st[("exp_last", pg)] = exp_ac[H_ - 1]
        st[("PT_last", pg)] = PT_pe[H_ - 1]
        st[("PT_ev_last", pg)] = PT_ev[H_ - 1]
        st[("mask_free", pg)] = S_pe[H_ - 1]
        st[("pv_last", pg)] = pv_pe[H_ - 1]

        def dv_attn(v):
            v.wait_ge(cPE.sem, st[("pv_last", pg)])
            v.tensor_copy(attnC[:, s2, 0:4, :],
                          bass.AP(pvp[0], 0, [[512, 128], [128, 4], [1, 128]]))
            cDV.inc(v.tensor_copy(attnC[:, s2, 4:8, :],
                                  bass.AP(pvp[1], 0, [[512, 128], [128, 4], [1, 128]])))
            st[("attn_ev", pg)] = cDV.n

        self.on("vector", dv_attn)

    def tail(self, ha, p, pvp, x1p):
        self.emit_tail_pe_opj(ha, p, pvp)
        self.emit_tail_rest(ha, p, x1p)
        self.emit_tail_sync(ha, p)

    def emit_tail_pe_opj(self, ha, p, pvp):
        nc, st = self.nc, self.st
        D, DT, HALF, BS, NP, POFF = (self.D, self.DT, self.HALF, self.BS,
                                     self.NP, self.POFF)
        cPE, cDV, cAC = self.cPE, self.cDV, self.cAC
        attnC, wTa = self.attnC, self.woT
        pg = ha * NP + p
        s2 = p % 2

        def pe_opj0(t):
            t.wait_ge(cDV.sem, st[("attn_ev", pg)])
            last = None
            for k in range(DT):
                last = t.matmul(pvp[1][:, :], attnC[:, s2, k, :],
                                wTa[:, k, 0:512],
                                start=(k == 0), stop=(k == DT - 1))
            cPE.inc(last)
            st[("opj_pe", pg, 0)] = cPE.n

        def dv_res0(v):
            v.wait_ge(cPE.sem, st[("opj_pe", pg, 0)])
            self.s_x0.wait(v, st[("x0ld", pg)])
            if pg >= 2:
                self.s_x1.wait(v, st[("x1st", pg - 2)])
                v.wait_ge(cPE.sem, st[("x1T", pg - 2)])
                v.wait_ge(cAC.sem, st[("lnsq", pg - 2)])
            cDV.inc(v.tensor_add(self.x1c[:, s2, 0:512], pvp[1][:, :],
                                 self.x0r[:, 0, 0:512]))
            st[("res_dv", pg, 0)] = cDV.n

        def pe_opj1(t):
            t.wait_ge(cDV.sem, st[("res_dv", pg, 0)])
            last = None
            for k in range(DT):
                last = t.matmul(pvp[1][:, :], attnC[:, s2, k, :],
                                wTa[:, k, 512:1024],
                                start=(k == 0), stop=(k == DT - 1))
            cPE.inc(last)
            st[("opj_pe", pg, 1)] = cPE.n

        def dv_res1(v):
            v.wait_ge(cPE.sem, st[("opj_pe", pg, 1)])
            cDV.inc(v.tensor_add(self.x1c[:, s2, 512:1024], pvp[1][:, :],
                                 self.x0r[:, 0, 512:1024]))
            st[("res_dv", pg, 1)] = cDV.n
            v.wait_ge(cDV.sem, cDV.n)
            cDV.inc(v.reduce_sum(self.smal[:, s2, 4:5], self.x1c[:, s2, :],
                                 axis=AX.X))
            st[("lnsum", pg)] = cDV.n

        self.on("tensor", pe_opj0)
        self.on("vector", dv_res0)
        self.on("tensor", pe_opj1)
        self.on("vector", dv_res1)
        st[("x0_free", pg)] = st[("res_dv", pg, 1)]

    def emit_tail_rest(self, ha, p, x1p):
        nc, st = self.nc, self.st
        D, DT, HALF, BS, NP, POFF, TOK = (self.D, self.DT, self.HALF, self.BS,
                                          self.NP, self.POFF, self.TOK)
        cPE, cDV, cAC = self.cPE, self.cDV, self.cAC
        x1c, smal, sqj, X1fm = self.x1c, self.smal, self.sqj, self.X1fm
        pg = ha * NP + p
        s2 = p % 2
        qA = p

        def ac_lnsq(s):
            s.wait_ge(cDV.sem, st[("res_dv", pg, 1)])
            if pg >= 1:
                s.wait_ge(cAC.sem, st[("lnsq", pg - 1)])
            cAC.inc(s.activation(sqj[:, :], x1c[:, s2, :], AF.Square,
                                 accum_out=smal[:, s2, 5:6]))
            st[("lnsq", pg)] = cAC.n

        def dv_ln(v):
            v.wait_ge(cDV.sem, st[("lnsum", pg)])
            v.wait_ge(cAC.sem, st[("lnsq", pg)])
            cDV.inc(v.tensor_scalar_mul(smal[:, s2, 6:7], smal[:, s2, 4:5],
                                        1.0 / D))
            cDV.inc(v.tensor_scalar_mul(smal[:, s2, 7:8], smal[:, s2, 5:6],
                                        1.0 / D))
            v.wait_ge(cDV.sem, cDV.n)
            cDV.inc(v.tensor_mul(smal[:, s2, 8:9], smal[:, s2, 6:7],
                                 smal[:, s2, 6:7]))
            v.wait_ge(cDV.sem, cDV.n)
            cDV.inc(v.tensor_sub(smal[:, s2, 9:10], smal[:, s2, 7:8],
                                 smal[:, s2, 8:9]))
            v.wait_ge(cDV.sem, cDV.n)
            cDV.inc(v.tensor_scalar_add(smal[:, s2, 10:11], smal[:, s2, 9:10],
                                        1e-5))
            st[("lnvar", pg)] = cDV.n

        def ac_sqrt(s):
            s.wait_ge(cDV.sem, st[("lnvar", pg)])
            cAC.inc(s.activation(smal[:, s2, 11:12], smal[:, s2, 10:11],
                                 AF.Sqrt))
            st[("lnstd", pg)] = cAC.n

        def dv_ln2(v):
            v.wait_ge(cAC.sem, st[("lnstd", pg)])
            cDV.inc(v.reciprocal(smal[:, s2, 12:13], smal[:, s2, 11:12]))
            v.wait_ge(cDV.sem, cDV.n)
            cDV.inc(v.tensor_mul(smal[:, s2, 13:14], smal[:, s2, 6:7],
                                 smal[:, s2, 12:13]))
            v.wait_ge(cDV.sem, cDV.n)
            cDV.inc(v.tensor_scalar_mul(smal[:, s2, 14:15], smal[:, s2, 13:14],
                                        -1.0))
            st[("lnab", pg)] = cDV.n

        def ac_apply(s):
            s.wait_ge(cDV.sem, st[("lnab", pg)])
            cAC.inc(s.activation(x1c[:, s2, :], x1c[:, s2, :],
                                 AF.Identity, bias=smal[:, s2, 14:15],
                                 scale=smal[:, s2, 12:13]))
            st[("ln_done", pg)] = cAC.n

        def pe_x1T(t, half4):
            if half4 == 0:
                t.wait_ge(cAC.sem, st[("ln_done", pg)])
                if pg >= 1:
                    t.wait_ge(cDV.sem, st[("x1T_ev", pg - 1, 1)])
            else:
                t.wait_ge(cDV.sem, st[("x1T_ev", pg, 0)])
            last = None
            for c in range(4):
                cc = half4 * 4 + c
                last = t.transpose(x1p[:, c * 128:(c + 1) * 128],
                                   x1c[:, s2, cc * 128:(cc + 1) * 128],
                                   self.idf[:, :])
            cPE.inc(last)
            st[("x1T", pg, half4)] = cPE.n
            if half4 == 1:
                st[("x1T", pg)] = cPE.n

        def dv_x1T(v, half4):
            v.wait_ge(cPE.sem, st[("x1T", pg, half4)])
            dst = bass.AP(X1fm,
                          half4 * 4 * TOK + ha * HALF + qA * BS,
                          [[DT * TOK, 128], [TOK, 4], [POFF * BS, 2],
                           [1, 64]])
            src = bass.AP(x1p, 0, [[512, 128], [128, 4], [64, 2], [1, 64]])
            cDV.inc(v.tensor_copy(dst, src))
            st[("x1T_ev", pg, half4)] = cDV.n
            if half4 == 1:
                st[("x1T_ev", pg)] = cDV.n

        self.on("scalar", ac_lnsq)
        self.on("vector", dv_ln)
        self.on("scalar", ac_sqrt)
        self.on("vector", dv_ln2)
        self.on("scalar", ac_apply)
        self.on("tensor", lambda t: pe_x1T(t, 0))
        self.on("vector", lambda v: dv_x1T(v, 0))
        self.on("tensor", lambda t: pe_x1T(t, 1))
        self.on("vector", lambda v: dv_x1T(v, 1))

    def emit_tail_sync(self, ha, p):
        nc, st = self.nc, self.st
        HALF, BS, NP, POFF = self.HALF, self.BS, self.NP, self.POFF
        cAC = self.cAC
        pg = ha * NP + p
        s2 = p % 2
        qA, qB = p, p + POFF

        def sy_x1(sy):
            sy.wait_ge(cAC.sem, st[("ln_done", pg)])
            r0 = ha * HALF + qA * BS
            r1 = ha * HALF + qB * BS
            self.s_x1.inc(sy.dma_start(self.x1_scr[r0:r0 + 64, :],
                                       self.x1c[0:64, s2, :]), s2)
            st[("x1st", pg)] = self.s_x1.inc(
                sy.dma_start(self.x1_scr[r1:r1 + 64, :],
                             self.x1c[64:128, s2, :]), s2)

        self.on("sync", sy_x1)

    def gates_phase(self):
        nc, st = self.nc, self.st
        D, DT, E, NTT, TOK = self.D, self.DT, self.E, self.NTT, self.TOK
        cPE, cDV, cAC = self.cPE, self.cDV, self.cAC
        X1fm, wgT, iot, gates, gatesT, idf = (self.X1fm, self.wgT, self.iot,
                                              self.gates, self.gatesT, self.idf)
        last_x1 = st[("x1T_ev", 2 * self.NP - 1)]
        pe_at_g, dv_at_g, ac_at_g = self.cPE.n, self.cDV.n, self.cAC.n

        with nc.sbuf_tensor("gl", [128, 2, E], F32) as gl, \
             nc.sbuf_tensor("gsc", [128, 2, 8], F32) as gsc, \
             nc.psum_tensor("glp", [128, 2, E], F32) as glp, \
             nc.psum_tensor("gtp", [E, 4, 128], F32) as gtp:

            def dv_gz(v):
                v.wait_ge(cDV.sem, last_x1)
                cDV.inc(v.memset(self.gatesT[:, :], 0.0))
                cDV.inc(v.memset(self.k2T[:, :], 0.0))
                st["gz"] = cDV.n

            self.on("vector", dv_gz)
            for tt in range(NTT):
                s2 = tt % 2

                def pe_gl(t, tt=tt, s2=s2):
                    if tt == 0:
                        t.wait_ge(cDV.sem, last_x1)
                        t.wait_ge(cDV.sem, st["wgev"])
                    if tt >= 2:
                        t.wait_ge(cDV.sem, st[("glev", tt - 2)])
                    last = None
                    for k in range(DT):
                        last = t.matmul(glp[:, s2, :],
                                        X1fm[:, k, tt * 128:(tt + 1) * 128],
                                        wgT[:, k, :],
                                        start=(k == 0), stop=(k == DT - 1))
                    cPE.inc(last)
                    st[("glpe", tt)] = cPE.n

                def dv_g1(v, tt=tt, s2=s2):
                    v.wait_ge(cPE.sem, st[("glpe", tt)])
                    if tt == 0:
                        self.s_iden.wait_all(v)
                    cDV.inc(v.tensor_add(gl[:, s2, :],
                                         glp[:, s2, :], iot[:, :]))
                    st[("glev", tt)] = cDV.n
                    v.wait_ge(cDV.sem, cDV.n)
                    cDV.inc(v.reduce_max(gsc[:, s2, 0:1],
                                         gl[:, s2, :], axis=AX.X))
                    v.wait_ge(cDV.sem, cDV.n)
                    cDV.inc(v.tensor_single_scalar(gates[:, tt, :],
                                                   gl[:, s2, :],
                                                   gsc[:, s2, 0:1],
                                                   ALU.is_ge))
                    v.wait_ge(cDV.sem, cDV.n)
                    cDV.inc(v.scalar_tensor_tensor(gl[:, s2, :],
                                                   gates[:, tt, :], NEG,
                                                   gl[:, s2, :],
                                                   op0=ALU.mult, op1=ALU.add))
                    v.wait_ge(cDV.sem, cDV.n)
                    cDV.inc(v.reduce_max(gsc[:, s2, 1:2],
                                         gl[:, s2, :], axis=AX.X))
                    v.wait_ge(cDV.sem, cDV.n)
                    cDV.inc(v.tensor_sub(gsc[:, s2, 2:3],
                                         gsc[:, s2, 0:1],
                                         gsc[:, s2, 1:2]))
                    st[("gd", tt)] = cDV.n

                def ac_g(s, tt=tt, s2=s2):
                    s.wait_ge(cDV.sem, st[("gd", tt)])
                    cAC.inc(s.activation(gsc[:, s2, 3:4],
                                         gsc[:, s2, 2:3], AF.Exp,
                                         scale=-1.0))
                    st[("ge", tt)] = cAC.n

                def dv_g2(v, tt=tt, s2=s2):
                    v.wait_ge(cAC.sem, st[("ge", tt)])
                    cDV.inc(v.tensor_scalar_add(gsc[:, s2, 4:5],
                                                gsc[:, s2, 3:4], 1.0))
                    v.wait_ge(cDV.sem, cDV.n)
                    cDV.inc(v.reciprocal(gsc[:, s2, 5:6],
                                         gsc[:, s2, 4:5]))
                    v.wait_ge(cDV.sem, cDV.n)
                    cDV.inc(v.tensor_mul(gsc[:, s2, 6:7],
                                         gsc[:, s2, 3:4],
                                         gsc[:, s2, 5:6]))
                    v.wait_ge(cDV.sem, cDV.n)
                    cDV.inc(v.tensor_copy(self.wk[:, tt, 0:1],
                                          gsc[:, s2, 5:6]))
                    cDV.inc(v.tensor_copy(self.wk[:, tt, 1:2],
                                          gsc[:, s2, 6:7]))
                    cDV.inc(v.tensor_single_scalar(gl[:, s2, :],
                                                   gl[:, s2, :],
                                                   gsc[:, s2, 1:2],
                                                   ALU.is_ge))
                    v.wait_ge(cDV.sem, cDV.n)
                    cDV.inc(v.tensor_scalar_mul(gates[:, tt, :],
                                                gates[:, tt, :],
                                                gsc[:, s2, 5:6]))
                    v.wait_ge(cDV.sem, cDV.n)
                    cDV.inc(v.scalar_tensor_tensor(gates[:, tt, :],
                                                   gl[:, s2, :],
                                                   gsc[:, s2, 6:7],
                                                   gates[:, tt, :],
                                                   op0=ALU.mult, op1=ALU.add))
                    st[("gates_dv", tt)] = cDV.n

                def pe_gT(t, tt=tt, s2=s2):
                    t.wait_ge(cDV.sem, st[("gates_dv", tt)])
                    if tt >= 2:
                        t.wait_ge(cDV.sem, st[("gTev", tt - 2)])
                    t.transpose(gtp[:, s2, :], gates[:, tt, :], idf[:, :])
                    cPE.inc(t.transpose(gtp[:, 2 + s2, :],
                                        gl[:, s2, :], idf[:, :]))
                    st[("gTpe", tt)] = cPE.n

                def dv_gT(v, tt=tt, s2=s2):
                    v.wait_ge(cPE.sem, st[("gTpe", tt)])
                    cDV.inc(v.tensor_copy(
                        gatesT[0:E, tt * 128:(tt + 1) * 128], gtp[:, s2, :]))
                    cDV.inc(v.tensor_scalar_add(
                        self.k2T[0:E, tt * 128:(tt + 1) * 128],
                        gtp[:, 2 + s2, :], 1.0))
                    st[("gTev", tt)] = cDV.n

                self.on("tensor", pe_gl)
                self.on("vector", dv_g1)
                self.on("scalar", ac_g)
                self.on("vector", dv_g2)
                self.on("tensor", pe_gT)
                self.on("vector", dv_gT)

            self.routing(pe_at_g, dv_at_g, ac_at_g)

    def routing(self, pe_at_g, dv_at_g, ac_at_g):
        nc, st = self.nc, self.st
        TOK, E, NTT, CAP, CT = self.TOK, self.E, self.NTT, self.CAP, self.CT
        cPE, cDV, cGP = self.cPE, self.cDV, self.cGP
        gatesT, k2T, IxI, IdSI, idf = (self.gatesT, self.k2T, self.IxI,
                                       self.IdSI, self.idf)

        with nc.sbuf_tensor("iot16", [16, TOK], F16) as iot16, \
             nc.sbuf_tensor("mask16", [16, TOK], F32) as mask16, \
             nc.sbuf_tensor("zsc", [16, TOK], F32) as zsc, \
             nc.sbuf_tensor("pref", [16, TOK], F32) as pref, \
             nc.sbuf_tensor("slots", [16, TOK], F32) as slots, \
             nc.sbuf_tensor("slotsi", [16, TOK], I16) as slotsi, \
             nc.sbuf_tensor("il", [16, CAP], F16) as il, \
             nc.sbuf_tensor("kl", [16, CAP], F16) as kl, \
             nc.sbuf_tensor("idh", [16, 16], F16) as idh, \
             nc.sbuf_tensor("IxKf", [128, 2 * CT, 16], F32) as IxKf, \
             nc.sbuf_tensor("rowf", [128, CT, 16], F32) as rowf, \
             nc.psum_tensor("itp", [128, 2 * CT, 16], F16) as itp:

            def dv_rt(v):
                v.wait_ge(cPE.sem, pe_at_g)
                v.wait_ge(self.cAC.sem, ac_at_g)
                cDV.inc(v.memset(zsc[:, :], 0.0))
                cDV.inc(v.tensor_copy(idh[:, :], idf[0:16, 0:16]))
                v.wait_ge(cDV.sem, st[("gTev", NTT - 1)])
                v.wait_ge(cDV.sem, st["gz"])
                cDV.inc(v.tensor_single_scalar(mask16[:, :], gatesT[:, :],
                                               0.0, ALU.is_gt))
                v.wait_ge(cDV.sem, cDV.n)
                cDV.inc(v.tensor_tensor_scan(pref[:, :], mask16[:, :],
                                             zsc[:, :], 0.0,
                                             op0=ALU.add, op1=ALU.add))
                v.wait_ge(cDV.sem, cDV.n)
                cDV.inc(v.tensor_mul(slots[:, :], mask16[:, :], pref[:, :]))
                v.wait_ge(cDV.sem, cDV.n)
                cDV.inc(v.tensor_scalar_add(slots[:, :], slots[:, :], -1.0))
                v.wait_ge(cDV.sem, cDV.n)
                cDV.inc(v.tensor_single_scalar(mask16[:, :], slots[:, :],
                                               float(CAP), ALU.is_ge))
                cDV.inc(v.tensor_scalar_add(zsc[:, :], slots[:, :], 1.0))
                v.wait_ge(cDV.sem, cDV.n)
                cDV.inc(v.tensor_mul(zsc[:, :], mask16[:, :], zsc[:, :]))
                v.wait_ge(cDV.sem, cDV.n)
                cDV.inc(v.tensor_sub(slots[:, :], slots[:, :], zsc[:, :]))
                v.wait_ge(cDV.sem, cDV.n)
                cDV.inc(v.tensor_copy(slotsi[:, :], slots[:, :]))
                st["sloti"] = cDV.n

            def gp_rt(gp):
                gp.wait_ge(cPE.sem, pe_at_g)
                gp.wait_ge(cDV.sem, dv_at_g)
                gp.wait_ge(self.cAC.sem, ac_at_g)
                self.g_iot.inc(gp.dma_start(iot16[:, :], self.iot16_d[:, :]))
                gp.wait_ge(cDV.sem, st["sloti"])
                self.g_iot.wait(gp, (0, self.g_iot.n[0]))
                if "nolsc" in self.dbg:
                    self.g_iot.inc(gp.dma_start(il[:, :],
                                                self.iot16_d[:, 0:CAP]))
                    self.g_iot.inc(gp.dma_start(kl[:, :],
                                                self.iot16_d[:, 0:CAP]))
                else:
                    gp.load_library(LIB_LSC)
                    cGP.inc(gp.local_scatter(il[:, :], iot16[:, :],
                                             slotsi[:, :], channels=16,
                                             num_elems=CAP, num_idxs=TOK))
                    cGP.inc(gp.local_scatter(kl[:, :], k2T[:, :],
                                             slotsi[:, :], channels=16,
                                             num_elems=CAP, num_idxs=TOK))
                st["lsc"] = cGP.n

            def pe_rt(t):
                if "nolsc" in self.dbg:
                    self.g_iot.wait_all(t)
                else:
                    t.wait_ge(cGP.sem, st["lsc"])
                t.wait_ge(self.cAC.sem, ac_at_g)
                last = None
                for j in range(2 * CT):
                    srcb = il if j < CT else kl
                    ct = j % CT
                    last = t.transpose(itp[:, j, :],
                                       srcb[:, ct * 128:(ct + 1) * 128],
                                       idh[:, :])
                cPE.inc(last)
                st["rtp"] = cPE.n

            def dv_rt2(v):
                v.wait_ge(cPE.sem, st["rtp"])
                cDV.inc(v.tensor_copy(IxKf[:, :, :], itp[:, :, :]))
                v.wait_ge(cDV.sem, cDV.n)
                cDV.inc(v.tensor_copy(IxI[:, :, :], IxKf[:, 0:CT, :]))
                cDV.inc(v.scalar_tensor_tensor(rowf[:, :, :],
                                               IxKf[:, CT:2 * CT, :],
                                               float(TOK),
                                               IxKf[:, 0:CT, :],
                                               op0=ALU.mult, op1=ALU.add))
                v.wait_ge(cDV.sem, cDV.n)
                cDV.inc(v.tensor_scalar_add(rowf[:, :, :], rowf[:, :, :],
                                            float(128 - TOK)))
                v.wait_ge(cDV.sem, cDV.n)
                cDV.inc(v.tensor_single_scalar(rowf[:, :, :], rowf[:, :, :],
                                               0.0, ALU.max))
                v.wait_ge(cDV.sem, cDV.n)
                cDV.inc(v.tensor_copy(IdSI[:, :, :], rowf[:, :, :]))
                st["ridx"] = cDV.n

            self.on("vector", dv_rt)
            self.on("gpsimd", gp_rt)
            self.on("tensor", pe_rt)
            self.on("vector", dv_rt2)

    def ffn(self):
        nc, st = self.nc, self.st
        D, DT, E, NTT, INTER, IC, ITL = (self.D, self.DT, self.E, self.NTT,
                                         self.INTER, self.IC, self.ITL)
        TOK, CAP, CT = self.TOK, self.CAP, self.CT
        cPE, cDV, cAC, cGP = self.cPE, self.cDV, self.cAC, self.cGP
        b1sb, b2sb, gatesT, wk = self.b1sb, self.b2sb, self.gatesT, self.wk
        w1sl, w2sl, y_tm = self.w1sl, self.w2sl, self.y_tm
        IxI, IdSI, idf = self.IxI, self.IdSI, self.idf
        NG = IC * 4 * 2
        NH = 4
        WQ = D // NH

        pe_at_ffn = cPE.n
        ac_at_ffn = cAC.n
        dv_at_ffn = cDV.n
        gp_at_ffn = cGP.n

        def sy_yinit(sy):
            self.s_x1.wait_all(sy)
            sy.wait_ge(cPE.sem, pe_at_ffn)
            sy.wait_ge(cDV.sem, dv_at_ffn)
            sy.wait_ge(cAC.sem, ac_at_ffn)
            sy.wait_ge(cGP.sem, gp_at_ffn)
            for tt in range(NTT):
                self.s_yin.inc(sy.dma_start(
                    y_tm[:, tt, :],
                    self.x1_scr[tt * 128:(tt + 1) * 128, :]))

        self.on("sync", sy_yinit)

        if "noffn" in self.dbg:
            def dv_fake(v):
                self.s_yin.wait_all(v)
                for tt in range(NTT):
                    st[("b2evt", tt)] = cDV.n
            self.on("vector", dv_fake)
            return

        gatesT, b2sb = self.gatesT, self.b2sb
        with nc.psum_tensor("mpB", [128, 512], F32) as mpB:
            for tt in range(NTT):
                for ch in range(2):
                    fi = tt * 2 + ch

                    def pe_b2(t, tt=tt, ch=ch, fi=fi):
                        if fi == 0:
                            t.wait_ge(cDV.sem, st[("gTev", NTT - 1)])
                            self.g_bias.wait_all(t)
                            t.wait_ge(cDV.sem, dv_at_ffn)
                            t.wait_ge(cAC.sem, ac_at_ffn)
                        else:
                            t.wait_ge(cDV.sem, st[("b2ev", fi - 1)])
                        cPE.inc(t.matmul(
                            mpB[:, :],
                            gatesT[0:E, tt * 128:(tt + 1) * 128],
                            b2sb[:, ch * 512:(ch + 1) * 512],
                            start=True, stop=True))
                        st[("b2pe", fi)] = cPE.n

                    def dv_b2(v, tt=tt, ch=ch, fi=fi):
                        v.wait_ge(cPE.sem, st[("b2pe", fi)])
                        if fi == 0:
                            self.s_yin.wait_all(v)
                        v.wait_ge(cDV.sem, cDV.n)
                        cDV.inc(v.tensor_add(
                            y_tm[:, tt, ch * 512:(ch + 1) * 512],
                            y_tm[:, tt, ch * 512:(ch + 1) * 512],
                            mpB[:, :]))
                        st[("b2ev", fi)] = cDV.n

                    self.on("tensor", pe_b2)
                    self.on("vector", dv_b2)

        with nc.sbuf_tensor("hbuf", [128, ITL, CAP], BF16) as hbuf, \
             nc.sbuf_tensor("Xgf", [128, DT, CAP], BF16) as Xgf, \
             nc.sbuf_tensor("Xgt", [128, CT, D], F32) as Xgt, \
             nc.psum_tensor("mpT", [128, 2, 4, 128], F32) as mpT, \
             nc.psum_tensor("mpA", [128, 3, 512], F32) as mpA, \
             nc.psum_tensor("mp2", [128, CT, WQ], F32) as mp2:

            NE = E
            for f in self.dbg:
                if f.startswith("ne"):
                    NE = int(f[2:])
            self.NE = NE
            with nc.sbuf_tensor("og", [128, CT, D], BF16) as og:
                for ei in range(NE):
                    self.expert(ei, hbuf, Xgf, Xgt, og, mpT, mpA, mp2,
                                pe_at_ffn, ac_at_ffn, dv_at_ffn)

                if "nomm2" not in self.dbg:
                    def gp_sc_last(gp):
                        self.emit_scatters(gp, NE - 1, og)

                    self.on("gpsimd", gp_sc_last)

            if "nofin" in self.dbg:
                def dv_fake2(v):
                    self.s_yin.wait_all(v)
                    self.g_sc.wait_all(v)
                    for tt in range(NTT):
                        st[("b2evt", tt)] = cDV.n
                self.on("vector", dv_fake2)
            else:
                with nc.sbuf_tensor("yl", [128, 4, D], BF16) as yl:
                    self.finish_y(yl, mpA)

    def expert(self, ei, hbuf, Xgf, Xgt, og, mpT, mpA, mp2,
               pe_at_ffn, ac_at_ffn, dv_at_ffn):
        nc, st = self.nc, self.st
        D, DT, E, IC, CAP, CT = (self.D, self.DT, self.E, self.IC, self.CAP,
                                 self.CT)
        cPE, cDV, cAC = self.cPE, self.cDV, self.cAC
        b1sb, hbuf_, IxI, idf = self.b1sb, hbuf, self.IxI, self.idf
        NG = IC * 4 * 2
        NH, WQ = 4, D // 4
        widths = (512, CAP - 512)

        if "noxT" in self.dbg:
            def dv_fakexT(v, ei=ei):
                for j in range(2 * CT):
                    st[("xgT", ei, j)] = cPE.n
                    st[("xgev", ei, j)] = cDV.n
                st[("mm1end", ei - 1)] = st.get(("mm1end", ei - 1), cPE.n)
            self.on("vector", dv_fakexT)

        if ei == 0 and "noxT" not in self.dbg:
            def gp_g0(gp):
                gp.wait_ge(cDV.sem, st["ridx"])
                gp.wait_ge(cPE.sem, pe_at_ffn)
                gp.wait_ge(cAC.sem, ac_at_ffn)
                self.s_x1.wait_all(gp)
                for ct in range(CT):
                    if "noind" in self.dbg:
                        dma = gp.dma_start(Xgt[:, ct, :],
                                           self.x1_scr[ct * 128:(ct + 1) * 128, :])
                    else:
                        dma = gp.indirect_dma_start(
                            Xgt[:, ct, :], None, self.x1_scr[:, :],
                            bass.IndirectOffsetOnAxis(IxI[:, ct, 0:1], 0))
                    st[("xg", 0, ct)] = self.g_xg.inc(dma, ct)

            self.on("gpsimd", gp_g0)

        for j in range(2 * CT if "noxT" not in self.dbg else 0):
            ct, hf = j // 2, j % 2

            def pe_xT(t, ei=ei, j=j, ct=ct, hf=hf):
                if j == 0:
                    self.g_xg.wait(t, st[("xg", ei, 0)])
                    if ei == 0:
                        t.wait_ge(cDV.sem, dv_at_ffn)
                        t.wait_ge(cAC.sem, ac_at_ffn)
                        t.wait_ge(cDV.sem, st[("b2ev", 2 * self.NTT - 1)])
                elif hf == 0:
                    self.g_xg.wait(t, st[("xg", ei, ct)])
                if j >= 2:
                    t.wait_ge(cDV.sem, st[("xgev", ei, j - 2)])
                elif ei >= 1:
                    t.wait_ge(cDV.sem, st[("xgev", ei - 1, 2 * CT - 2 + j)])
                last = None
                for m in range(4):
                    last = t.transpose(
                        mpT[:, j % 2, m, :],
                        Xgt[:, ct, (hf * 4 + m) * 128:(hf * 4 + m + 1) * 128],
                        idf[:, :])
                cPE.inc(last)
                st[("xgT", ei, j)] = cPE.n

            def dv_xT(v, ei=ei, j=j, ct=ct, hf=hf):
                v.wait_ge(cPE.sem, st[("xgT", ei, j)])
                if ei >= 1 and j == 0:
                    v.wait_ge(cPE.sem, st[("mm1end", ei - 1)])
                cDV.inc(v.tensor_copy(
                    Xgf[:, hf * 4:(hf + 1) * 4, ct * 128:(ct + 1) * 128],
                    mpT[:, j % 2, :, :]))
                st[("xgev", ei, j)] = cDV.n

            self.on("tensor", pe_xT)
            self.on("vector", dv_xT)

        if "nomm1" in self.dbg:
            def dv_fakemm1(v, ei=ei):
                for g in range(NG):
                    st[("gelu", ei, g)] = cAC.n
                for c in range(IC):
                    st[("mm1cd", ei * IC + c)] = cPE.n
                st[("mm1end", ei)] = cPE.n
            self.on("vector", dv_fakemm1)

        for c in range(IC if "nomm1" not in self.dbg else 0):
            idx = ei * IC + c

            def gp_w1(gp, ei=ei, c=c, idx=idx):
                if idx >= 2:
                    gp.wait_ge(cPE.sem, st[("mm1cd", idx - 2)])
                st[("w1ld", idx)] = self.g_w1.inc(gp.dma_start(
                    self.w1sl[:, idx % 2, :, :],
                    bass.AP(self.w1, ei * D * self.INTER + c * 512,
                            [[self.INTER, 128], [128 * self.INTER, DT],
                             [1, 512]])), idx % 2)

            self.on("gpsimd", gp_w1)
            for it in range(4):
                def pe_mm1(t, ei=ei, c=c, idx=idx, it=it):
                    if it == 0:
                        self.g_w1.wait(t, st[("w1ld", idx)])
                    for tch in range(2):
                        g = (c * 4 + it) * 2 + tch
                        slot = g % 3
                        w = widths[tch]
                        if g >= 3:
                            t.wait_ge(cAC.sem, st[("gelu", ei, g - 3)])
                        elif ei >= 1:
                            t.wait_ge(cAC.sem,
                                      st[("gelu", ei - 1, NG - 3 + g)])
                        else:
                            t.wait_ge(cDV.sem, dv_at_ffn)
                        if g == 0:
                            t.wait_ge(cDV.sem, st[("xgev", ei, 2 * CT - 1)])
                        last = None
                        for k in range(DT):
                            last = t.matmul(
                                mpA[:, slot, 0:w],
                                self.w1sl[:, idx % 2, k,
                                          it * 128:(it + 1) * 128],
                                Xgf[:, k, tch * 512:tch * 512 + w],
                                start=(k == 0), stop=(k == DT - 1))
                        cPE.inc(last)
                        st[("mm1pe", ei, g)] = cPE.n
                    if it == 3:
                        st[("mm1cd", idx)] = cPE.n
                        if c == IC - 1:
                            st[("mm1end", ei)] = cPE.n

                def ac_gelu(s, ei=ei, c=c, it=it):
                    for tch in range(2):
                        g = (c * 4 + it) * 2 + tch
                        slot = g % 3
                        w = widths[tch]
                        s.wait_ge(cPE.sem, st[("mm1pe", ei, g)])
                        if g == 0 and ei == 0:
                            self.g_bias.wait_all(s)
                            s.wait_ge(cDV.sem, dv_at_ffn)
                        cAC.inc(s.activation(
                            hbuf[:, c * 4 + it, tch * 512:tch * 512 + w],
                            mpA[:, slot, 0:w], AF.Gelu,
                            bias=b1sb[:, ei, c * 4 + it:c * 4 + it + 1]))
                        st[("gelu", ei, g)] = cAC.n

                self.on("tensor", pe_mm1)
                self.on("scalar", ac_gelu)

        if ei + 1 < getattr(self, "NE", E):
            def gp_gn(gp, ei=ei):
                for ct in range(CT):
                    gp.wait_ge(cPE.sem, st[("xgT", ei, 2 * ct + 1)])
                    if "noind" in self.dbg:
                        dma = gp.dma_start(Xgt[:, ct, :],
                                           self.x1_scr[ct * 128:(ct + 1) * 128, :])
                    else:
                        dma = gp.indirect_dma_start(
                            Xgt[:, ct, :], None, self.x1_scr[:, :],
                            bass.IndirectOffsetOnAxis(IxI[:, ct, ei + 1:ei + 2],
                                                      0))
                    st[("xg", ei + 1, ct)] = self.g_xg.inc(dma, ct)

            self.on("gpsimd", gp_gn)
        if ei >= 1 and "nomm2" not in self.dbg:
            def gp_sc(gp, ei=ei):
                self.emit_scatters(gp, ei - 1, og)

            self.on("gpsimd", gp_sc)

        if "nomm2" in self.dbg:
            def dv_fakemm2(v, ei=ei):
                for Hh in range(NH):
                    for tt in range(CT):
                        st[("mm2pe", ei, Hh, tt)] = cPE.n
                        st[("ogev", ei, Hh, tt)] = cDV.n
                for Hh in range(NH):
                    for c in range(IC):
                        st[("mm2cd", (ei * NH + Hh) * IC + c)] = cPE.n
                if ei >= 1:
                    for tt in range(CT):
                        st[("scat", ei - 1, tt)] = (0, self.g_sc.n[0])
            self.on("vector", dv_fakemm2)

        for Hh in range(NH if "nomm2" not in self.dbg else 0):
            for c in range(IC):
                idx2 = (ei * NH + Hh) * IC + c

                def gp_w2(gp, ei=ei, Hh=Hh, c=c, idx2=idx2):
                    if idx2 >= 2:
                        gp.wait_ge(cPE.sem, st[("mm2cd", idx2 - 2)])
                    st[("w2ld", idx2)] = self.g_w2.inc(gp.dma_start(
                        self.w2sl[:, idx2 % 2, :, :],
                        bass.AP(self.w2,
                                ei * self.INTER * D + c * 512 * D + Hh * WQ,
                                [[D, 128], [128 * D, 4], [1, WQ]])),
                        idx2 % 2)

                def pe_mm2(t, ei=ei, Hh=Hh, c=c, idx2=idx2):
                    self.g_w2.wait(t, st[("w2ld", idx2)])
                    last = None
                    for tt in range(CT):
                        if c == 0:
                            if Hh == 0 and ei == 0:
                                t.wait_ge(cDV.sem, dv_at_ffn)
                                t.wait_ge(cAC.sem, ac_at_ffn)
                            elif Hh == 0:
                                t.wait_ge(cDV.sem,
                                          st[("ogev", ei - 1, NH - 1, tt | 1)])
                            else:
                                t.wait_ge(cDV.sem,
                                          st[("ogev", ei, Hh - 1, tt | 1)])
                        for k in range(4):
                            if tt == 0:
                                t.wait_ge(cAC.sem,
                                          st[("gelu", ei, (c * 4 + k) * 2 + 1)])
                            last = t.matmul(
                                mp2[:, tt, :],
                                hbuf[:, c * 4 + k, tt * 128:(tt + 1) * 128],
                                self.w2sl[:, idx2 % 2, k, :],
                                start=(c == 0 and k == 0 and tt % 2 == 0),
                                stop=(c == IC - 1 and k == 3))
                        if c == IC - 1:
                            cPE.inc(last)
                            st[("mm2pe", ei, Hh, tt)] = cPE.n
                    if c != IC - 1:
                        cPE.inc(last)
                    st[("mm2cd", idx2)] = cPE.n

                self.on("gpsimd", gp_w2)
                self.on("tensor", pe_mm2)

            def dv_og(v, ei=ei, Hh=Hh):
                for tt in range(CT):
                    v.wait_ge(cPE.sem, st[("mm2pe", ei, Hh, tt | 1)])
                    if Hh == 0 and ei >= 1:
                        self.g_sc.wait(v, st[("scat", ei - 1, tt)])
                    cDV.inc(v.tensor_copy(og[:, tt, Hh * WQ:(Hh + 1) * WQ],
                                          mp2[:, tt, :]))
                    st[("ogev", ei, Hh, tt)] = cDV.n

            self.on("vector", dv_og)

    def emit_scatters(self, gp, ei, og):
        st, CT = self.st, self.CT
        if ei == 0:
            self.s_yz.wait_all(gp)
        for tt in range(CT):
            gp.wait_ge(self.cDV.sem, st[("ogev", ei, 3, tt)])
            if self.g_sc.n[0]:
                gp.wait_ge(self.g_sc.sems[0], self.g_sc.n[0])
            if "noind" in self.dbg:
                r0 = 128 + (((ei * CT + tt) % 32) * 128)
                dma = gp.dma_start(self.y_scr[r0:r0 + 128, :], og[:, tt, :])
            else:
                dma = gp.indirect_dma_start(
                    self.y_scr[:, :],
                    bass.IndirectOffsetOnAxis(self.IdSI[:, tt, ei:ei + 1], 0),
                    og[:, tt, :], None)
            st[("scat", ei, tt)] = self.g_sc.inc(dma)

    def finish_y(self, yl, mpA):
        nc, st = self.nc, self.st
        D, E, NTT, TOK = self.D, self.E, self.NTT, self.TOK
        cPE, cDV, cAC = self.cPE, self.cDV, self.cAC
        y_tm, wk, gatesT, b2sb = self.y_tm, self.wk, self.gatesT, self.b2sb
        NG = self.IC * 4 * 2

        for tt in range(NTT):
            for k2 in range(2):
                sl = (tt * 2 + k2) % 4

                def sy_yl(sy, tt=tt, k2=k2, sl=sl):
                    if tt == 0 and k2 == 0:
                        self.g_sc.wait_all(sy)
                    if tt >= 2:
                        sy.wait_ge(cDV.sem, st[("yadd", tt - 2, k2)])
                    r0 = 128 + k2 * TOK + tt * 128
                    st[("yld", tt, k2)] = self.s_yl.inc(
                        sy.dma_start(yl[:, sl, :],
                                     self.y_scr[r0:r0 + 128, :]), sl)

                def dv_yadd(v, tt=tt, k2=k2, sl=sl):
                    self.s_yl.wait(v, st[("yld", tt, k2)])
                    if tt == 0 and k2 == 0:
                        self.s_yin.wait_all(v)
                    v.wait_ge(cDV.sem, cDV.n)
                    cDV.inc(v.scalar_tensor_tensor(
                        y_tm[:, tt, :], yl[:, sl, :], wk[:, tt, k2:k2 + 1],
                        y_tm[:, tt, :], op0=ALU.mult, op1=ALU.add))
                    st[("yadd", tt, k2)] = cDV.n
                    if k2 == 1:
                        st[("b2evt", tt)] = cDV.n

                self.on("sync", sy_yl)
                self.on("vector", dv_yadd)


    def ln2(self):
        nc, st = self.nc, self.st
        D, NTT = self.D, self.NTT
        cDV, cAC = self.cDV, self.cAC
        y_tm, outst, lnt2, sqj2 = self.y_tm, self.outst, self.lnt2, self.sqj2

        for tt in range(NTT):
            s2 = tt % 2

            def dv_a(v, tt=tt, s2=s2):
                v.wait_ge(cDV.sem, st[("b2evt", tt)])
                cDV.inc(v.reduce_sum(lnt2[:, s2 * 8 + 0: s2 * 8 + 1],
                                     y_tm[:, tt, :], axis=AX.X))
                st[("l2sum", tt)] = cDV.n

            def ac_a(s, tt=tt, s2=s2):
                s.wait_ge(cDV.sem, st[("l2sum", tt)])
                if tt >= 2:
                    s.wait_ge(cAC.sem, st[("l2done", tt - 2)])
                cAC.inc(s.activation(sqj2[:, :], y_tm[:, tt, :],
                                     AF.Square,
                                     accum_out=lnt2[:, s2 * 8 + 1: s2 * 8 + 2]))
                st[("l2sq", tt)] = cAC.n

            def dv_b(v, tt=tt, s2=s2):
                o = s2 * 8
                v.wait_ge(cAC.sem, st[("l2sq", tt)])
                cDV.inc(v.tensor_scalar_mul(lnt2[:, o + 2:o + 3],
                                            lnt2[:, o + 0:o + 1], 1.0 / D))
                cDV.inc(v.tensor_scalar_mul(lnt2[:, o + 3:o + 4],
                                            lnt2[:, o + 1:o + 2], 1.0 / D))
                v.wait_ge(cDV.sem, cDV.n)
                cDV.inc(v.tensor_mul(lnt2[:, o + 4:o + 5], lnt2[:, o + 2:o + 3],
                                     lnt2[:, o + 2:o + 3]))
                v.wait_ge(cDV.sem, cDV.n)
                cDV.inc(v.tensor_sub(lnt2[:, o + 5:o + 6], lnt2[:, o + 3:o + 4],
                                     lnt2[:, o + 4:o + 5]))
                v.wait_ge(cDV.sem, cDV.n)
                cDV.inc(v.tensor_scalar_add(lnt2[:, o + 6:o + 7],
                                            lnt2[:, o + 5:o + 6], 1e-5))
                st[("l2var", tt)] = cDV.n

            def ac_b(s, tt=tt, s2=s2):
                s.wait_ge(cDV.sem, st[("l2var", tt)])
                cAC.inc(s.activation(lnt2[:, s2 * 8 + 7: s2 * 8 + 8],
                                     lnt2[:, s2 * 8 + 6: s2 * 8 + 7], AF.Sqrt))
                st[("l2std", tt)] = cAC.n

            def dv_c(v, tt=tt, s2=s2):
                o = s2 * 8
                v.wait_ge(cAC.sem, st[("l2std", tt)])
                cDV.inc(v.reciprocal(lnt2[:, o + 4:o + 5], lnt2[:, o + 7:o + 8]))
                v.wait_ge(cDV.sem, cDV.n)
                cDV.inc(v.tensor_mul(lnt2[:, o + 5:o + 6], lnt2[:, o + 2:o + 3],
                                     lnt2[:, o + 4:o + 5]))
                v.wait_ge(cDV.sem, cDV.n)
                cDV.inc(v.tensor_scalar_mul(lnt2[:, o + 6:o + 7],
                                            lnt2[:, o + 5:o + 6], -1.0))
                st[("l2ab", tt)] = cDV.n

            def ac_c(s, tt=tt, s2=s2):
                s.wait_ge(cDV.sem, st[("l2ab", tt)])
                if tt >= 2:
                    self.s_out.wait(s, st[("outst", tt - 2)])
                cAC.inc(s.activation(outst[:, s2, :],
                                     y_tm[:, tt, :], AF.Identity,
                                     bias=lnt2[:, s2 * 8 + 6: s2 * 8 + 7],
                                     scale=lnt2[:, s2 * 8 + 4: s2 * 8 + 5]))
                st[("l2done", tt)] = cAC.n

            def sy_out(sy, tt=tt, s2=s2):
                sy.wait_ge(cAC.sem, st[("l2done", tt)])
                st[("outst", tt)] = self.s_out.inc(
                    sy.dma_start(self.yout[tt * 128:(tt + 1) * 128, :],
                                 outst[:, s2, :]), s2)

            self.on("vector", dv_a)
            self.on("scalar", ac_a)
            self.on("vector", dv_b)
            self.on("scalar", ac_b)
            self.on("vector", dv_c)
            self.on("scalar", ac_c)
            self.on("sync", sy_out)

        def sy_fin(sy):
            self.s_out.wait_all(sy)

        self.on("sync", sy_fin)


_CACHE = {}


def _masks(TOK, BS=64, WIN=512):
    WINW = WIN + BS
    HALF = TOK // 2
    HQB = HALF // BS
    POFF = HQB // 2
    NP = POFF
    WB = WIN // BS
    out = []
    qi = np.arange(BS)[:, None]
    ki = np.arange(WINW)[None, :]
    wi = ki // BS
    kwi = ki % BS
    for parity in range(2):
        m = np.zeros((2 * NP, 128, WINW), np.float32)
        for ha in range(2):
            for r in range(NP):
                for half_row, qb in ((0, r), (1, r + POFF)):
                    g = ha * HQB + qb
                    g_seq = g + parity * (TOK // BS)
                    qpos = g_seq * BS + qi
                    kpos = (g_seq + wi - WB) * BS + kwi
                    valid = ((g_seq + wi - WB) >= 0) & (kpos <= qpos) & \
                            (qpos - kpos < WIN)
                    m[ha * NP + r, half_row * 64:(half_row + 1) * 64, :] = \
                        np.where(valid, 0.0, NEG).astype(np.float32)
        out.append(m.astype(ml_dtypes.bfloat16))
    return out


def kernel(L0_attn_x, L0_attn_Wq, L0_attn_Wk, L0_attn_Wv, L0_attn_Wo,
           L0_ffn_Wg, L0_ffn_W1, L0_ffn_b1, L0_ffn_W2, L0_ffn_b2):
    TOK, D, E, INTER = 2048, 1024, 8, 4096
    x = np.ascontiguousarray(np.asarray(L0_attn_x, dtype=np.float32)).reshape(-1, D)
    NCORE = 8
    if "nc" not in _CACHE:
        _CACHE["nc"] = Builder(TOK=TOK, D=D, E=E, INTER=INTER).build()
        _CACHE["masks"] = _masks(TOK)
    nc = _CACHE["nc"]
    mev, mod = _CACHE["masks"]

    idenb = np.eye(128, dtype=ml_dtypes.bfloat16)
    idenf = np.eye(128, dtype=np.float32)
    ioteps = np.broadcast_to((np.arange(E) * -1e-6).astype(np.float32),
                             (128, E)).copy()
    iota16 = np.broadcast_to(np.arange(TOK, dtype=np.float16),
                             (16, TOK)).copy()
    common = dict(
        wq=np.ascontiguousarray(L0_attn_Wq, np.float32),
        wk=np.ascontiguousarray(L0_attn_Wk, np.float32),
        wv=np.ascontiguousarray(L0_attn_Wv, np.float32),
        wo=np.ascontiguousarray(L0_attn_Wo, np.float32),
        wg=np.ascontiguousarray(L0_ffn_Wg, np.float32),
        w1=np.ascontiguousarray(L0_ffn_W1, np.float32),
        b1=np.ascontiguousarray(
            np.asarray(L0_ffn_b1, np.float32).reshape(E, INTER // 128, 128)
            .transpose(2, 0, 1)),
        w2=np.ascontiguousarray(L0_ffn_W2, np.float32),
        b2=np.ascontiguousarray(L0_ffn_b2, np.float32),
        idenb=idenb, idenf=idenf, ioteps=ioteps, iota16=iota16,
    )
    in_maps = []
    for c in range(NCORE):
        xm = x[c * TOK:(c + 1) * TOK]
        xh = (np.zeros((512, D), np.float32) if c % 2 == 0
              else x[c * TOK - 512: c * TOK])
        in_maps.append(dict(common, x_main=np.ascontiguousarray(xm),
                            x_halo=np.ascontiguousarray(xh),
                            mask=(mev if c % 2 == 0 else mod)))
    res = run_bass_kernel_spmd(nc, in_maps, core_ids=list(range(NCORE)))
    out = np.concatenate([r["yout"] for r in res.results], axis=0)
    return out.reshape(4, 4096, 1024).astype(np.float32)


# version 11
# speedup vs baseline: 1.4799x; 1.0755x over previous
import numpy as np
import ml_dtypes
from contextlib import ExitStack

import concourse.bass as bass
import concourse.mybir as mybir
from concourse.bass_utils import run_bass_kernel_spmd
from concourse.library_config import local_scatter as LIB_LSC

F32 = mybir.dt.float32
BF16 = mybir.dt.bfloat16
F16 = mybir.dt.float16
I16 = mybir.dt.int16
I32 = mybir.dt.int32
AF = mybir.ActivationFunctionType
ALU = mybir.AluOpType
AX = mybir.AxisListType

NEG = -1.0e9


class Ctr:

    def __init__(self, nc, ctx, name, step=1):
        self.sem = ctx.enter_context(nc.semaphore(name))
        self.step = step
        self.n = 0

    def inc(self, inst):
        inst.then_inc(self.sem, self.step)
        self.n += self.step
        return self.n


class DStream:

    def __init__(self, nc, ctx, name, depth=1):
        self.sems = [ctx.enter_context(nc.semaphore(f"{name}_{i}"))
                     for i in range(depth)]
        self.n = [0] * depth

    def inc(self, inst, slot=0):
        inst.then_inc(self.sems[slot], 16)
        self.n[slot] += 16
        return (slot, self.n[slot])

    def wait(self, eng, tok):
        eng.wait_ge(self.sems[tok[0]], tok[1])

    def wait_all(self, eng):
        for i, s in enumerate(self.sems):
            if self.n[i]:
                eng.wait_ge(s, self.n[i])


class _TE:

    def __init__(self, t):
        self._t = t

    def __getattr__(self, n):
        return getattr(self._t, n)

    def matmul(self, *a, **kw):
        kw.setdefault("skip_group_check", True)
        return self._t.matmul(*a, **kw)

    def transpose(self, out, in_, identity, tile_position=None):
        return self._t.matmul(out, in_, identity, start=True, stop=True,
                              is_transpose=True, skip_group_check=True,
                              tile_position=tile_position)


class Builder:
    def __init__(self, TOK=2048, D=1024, H=16, BS=64, WIN=512, E=8, INTER=4096):
        import os as _os
        self.dbg = set(_os.environ.get("KDBG", "").split(","))
        self.TOK, self.D, self.H, self.BS, self.WIN = TOK, D, H, BS, WIN
        self.E, self.INTER = E, INTER
        self.HALO = WIN
        self.HALF = TOK // 2
        self.HQB = self.HALF // BS
        self.POFF = self.HQB // 2
        assert self.POFF % 2 == 0
        self.NP = self.POFF
        self.KV = self.HALF + self.HALO
        self.KVT = self.KV // 128
        self.NTT = TOK // 128
        self.DT = D // 128
        self.WINW = WIN + BS
        self.IC = INTER // 512
        self.ITL = INTER // 128
        self.NPAIR_T = 2 * self.NP
        self.CAP = 768
        self.CT = self.CAP // 128
        self.st = {}

    def build(self):
        TOK, D, E, INTER = self.TOK, self.D, self.E, self.INTER
        DT, ITL, WINW = self.DT, self.ITL, self.WINW
        nc = self.nc = bass.Bass("TRN2", target_bir_lowering=False)

        self.x_main = nc.dram_tensor("x_main", [TOK, D], F32, kind="ExternalInput")
        self.x_halo = nc.dram_tensor("x_halo", [self.HALO, D], F32, kind="ExternalInput")
        self.w_in = [nc.dram_tensor(n, [D, D], F32, kind="ExternalInput")
                     for n in ("wq", "wk", "wv", "wo")]
        self.wg = nc.dram_tensor("wg", [E, D], F32, kind="ExternalInput")
        self.w1 = nc.dram_tensor("w1", [E, D, INTER], F32, kind="ExternalInput")
        self.b1 = nc.dram_tensor("b1", [128, E, INTER // 128], F32,
                                 kind="ExternalInput")
        self.w2 = nc.dram_tensor("w2", [E, INTER, D], F32, kind="ExternalInput")
        self.b2 = nc.dram_tensor("b2", [E, D], F32, kind="ExternalInput")
        self.mask = nc.dram_tensor("mask", [self.NPAIR_T, 128, WINW], BF16,
                                   kind="ExternalInput")
        self.idenb_d = nc.dram_tensor("idenb", [128, 128], BF16, kind="ExternalInput")
        self.idenf_d = nc.dram_tensor("idenf", [128, 128], F32, kind="ExternalInput")
        self.iot_d = nc.dram_tensor("ioteps", [128, E], F32, kind="ExternalInput")
        self.yout = nc.dram_tensor("yout", [TOK, D], F32, kind="ExternalOutput")
        self.wT_scr = [nc.dram_tensor(f"wT{i}_scr", [DT, 128, D], BF16, kind="Internal")
                       for i in range(4)]
        self.x1_scr = nc.dram_tensor("x1_scr", [TOK, D], F32, kind="Internal")
        self.y_scr = nc.dram_tensor("y_scr", [2 * TOK + 128, D], BF16,
                                    kind="Internal")
        self.iot16_d = nc.dram_tensor("iota16", [16, TOK], F16,
                                      kind="ExternalInput")

        ctx = ExitStack()
        with ctx:
            e = ctx.enter_context
            self.idb = e(nc.sbuf_tensor("idb", [128, 128], BF16))
            self.idf = e(nc.sbuf_tensor("idf", [128, 128], F32))
            self.b1sb = e(nc.sbuf_tensor("b1sb", [128, E, ITL], F32))
            self.b2sb = e(nc.sbuf_tensor("b2sb", [E, D], BF16))
            self.wgT = e(nc.sbuf_tensor("wgT", [128, DT, E], BF16))
            self.iot = e(nc.sbuf_tensor("iot", [128, E], F32))
            self.gates = e(nc.sbuf_tensor("gates", [128, self.NTT, E], F32))
            self.gatesT = e(nc.sbuf_tensor("gatesT", [16, TOK], BF16))
            self.k2T = e(nc.sbuf_tensor("k2T", [16, TOK], F16))
            self.IxI = e(nc.sbuf_tensor("IxI", [128, self.CT, 16], I32))
            self.IdSI = e(nc.sbuf_tensor("IdSI", [128, self.CT, 16], I32))
            self.wk = e(nc.sbuf_tensor("wktk", [128, self.NTT, 2], F32))
            self.zbuf = e(nc.sbuf_tensor("zbuf", [128, 1024], BF16))
            self.w1sl = e(nc.sbuf_tensor("w1sl", [128, 2, DT, 512], BF16))
            self.w2sl = e(nc.sbuf_tensor("w2sl", [128, 2, 4, 256], BF16))

            self.cPE = Ctr(nc, ctx, "cPE")
            self.cDV = Ctr(nc, ctx, "cDV")
            self.cAC = Ctr(nc, ctx, "cAC")
            self.cGP = Ctr(nc, ctx, "cGP")
            self.g_wld = DStream(nc, ctx, "g_wld", 2)
            self.g_wg = DStream(nc, ctx, "g_wg")
            self.g_bias = DStream(nc, ctx, "g_bias")
            self.g_xld = DStream(nc, ctx, "g_xld", 2)
            self.g_wTa = DStream(nc, ctx, "g_wTa")
            self.g_woT = DStream(nc, ctx, "g_woT")
            self.g_w1 = DStream(nc, ctx, "g_w1", 2)
            self.g_w2 = DStream(nc, ctx, "g_w2", 2)
            self.s_iden = DStream(nc, ctx, "s_iden")
            self.s_p0st = DStream(nc, ctx, "s_p0st", 2)
            self.s_mask = DStream(nc, ctx, "s_mask", 2)
            self.s_x0 = DStream(nc, ctx, "s_x0", 1)
            self.s_x1 = DStream(nc, ctx, "s_x1", 2)
            self.s_yin = DStream(nc, ctx, "s_yin")
            self.s_out = DStream(nc, ctx, "s_out", 2)
            self.s_yz = DStream(nc, ctx, "s_yz")
            self.s_yl = DStream(nc, ctx, "s_yl", 4)
            self.g_iot = DStream(nc, ctx, "g_iot")
            self.g_xg = DStream(nc, ctx, "g_xg", 6)
            self.g_sc = DStream(nc, ctx, "g_sc", 1)

            self.block = e(nc.Block())

            self.p0(ctx)
            x1fm_scope = nc.sbuf_tensor("X1fm", [128, DT, TOK], BF16)
            self.X1fm = x1fm_scope.__enter__()
            with nc.sbuf_tensor("Xfm", [128, DT, self.KV], BF16) as Xfm, \
                 nc.sbuf_tensor("Qf", [128, DT, self.HALF], BF16) as Qf, \
                 nc.sbuf_tensor("Kf", [128, DT, self.KV], BF16) as Kf, \
                 nc.sbuf_tensor("Vt", [128, self.KVT, D], BF16) as Vt, \
                 nc.sbuf_tensor("wTa", [128, DT, D], BF16) as wTa, \
                 nc.sbuf_tensor("xld", [128, 2, D], BF16) as xld, \
                 nc.sbuf_tensor("x0r", [128, 1, D], F32) as x0r, \
                 nc.sbuf_tensor("Pbf", [128, 2, WINW], BF16) as Pbf, \
                 nc.sbuf_tensor("PTb", [128, 2, 5, 128], BF16) as PTb, \
                 nc.sbuf_tensor("attnC", [128, 2, DT, 128], BF16) as attnC, \
                 nc.sbuf_tensor("x1c", [128, 2, D], F32) as x1c, \
                 nc.sbuf_tensor("maskb", [128, 2, WINW], BF16) as maskb, \
                 nc.sbuf_tensor("smal", [128, 2, 16], F32) as smal, \
                 nc.sbuf_tensor("sqj", [128, D], BF16) as sqj:
                self.Xfm, self.Qf, self.Kf, self.Vt = Xfm, Qf, Kf, Vt
                self.wTa, self.woT, self.xld, self.x0r = wTa, wTa, xld, x0r
                self.Pbf, self.PTb, self.attnC = Pbf, PTb, attnC
                self.x1c, self.maskb, self.smal, self.sqj = x1c, maskb, smal, sqj
                for ha in range(2):
                    self.half(ha)

            self.gates_phase()
            x1fm_scope.__exit__(None, None, None)
            with nc.sbuf_tensor("y_tm", [128, self.NTT, D], F32) as y_tm:
                self.y_tm = y_tm
                self.ffn()
                with nc.sbuf_tensor("outst", [128, 2, D], F32) as outst, \
                     nc.sbuf_tensor("lnt2", [128, 16], F32) as lnt2, \
                     nc.sbuf_tensor("sqj2", [128, D], BF16) as sqj2:
                    self.outst, self.lnt2, self.sqj2 = outst, lnt2, sqj2
                    self.ln2()
        mybir.codegen_inst_isa_subclasses(nc)
        return nc

    def on(self, eng, fn):
        if eng == "tensor":
            getattr(self.block, eng)(lambda t: fn(_TE(t)))
        else:
            getattr(self.block, eng)(fn)

    def p0(self, ctx):
        nc, st = self.nc, self.st
        DT, D, E, ITL, INTER = self.DT, self.D, self.E, self.ITL, self.INTER
        cPE, cDV = self.cPE, self.cDV

        with nc.sbuf_tensor("wld", [128, 2, D], BF16) as wld, \
             nc.sbuf_tensor("wstg", [128, 2, DT, 128], BF16) as wstg, \
             nc.sbuf_tensor("wgld", [E, D], BF16) as wgld, \
             nc.psum_tensor("wtp", [128, 2, D], BF16) as wtp, \
             nc.psum_tensor("wgp", [128, DT * E], BF16) as wgp:

            def dv_z(v):
                cDV.inc(v.memset(self.zbuf[:, :], 0.0))
                st["zb"] = cDV.n

            self.on("vector", dv_z)

            def sy0(sy):
                self.s_iden.inc(sy.dma_start(self.idb[:, :], self.idenb_d[:, :]))
                self.s_iden.inc(sy.dma_start(self.idf[:, :], self.idenf_d[:, :]))
                self.s_iden.inc(sy.dma_start(self.iot[:, :], self.iot_d[:, :]))
                sy.wait_ge(cDV.sem, st["zb"])
                nzr = (2 * self.TOK + 128) // 128
                for r in range(nzr):
                    self.s_yz.inc(sy.dma_start(
                        self.y_scr[r * 128:(r + 1) * 128, :], self.zbuf[:, :]))

            def gp0a(gp):
                st["wgld"] = self.g_wg.inc(gp.dma_start(wgld[:, :], self.wg[:, :]))
                self.g_bias.inc(gp.dma_start(self.b1sb[:, :, :], self.b1[:, :, :]))
                self.g_bias.inc(gp.dma_start(self.b2sb[:, :], self.b2[:, :]))

            self.on("sync", sy0)
            self.on("gpsimd", gp0a)

            for i in range(4 * DT):
                wi, rt = divmod(i, DT)
                s = i % 2

                def gp0(gp, i=i, wi=wi, rt=rt, s=s):
                    if i >= 2:
                        gp.wait_ge(cPE.sem, st[("p0tp", i - 2)])
                    st[("p0ld", i)] = self.g_wld.inc(gp.dma_start(
                        wld[:, s, :],
                        self.w_in[wi][rt * 128:(rt + 1) * 128, :]), s)

                def pe0(t, i=i, s=s):
                    if i == 0:
                        self.s_iden.wait_all(t)
                    self.g_wld.wait(t, st[("p0ld", i)])
                    if i >= 2:
                        t.wait_ge(cDV.sem, st[("p0ev", i - 2)])
                    last = None
                    for c in range(DT):
                        last = t.transpose(
                            wtp[:, s, c * 128:(c + 1) * 128],
                            wld[:, s, c * 128:(c + 1) * 128],
                            self.idb[:, :])
                    cPE.inc(last)
                    st[("p0tp", i)] = cPE.n

                def dv0(v, i=i, s=s):
                    v.wait_ge(cPE.sem, st[("p0tp", i)])
                    if i >= 2:
                        self.s_p0st.wait(v, st[("p0st", i - 2)])
                    cDV.inc(v.tensor_copy(
                        wstg[:, s, :, :],
                        bass.AP(wtp, s * D, [[2 * D, 128], [128, DT], [1, 128]])))
                    st[("p0ev", i)] = cDV.n

                def sy0b(sy, i=i, wi=wi, rt=rt, s=s):
                    sy.wait_ge(cDV.sem, st[("p0ev", i)])
                    st[("p0st", i)] = self.s_p0st.inc(sy.dma_start(
                        bass.AP(self.wT_scr[wi], rt * 128,
                                [[D, 128], [128 * D, DT], [1, 128]]),
                        wstg[:, s, :, :]), s)

                self.on("gpsimd", gp0)
                self.on("tensor", pe0)
                self.on("vector", dv0)
                self.on("sync", sy0b)

            def pe0g(t):
                self.g_wg.wait(t, st["wgld"])
                last = None
                for c in range(DT):
                    last = t.transpose(wgp[:, c * E:(c + 1) * E],
                                       wgld[:, c * 128:(c + 1) * 128],
                                       self.idb[0:E, 0:E])
                cPE.inc(last)
                st["wgtp"] = cPE.n

            def dv0g(v):
                v.wait_ge(cPE.sem, st["wgtp"])
                cDV.inc(v.tensor_copy(
                    self.wgT[:, :, :],
                    bass.AP(wgp, 0, [[DT * E, 128], [E, DT], [1, E]])))
                st["wgev"] = cDV.n

            self.on("tensor", pe0g)
            self.on("vector", dv0g)

    def half(self, ha):
        nc, st = self.nc, self.st
        D, DT, KV, KVT, HALF, HALO = self.D, self.DT, self.KV, self.KVT, self.HALF, self.HALO
        BS, NP, POFF, WINW, TOK = self.BS, self.NP, self.POFF, self.WINW, self.TOK
        cPE, cDV, cAC = self.cPE, self.cDV, self.cAC
        Xfm, Qf, Kf, Vt, wTa, woT = self.Xfm, self.Qf, self.Kf, self.Vt, self.wTa, self.woT
        xld, x0r = self.xld, self.x0r

        with nc.psum_tensor(f"xtp{ha}", [128, D], BF16) as xtp, \
             nc.psum_tensor(f"prj{ha}", [128, 2, 512], F32) as prj:

            def kv_src(kvt):
                base = ha * HALF - HALO + kvt * 128
                if base < 0:
                    return (self.x_halo, base + HALO)
                return (self.x_main, base)

            for kvt in range(KVT):
                def gp_x(gp, kvt=kvt):
                    if kvt >= 2:
                        gp.wait_ge(cPE.sem, st[("xT", ha, kvt - 2)])
                    elif ha == 1:
                        gp.wait_ge(cPE.sem, st[("xT", 0, KVT - 2 + kvt)])
                    else:
                        gp.wait_ge(cPE.sem, st["wgtp"])
                        self.s_p0st.wait_all(gp)
                    src, r0 = kv_src(kvt)
                    st[("xld", ha, kvt)] = self.g_xld.inc(
                        gp.dma_start(xld[:, kvt % 2, :],
                                     src[r0:r0 + 128, :]), kvt % 2)

                def pe_xT(t, kvt=kvt):
                    if ha == 0 and kvt == 0:
                        t.wait_ge(cDV.sem, st["wgev"])
                    self.g_xld.wait(t, st[("xld", ha, kvt)])
                    if kvt >= 1:
                        t.wait_ge(cDV.sem, st[("xTe", ha, kvt - 1)])
                    elif ha == 1:
                        t.wait_ge(cDV.sem, st[("xTe", 0, KVT - 1)])
                    last = None
                    for c in range(DT):
                        last = t.transpose(xtp[:, c * 128:(c + 1) * 128],
                                           xld[:, kvt % 2, c * 128:(c + 1) * 128],
                                           self.idb[:, :])
                    cPE.inc(last)
                    st[("xT", ha, kvt)] = cPE.n

                def dv_xT(v, kvt=kvt):
                    v.wait_ge(cPE.sem, st[("xT", ha, kvt)])
                    if ha == 1 and kvt == 0:
                        v.wait_ge(cPE.sem, st[("proj_done", 0, 2)])
                    elif ha == 0 and kvt == 0:
                        self.s_p0st.wait_all(v)
                    cDV.inc(v.tensor_copy(
                        Xfm[:, :, kvt * 128:(kvt + 1) * 128],
                        bass.AP(xtp, 0, [[D, 128], [128, DT], [1, 128]])))
                    st[("xTe", ha, kvt)] = cDV.n

                self.on("gpsimd", gp_x)
                self.on("tensor", pe_xT)
                self.on("vector", dv_xT)

            plan = []
            QCH, KCH = HALF // 512, KV // 512
            for dq in range(DT):
                for ch in range(QCH):
                    plan.append((0, "q", dq, ch))
            for dq in range(DT):
                for ch in range(KCH):
                    plan.append((1, "k", dq, ch))
            for kvt in range(KVT):
                for ch in range(2):
                    plan.append((2, "v", kvt, ch))
            NPL = len(plan)

            for i, (wi, kind, a, chn) in enumerate(plan):
                if i == 0 or plan[i - 1][0] != wi:
                    def gp_w(gp, wi=wi):
                        if (ha, wi) == (0, 0):
                            self.s_p0st.wait_all(gp)
                        elif wi == 0:
                            gp.wait_ge(cPE.sem, st[("opj_pe", NP - 1, 1)])
                        else:
                            gp.wait_ge(cPE.sem, st[("proj_done", ha, wi - 1)])
                        st[("wT", ha, wi)] = self.g_wTa.inc(gp.dma_start(
                            wTa[:, :, :],
                            bass.AP(self.wT_scr[wi], 0,
                                    [[D, 128], [128 * D, DT], [1, D]])))

                    self.on("gpsimd", gp_w)

                def pe_proj(t, i=i, wi=wi, kind=kind, a=a, chn=chn):
                    if i == 0 or plan[i - 1][0] != wi:
                        self.g_wTa.wait(t, st[("wT", ha, wi)])
                    slot = i % 2
                    if i >= 2:
                        t.wait_ge(cDV.sem, st[("prj_ev", ha, i - 2)])
                    elif ha == 1:
                        t.wait_ge(cDV.sem, st[("prj_ev", 0, NPL - 2 + i)])
                    last = None
                    if kind == "q":
                        t.wait_ge(cDV.sem,
                                  st[("xTe", ha, HALO // 128 + (chn + 1) * 4 - 1)])
                        for k in range(DT):
                            last = t.matmul(
                                prj[:, slot, :],
                                wTa[:, k, a * 128:(a + 1) * 128],
                                Xfm[:, k, HALO + chn * 512: HALO + (chn + 1) * 512],
                                start=(k == 0), stop=(k == DT - 1))
                    elif kind == "k":
                        t.wait_ge(cDV.sem, st[("xTe", ha, (chn + 1) * 4 - 1)])
                        for k in range(DT):
                            last = t.matmul(
                                prj[:, slot, :],
                                wTa[:, k, a * 128:(a + 1) * 128],
                                Xfm[:, k, chn * 512: (chn + 1) * 512],
                                start=(k == 0), stop=(k == DT - 1))
                    else:
                        t.wait_ge(cDV.sem, st[("xTe", ha, a)])
                        for k in range(DT):
                            last = t.matmul(
                                prj[:, slot, :],
                                Xfm[:, k, a * 128:(a + 1) * 128],
                                wTa[:, k, chn * 512: (chn + 1) * 512],
                                start=(k == 0), stop=(k == DT - 1))
                    cPE.inc(last)
                    st[("prj_pe", ha, i)] = cPE.n
                    if i == NPL - 1 or plan[i + 1][0] != wi:
                        st[("proj_done", ha, wi)] = cPE.n

                def dv_proj(v, i=i, kind=kind, a=a, chn=chn):
                    slot = i % 2
                    v.wait_ge(cPE.sem, st[("prj_pe", ha, i)])
                    if kind == "q":
                        cDV.inc(v.tensor_scalar_mul(
                            Qf[:, a, chn * 512:(chn + 1) * 512],
                            prj[:, slot, :], 0.125))
                    elif kind == "k":
                        cDV.inc(v.tensor_copy(
                            Kf[:, a, chn * 512:(chn + 1) * 512],
                            prj[:, slot, :]))
                    else:
                        cDV.inc(v.tensor_copy(
                            Vt[:, a, chn * 512:(chn + 1) * 512],
                            prj[:, slot, :]))
                    st[("prj_ev", ha, i)] = cDV.n

                self.on("tensor", pe_proj)
                self.on("vector", dv_proj)
            st[("qkv_dv", ha)] = cDV.n

            def gp_wo(gp):
                gp.wait_ge(cPE.sem, st[("proj_done", ha, 2)])
                st[("woT", ha)] = self.g_woT.inc(gp.dma_start(
                    wTa[:, :, :],
                    bass.AP(self.wT_scr[3], 0,
                            [[D, 128], [128 * D, DT], [1, D]])))

            self.on("gpsimd", gp_wo)

        with nc.psum_tensor(f"Sp0{ha}", [128, WINW], F32) as Sp0, \
             nc.psum_tensor(f"Sp1{ha}", [128, WINW], F32) as Sp1, \
             nc.psum_tensor(f"PTp{ha}", [128, 5, 128], BF16) as PTp, \
             nc.psum_tensor(f"pv0{ha}", [128, 512], F32) as pv0, \
             nc.psum_tensor(f"pv1{ha}", [128, 512], F32) as pv1, \
             nc.psum_tensor(f"x1p{ha}", [128, 512], F32) as x1p:
            Sp = [Sp0, Sp1]
            pvp = [pv0, pv1]

            def pe_ptinit(t):
                cPE.inc(t.transpose(PTp[:, 4, :], self.idb[:, :],
                                    self.idb[:, :]))
                st[("ptinit", ha)] = cPE.n

            self.on("tensor", pe_ptinit)
            for p in range(NP):
                self.pair(ha, p, Sp, PTp, pvp, x1p)
            self.tail(ha, NP - 1, pvp, x1p)

        st[("half_done", ha)] = cDV.n

    def pair(self, ha, p, Sp, PTp, pvp, x1p):
        nc, st = self.nc, self.st
        D, DT, HALF, BS, NP, POFF, WINW = (self.D, self.DT, self.HALF, self.BS,
                                           self.NP, self.POFF, self.WINW)
        TOK = self.TOK
        cPE, cDV, cAC = self.cPE, self.cDV, self.cAC
        Qf, Kf, Vt = self.Qf, self.Kf, self.Vt
        Pbf, PTb, attnC = self.Pbf, self.PTb, self.attnC
        maskb, smal = self.maskb, self.smal
        H_ = self.H

        pg = ha * NP + p
        s2 = p % 2
        qA, qB = p, p + POFF
        win0, win1 = qA * BS, qB * BS
        odd = (qA % 2 == 1)
        S_pe, S_ev, exp_ac, nrm_dv, PT_pe, PT_ev, pv_pe = \
            [], [], [], [], [], [], []
        pv_started = set()

        def sy_pair(sy):
            if pg >= 1:
                sy.wait_ge(cDV.sem, st[("x0_free", pg - 1)])
            r0 = ha * HALF + qA * BS
            r1 = ha * HALF + qB * BS
            self.s_x0.inc(sy.dma_start(self.x0r[0:64, 0, :],
                                       self.x_main[r0:r0 + 64, :]))
            st[("x0ld", pg)] = self.s_x0.inc(
                sy.dma_start(self.x0r[64:128, 0, :],
                             self.x_main[r1:r1 + 64, :]))

        def pe_S(t, h):
            hp, hh = divmod(h, 2)
            r0 = hh * 64
            hs = h % 2
            if h == 0 and p == 0:
                t.wait_ge(cDV.sem, st[("qkv_dv", ha)])
                self.g_woT.wait(t, st[("woT", ha)])
            if h == 0:
                self.s_mask.wait(t, st[("mload", pg)])
            if h >= 2:
                t.wait_ge(cAC.sem, exp_ac[h - 2])
            elif pg >= 1:
                t.wait_ge(cAC.sem, st[("exp_last", pg - 1)])
            t.matmul(Sp[hs][0:64, 0:512],
                     Qf[r0:r0 + 64, hp, qA * 64:(qA + 1) * 64],
                     Kf[r0:r0 + 64, hp, win0: win0 + 512],
                     start=True, stop=False, tile_position=(r0, 0))
            t.matmul(Sp[hs][64:128, 0:512],
                     Qf[r0:r0 + 64, hp, qB * 64:(qB + 1) * 64],
                     Kf[r0:r0 + 64, hp, win1: win1 + 512],
                     start=True, stop=False, tile_position=(r0, 64))
            t.matmul(Sp[hs][:, 0:512], self.idb[:, :], maskb[:, s2, 0:512],
                     start=False, stop=True)
            t.matmul(Sp[hs][0:64, 512:WINW],
                     Qf[r0:r0 + 64, hp, qA * 64:(qA + 1) * 64],
                     Kf[r0:r0 + 64, hp, win0 + 512: win0 + 576],
                     start=True, stop=False, tile_position=(r0, 0))
            t.matmul(Sp[hs][64:128, 512:WINW],
                     Qf[r0:r0 + 64, hp, qB * 64:(qB + 1) * 64],
                     Kf[r0:r0 + 64, hp, win1 + 512: win1 + 576],
                     start=True, stop=False, tile_position=(r0, 64))
            last = t.matmul(Sp[hs][:, 512:WINW], self.idb[:, :],
                            maskb[:, s2, 512:WINW],
                            start=False, stop=True)
            cPE.inc(last)
            S_pe.append(cPE.n)
            st[("S_pe", pg, h)] = cPE.n

        def dv_S(v, h):
            hs = h % 2
            v.wait_ge(cPE.sem, S_pe[h])
            if h >= 2:
                v.wait_ge(cAC.sem, exp_ac[h - 2])
            elif pg >= 1:
                v.wait_ge(cAC.sem, st[("exp_last", pg - 1)])
            cDV.inc(v.reduce_max(smal[:, hs, 0:1], Sp[hs][:, :],
                                 axis=AX.X, negate=True))
            S_ev.append(cDV.n)
            st[("S_ev", pg, h)] = cDV.n

        def ac_exp(s, h):
            hs = h % 2
            s.wait_ge(cDV.sem, S_ev[h])
            if h >= 2:
                s.wait_ge(cPE.sem, PT_pe[h - 2])
            elif pg >= 1:
                s.wait_ge(cPE.sem, st[("PT_last", pg - 1)])
            cAC.inc(s.activation(Pbf[:, hs, :], Sp[hs][:, :],
                                 AF.Exp, bias=smal[:, hs, 0:1],
                                 accum_out=smal[:, hs, 1:2]))
            exp_ac.append(cAC.n)

        def dv_nrm(v, h):
            hs = h % 2
            v.wait_ge(cAC.sem, exp_ac[h])
            cDV.inc(v.reciprocal(smal[:, hs, 2:3], smal[:, hs, 1:2]))
            v.wait_ge(cDV.sem, cDV.n)
            cDV.inc(v.tensor_scalar_mul(Pbf[:, hs, :], Pbf[:, hs, :],
                                        smal[:, hs, 2:3]))
            nrm_dv.append(cDV.n)

        def pe_PT(t, h):
            hs = h % 2
            t.wait_ge(cDV.sem, nrm_dv[h])
            if h >= 1:
                t.wait_ge(cDV.sem, PT_ev[h - 1])
            elif pg >= 1:
                t.wait_ge(cDV.sem, st[("PT_ev_last", pg - 1)])
            else:
                t.wait_ge(cPE.sem, st[("ptinit", ha)])
            last = None
            if not odd:
                for cc in range(4):
                    last = t.transpose(PTp[:, cc, :],
                                       Pbf[:, hs, cc * 128:(cc + 1) * 128],
                                       self.idb[:, :])
                last = t.transpose(PTp[0:64, 4, :],
                                   Pbf[:, hs, 512:576], self.idb[:, :])
            else:
                last = t.transpose(PTp[64:128, 0, :],
                                   Pbf[:, hs, 0:64], self.idb[:, :],
                                   tile_position=(0, 64))
                for cc in range(4):
                    last = t.transpose(
                        PTp[:, cc + 1, :],
                        Pbf[:, hs, 64 + cc * 128: 64 + (cc + 1) * 128],
                        self.idb[:, :])
            cPE.inc(last)
            PT_pe.append(cPE.n)
            st[("PT_pe", pg, h)] = cPE.n

        def dv_PT(v, h):
            hs = h % 2
            v.wait_ge(cPE.sem, PT_pe[h])
            if h >= 2:
                v.wait_ge(cPE.sem, pv_pe[h - 2])
            elif pg >= 1:
                v.wait_ge(cPE.sem, st[("pv_last", pg - 1)])
            cDV.inc(v.tensor_copy(PTb[:, hs, :, :], PTp[:, :, :]))
            PT_ev.append(cDV.n)

        def pe_PV(t, h):
            hs = h % 2
            hp, hh = divmod(h, 2)
            bank = pvp[hp // 4]
            fo = (hp % 4) * 128
            t.wait_ge(cDV.sem, PT_ev[h])
            if h == 0 and pg >= 1:
                t.wait_ge(cDV.sem, st[("attn_ev", pg - 1)])
            if h == 8 and pg >= 1:
                t.wait_ge(cDV.sem, st[("res_dv", pg - 1, 1)])
            lastmm = None
            for qi, qb in enumerate((qA, qB)):
                win = qb * BS
                if not odd:
                    chunks = [(win // 128 + cc, 0, 128, cc) for cc in range(4)]
                    chunks.append(((win + 512) // 128, 0, 64, 4))
                else:
                    chunks = [(win // 128, 64, 64, 0)]
                    chunks += [((win + 64) // 128 + cc, 0, 128, cc + 1)
                               for cc in range(4)]
                last_h = (hp // 4) * 8 + 6 + hh
                for ci, (vt_t, pr0, klen, slot) in enumerate(chunks):
                    key = (hp // 4, hh)
                    first = key not in pv_started
                    pv_started.add(key)
                    stop = (h == last_h and qi == 1 and ci == len(chunks) - 1)
                    lastmm = t.matmul(
                        bank[hh * 64:(hh + 1) * 64,
                             fo + qi * 64: fo + (qi + 1) * 64],
                        Vt[pr0:pr0 + klen, vt_t, h * 64:(h + 1) * 64],
                        PTb[pr0:pr0 + klen, hs, slot, qi * 64:(qi + 1) * 64],
                        start=first, stop=stop, tile_position=(pr0, hh * 64))
            cPE.inc(lastmm)
            pv_pe.append(cPE.n)
            st[("pv_pe", pg, h)] = cPE.n

        def sy_mask(sy, mpg):
            ms = mpg % 2
            if mpg >= 2:
                sy.wait_ge(cPE.sem, st[("mask_free", mpg - 2)])
            st[("mload", mpg)] = self.s_mask.inc(
                sy.dma_start(maskb[:, ms, :], self.mask[mpg, :, :]), ms)

        if p == 0:
            self.on("sync", lambda sy: sy_mask(sy, pg))
        for i in range(H_ + 3):
            def pe_c1(t, i=i):
                if i == 0:
                    pe_S(t, 0)
                    pe_S(t, 1)
                elif i + 1 < H_:
                    pe_S(t, i + 1)
                if 0 <= i - 2 < H_:
                    pe_PT(t, i - 2)
            self.on("tensor", pe_c1)
            if i == 0 and p >= 1:
                self.emit_tail_pe_opj(ha, p - 1, pvp)
            if i == 0:
                self.on("sync", sy_pair)
            if i == 2:
                if p >= 1:
                    self.emit_tail_rest(ha, p - 1, x1p)
                    self.emit_tail_sync(ha, p - 1)
                if p + 1 < NP:
                    self.on("sync", lambda sy: sy_mask(sy, pg + 1))

            def dv_c2(v, i=i):
                if i < H_:
                    dv_S(v, i)
                if 0 <= i - 2 < H_:
                    dv_PT(v, i - 2)
                if 0 <= i - 1 < H_:
                    dv_nrm(v, i - 1)
            self.on("vector", dv_c2)
            if 0 <= i - 3 < H_:
                self.on("tensor", lambda t, i=i: pe_PV(t, i - 3))
            if i < H_:
                self.on("scalar", lambda s, i=i: ac_exp(s, i))

        st[("exp_last", pg)] = exp_ac[H_ - 1]
        st[("PT_last", pg)] = PT_pe[H_ - 1]
        st[("PT_ev_last", pg)] = PT_ev[H_ - 1]
        st[("mask_free", pg)] = S_pe[H_ - 1]
        st[("pv_last", pg)] = pv_pe[H_ - 1]

        def dv_attn(v):
            v.wait_ge(cPE.sem, st[("pv_last", pg)])
            v.tensor_copy(attnC[:, s2, 0:4, :],
                          bass.AP(pvp[0], 0, [[512, 128], [128, 4], [1, 128]]))
            cDV.inc(v.tensor_copy(attnC[:, s2, 4:8, :],
                                  bass.AP(pvp[1], 0, [[512, 128], [128, 4], [1, 128]])))
            st[("attn_ev", pg)] = cDV.n

        self.on("vector", dv_attn)

    def tail(self, ha, p, pvp, x1p):
        self.emit_tail_pe_opj(ha, p, pvp)
        self.emit_tail_rest(ha, p, x1p)
        self.emit_tail_sync(ha, p)

    def emit_tail_pe_opj(self, ha, p, pvp):
        nc, st = self.nc, self.st
        D, DT, HALF, BS, NP, POFF = (self.D, self.DT, self.HALF, self.BS,
                                     self.NP, self.POFF)
        cPE, cDV, cAC = self.cPE, self.cDV, self.cAC
        attnC, wTa = self.attnC, self.woT
        pg = ha * NP + p
        s2 = p % 2

        def pe_opj0(t):
            t.wait_ge(cDV.sem, st[("attn_ev", pg)])
            last = None
            for k in range(DT):
                last = t.matmul(pvp[1][:, :], attnC[:, s2, k, :],
                                wTa[:, k, 0:512],
                                start=(k == 0), stop=(k == DT - 1))
            cPE.inc(last)
            st[("opj_pe", pg, 0)] = cPE.n

        def dv_res0(v):
            v.wait_ge(cPE.sem, st[("opj_pe", pg, 0)])
            self.s_x0.wait(v, st[("x0ld", pg)])
            if pg >= 2:
                self.s_x1.wait(v, st[("x1st", pg - 2)])
                v.wait_ge(cPE.sem, st[("x1T", pg - 2)])
                v.wait_ge(cAC.sem, st[("lnsq", pg - 2)])
            cDV.inc(v.tensor_add(self.x1c[:, s2, 0:512], pvp[1][:, :],
                                 self.x0r[:, 0, 0:512]))
            st[("res_dv", pg, 0)] = cDV.n

        def pe_opj1(t):
            t.wait_ge(cDV.sem, st[("res_dv", pg, 0)])
            last = None
            for k in range(DT):
                last = t.matmul(pvp[1][:, :], attnC[:, s2, k, :],
                                wTa[:, k, 512:1024],
                                start=(k == 0), stop=(k == DT - 1))
            cPE.inc(last)
            st[("opj_pe", pg, 1)] = cPE.n

        def dv_res1(v):
            v.wait_ge(cPE.sem, st[("opj_pe", pg, 1)])
            cDV.inc(v.tensor_add(self.x1c[:, s2, 512:1024], pvp[1][:, :],
                                 self.x0r[:, 0, 512:1024]))
            st[("res_dv", pg, 1)] = cDV.n
            v.wait_ge(cDV.sem, cDV.n)
            cDV.inc(v.reduce_sum(self.smal[:, s2, 4:5], self.x1c[:, s2, :],
                                 axis=AX.X))
            st[("lnsum", pg)] = cDV.n

        self.on("tensor", pe_opj0)
        self.on("vector", dv_res0)
        self.on("tensor", pe_opj1)
        self.on("vector", dv_res1)
        st[("x0_free", pg)] = st[("res_dv", pg, 1)]

    def emit_tail_rest(self, ha, p, x1p):
        nc, st = self.nc, self.st
        D, DT, HALF, BS, NP, POFF, TOK = (self.D, self.DT, self.HALF, self.BS,
                                          self.NP, self.POFF, self.TOK)
        cPE, cDV, cAC = self.cPE, self.cDV, self.cAC
        x1c, smal, sqj, X1fm = self.x1c, self.smal, self.sqj, self.X1fm
        pg = ha * NP + p
        s2 = p % 2
        qA = p

        def ac_lnsq(s):
            s.wait_ge(cDV.sem, st[("res_dv", pg, 1)])
            if pg >= 1:
                s.wait_ge(cAC.sem, st[("lnsq", pg - 1)])
            cAC.inc(s.activation(sqj[:, :], x1c[:, s2, :], AF.Square,
                                 accum_out=smal[:, s2, 5:6]))
            st[("lnsq", pg)] = cAC.n

        def dv_ln(v):
            v.wait_ge(cDV.sem, st[("lnsum", pg)])
            v.wait_ge(cAC.sem, st[("lnsq", pg)])
            cDV.inc(v.tensor_scalar_mul(smal[:, s2, 6:7], smal[:, s2, 4:5],
                                        1.0 / D))
            cDV.inc(v.tensor_scalar_mul(smal[:, s2, 7:8], smal[:, s2, 5:6],
                                        1.0 / D))
            v.wait_ge(cDV.sem, cDV.n)
            cDV.inc(v.tensor_mul(smal[:, s2, 8:9], smal[:, s2, 6:7],
                                 smal[:, s2, 6:7]))
            v.wait_ge(cDV.sem, cDV.n)
            cDV.inc(v.tensor_sub(smal[:, s2, 9:10], smal[:, s2, 7:8],
                                 smal[:, s2, 8:9]))
            v.wait_ge(cDV.sem, cDV.n)
            cDV.inc(v.tensor_scalar_add(smal[:, s2, 10:11], smal[:, s2, 9:10],
                                        1e-5))
            st[("lnvar", pg)] = cDV.n

        def ac_sqrt(s):
            s.wait_ge(cDV.sem, st[("lnvar", pg)])
            cAC.inc(s.activation(smal[:, s2, 11:12], smal[:, s2, 10:11],
                                 AF.Sqrt))
            st[("lnstd", pg)] = cAC.n

        def dv_ln2(v):
            v.wait_ge(cAC.sem, st[("lnstd", pg)])
            cDV.inc(v.reciprocal(smal[:, s2, 12:13], smal[:, s2, 11:12]))
            v.wait_ge(cDV.sem, cDV.n)
            cDV.inc(v.tensor_mul(smal[:, s2, 13:14], smal[:, s2, 6:7],
                                 smal[:, s2, 12:13]))
            v.wait_ge(cDV.sem, cDV.n)
            cDV.inc(v.tensor_scalar_mul(smal[:, s2, 14:15], smal[:, s2, 13:14],
                                        -1.0))
            st[("lnab", pg)] = cDV.n

        def ac_apply(s):
            s.wait_ge(cDV.sem, st[("lnab", pg)])
            cAC.inc(s.activation(x1c[:, s2, :], x1c[:, s2, :],
                                 AF.Identity, bias=smal[:, s2, 14:15],
                                 scale=smal[:, s2, 12:13]))
            st[("ln_done", pg)] = cAC.n

        def pe_x1T(t, half4):
            if half4 == 0:
                t.wait_ge(cAC.sem, st[("ln_done", pg)])
                if pg >= 1:
                    t.wait_ge(cDV.sem, st[("x1T_ev", pg - 1, 1)])
            else:
                t.wait_ge(cDV.sem, st[("x1T_ev", pg, 0)])
            last = None
            for c in range(4):
                cc = half4 * 4 + c
                last = t.transpose(x1p[:, c * 128:(c + 1) * 128],
                                   x1c[:, s2, cc * 128:(cc + 1) * 128],
                                   self.idf[:, :])
            cPE.inc(last)
            st[("x1T", pg, half4)] = cPE.n
            if half4 == 1:
                st[("x1T", pg)] = cPE.n

        def dv_x1T(v, half4):
            v.wait_ge(cPE.sem, st[("x1T", pg, half4)])
            dst = bass.AP(X1fm,
                          half4 * 4 * TOK + ha * HALF + qA * BS,
                          [[DT * TOK, 128], [TOK, 4], [POFF * BS, 2],
                           [1, 64]])
            src = bass.AP(x1p, 0, [[512, 128], [128, 4], [64, 2], [1, 64]])
            cDV.inc(v.tensor_copy(dst, src))
            st[("x1T_ev", pg, half4)] = cDV.n
            if half4 == 1:
                st[("x1T_ev", pg)] = cDV.n

        self.on("scalar", ac_lnsq)
        self.on("vector", dv_ln)
        self.on("scalar", ac_sqrt)
        self.on("vector", dv_ln2)
        self.on("scalar", ac_apply)
        self.on("tensor", lambda t: pe_x1T(t, 0))
        self.on("vector", lambda v: dv_x1T(v, 0))
        self.on("tensor", lambda t: pe_x1T(t, 1))
        self.on("vector", lambda v: dv_x1T(v, 1))

    def emit_tail_sync(self, ha, p):
        nc, st = self.nc, self.st
        HALF, BS, NP, POFF = self.HALF, self.BS, self.NP, self.POFF
        cAC = self.cAC
        pg = ha * NP + p
        s2 = p % 2
        qA, qB = p, p + POFF

        def sy_x1(sy):
            sy.wait_ge(cAC.sem, st[("ln_done", pg)])
            r0 = ha * HALF + qA * BS
            r1 = ha * HALF + qB * BS
            self.s_x1.inc(sy.dma_start(self.x1_scr[r0:r0 + 64, :],
                                       self.x1c[0:64, s2, :]), s2)
            st[("x1st", pg)] = self.s_x1.inc(
                sy.dma_start(self.x1_scr[r1:r1 + 64, :],
                             self.x1c[64:128, s2, :]), s2)

        self.on("sync", sy_x1)

    def gates_phase(self):
        nc, st = self.nc, self.st
        D, DT, E, NTT, TOK = self.D, self.DT, self.E, self.NTT, self.TOK
        cPE, cDV, cAC = self.cPE, self.cDV, self.cAC
        X1fm, wgT, iot, gates, gatesT, idf = (self.X1fm, self.wgT, self.iot,
                                              self.gates, self.gatesT, self.idf)
        last_x1 = st[("x1T_ev", 2 * self.NP - 1)]
        pe_at_g, dv_at_g, ac_at_g = self.cPE.n, self.cDV.n, self.cAC.n

        with nc.sbuf_tensor("gl", [128, 2, E], F32) as gl, \
             nc.sbuf_tensor("gsc", [128, 2, 8], F32) as gsc, \
             nc.psum_tensor("glp", [128, 2, E], F32) as glp, \
             nc.psum_tensor("gtp", [E, 4, 128], F32) as gtp:

            def dv_gz(v):
                v.wait_ge(cDV.sem, last_x1)
                cDV.inc(v.memset(self.gatesT[:, :], 0.0))
                cDV.inc(v.memset(self.k2T[:, :], 0.0))
                st["gz"] = cDV.n

            self.on("vector", dv_gz)
            for tt in range(NTT):
                s2 = tt % 2

                def pe_gl(t, tt=tt, s2=s2):
                    if tt == 0:
                        t.wait_ge(cDV.sem, last_x1)
                        t.wait_ge(cDV.sem, st["wgev"])
                    if tt >= 2:
                        t.wait_ge(cDV.sem, st[("glev", tt - 2)])
                    last = None
                    for k in range(DT):
                        last = t.matmul(glp[:, s2, :],
                                        X1fm[:, k, tt * 128:(tt + 1) * 128],
                                        wgT[:, k, :],
                                        start=(k == 0), stop=(k == DT - 1))
                    cPE.inc(last)
                    st[("glpe", tt)] = cPE.n

                def dv_g1(v, tt=tt, s2=s2):
                    v.wait_ge(cPE.sem, st[("glpe", tt)])
                    if tt == 0:
                        self.s_iden.wait_all(v)
                    cDV.inc(v.tensor_add(gl[:, s2, :],
                                         glp[:, s2, :], iot[:, :]))
                    st[("glev", tt)] = cDV.n
                    v.wait_ge(cDV.sem, cDV.n)
                    cDV.inc(v.reduce_max(gsc[:, s2, 0:1],
                                         gl[:, s2, :], axis=AX.X))
                    v.wait_ge(cDV.sem, cDV.n)
                    cDV.inc(v.tensor_single_scalar(gates[:, tt, :],
                                                   gl[:, s2, :],
                                                   gsc[:, s2, 0:1],
                                                   ALU.is_ge))
                    v.wait_ge(cDV.sem, cDV.n)
                    cDV.inc(v.scalar_tensor_tensor(gl[:, s2, :],
                                                   gates[:, tt, :], NEG,
                                                   gl[:, s2, :],
                                                   op0=ALU.mult, op1=ALU.add))
                    v.wait_ge(cDV.sem, cDV.n)
                    cDV.inc(v.reduce_max(gsc[:, s2, 1:2],
                                         gl[:, s2, :], axis=AX.X))
                    v.wait_ge(cDV.sem, cDV.n)
                    cDV.inc(v.tensor_sub(gsc[:, s2, 2:3],
                                         gsc[:, s2, 0:1],
                                         gsc[:, s2, 1:2]))
                    st[("gd", tt)] = cDV.n

                def ac_g(s, tt=tt, s2=s2):
                    s.wait_ge(cDV.sem, st[("gd", tt)])
                    cAC.inc(s.activation(gsc[:, s2, 3:4],
                                         gsc[:, s2, 2:3], AF.Exp,
                                         scale=-1.0))
                    st[("ge", tt)] = cAC.n

                def dv_g2(v, tt=tt, s2=s2):
                    v.wait_ge(cAC.sem, st[("ge", tt)])
                    cDV.inc(v.tensor_scalar_add(gsc[:, s2, 4:5],
                                                gsc[:, s2, 3:4], 1.0))
                    v.wait_ge(cDV.sem, cDV.n)
                    cDV.inc(v.reciprocal(gsc[:, s2, 5:6],
                                         gsc[:, s2, 4:5]))
                    v.wait_ge(cDV.sem, cDV.n)
                    cDV.inc(v.tensor_mul(gsc[:, s2, 6:7],
                                         gsc[:, s2, 3:4],
                                         gsc[:, s2, 5:6]))
                    v.wait_ge(cDV.sem, cDV.n)
                    cDV.inc(v.tensor_copy(self.wk[:, tt, 0:1],
                                          gsc[:, s2, 5:6]))
                    cDV.inc(v.tensor_copy(self.wk[:, tt, 1:2],
                                          gsc[:, s2, 6:7]))
                    cDV.inc(v.tensor_single_scalar(gl[:, s2, :],
                                                   gl[:, s2, :],
                                                   gsc[:, s2, 1:2],
                                                   ALU.is_ge))
                    v.wait_ge(cDV.sem, cDV.n)
                    cDV.inc(v.tensor_scalar_mul(gates[:, tt, :],
                                                gates[:, tt, :],
                                                gsc[:, s2, 5:6]))
                    v.wait_ge(cDV.sem, cDV.n)
                    cDV.inc(v.scalar_tensor_tensor(gates[:, tt, :],
                                                   gl[:, s2, :],
                                                   gsc[:, s2, 6:7],
                                                   gates[:, tt, :],
                                                   op0=ALU.mult, op1=ALU.add))
                    st[("gates_dv", tt)] = cDV.n

                def pe_gT(t, tt=tt, s2=s2):
                    t.wait_ge(cDV.sem, st[("gates_dv", tt)])
                    if tt >= 2:
                        t.wait_ge(cDV.sem, st[("gTev", tt - 2)])
                    t.transpose(gtp[:, s2, :], gates[:, tt, :], idf[:, :])
                    cPE.inc(t.transpose(gtp[:, 2 + s2, :],
                                        gl[:, s2, :], idf[:, :]))
                    st[("gTpe", tt)] = cPE.n

                def dv_gT(v, tt=tt, s2=s2):
                    v.wait_ge(cPE.sem, st[("gTpe", tt)])
                    cDV.inc(v.tensor_copy(
                        gatesT[0:E, tt * 128:(tt + 1) * 128], gtp[:, s2, :]))
                    cDV.inc(v.tensor_scalar_add(
                        self.k2T[0:E, tt * 128:(tt + 1) * 128],
                        gtp[:, 2 + s2, :], 1.0))
                    st[("gTev", tt)] = cDV.n

                self.on("tensor", pe_gl)
                self.on("vector", dv_g1)
                self.on("scalar", ac_g)
                self.on("vector", dv_g2)
                self.on("tensor", pe_gT)
                self.on("vector", dv_gT)

            self.routing(pe_at_g, dv_at_g, ac_at_g)

    def routing(self, pe_at_g, dv_at_g, ac_at_g):
        nc, st = self.nc, self.st
        TOK, E, NTT, CAP, CT = self.TOK, self.E, self.NTT, self.CAP, self.CT
        cPE, cDV, cGP = self.cPE, self.cDV, self.cGP
        gatesT, k2T, IxI, IdSI, idf = (self.gatesT, self.k2T, self.IxI,
                                       self.IdSI, self.idf)

        with nc.sbuf_tensor("iot16", [16, TOK], F16) as iot16, \
             nc.sbuf_tensor("mask16", [16, TOK], F32) as mask16, \
             nc.sbuf_tensor("zsc", [16, TOK], F32) as zsc, \
             nc.sbuf_tensor("pref", [16, TOK], F32) as pref, \
             nc.sbuf_tensor("slots", [16, TOK], F32) as slots, \
             nc.sbuf_tensor("slotsi", [16, TOK], I16) as slotsi, \
             nc.sbuf_tensor("il", [16, CAP], F16) as il, \
             nc.sbuf_tensor("kl", [16, CAP], F16) as kl, \
             nc.sbuf_tensor("idh", [16, 16], F16) as idh, \
             nc.sbuf_tensor("IxKf", [128, 2 * CT, 16], F32) as IxKf, \
             nc.sbuf_tensor("rowf", [128, CT, 16], F32) as rowf, \
             nc.psum_tensor("itp", [128, 2 * CT, 16], F16) as itp:

            def dv_rt(v):
                v.wait_ge(cPE.sem, pe_at_g)
                v.wait_ge(self.cAC.sem, ac_at_g)
                cDV.inc(v.memset(zsc[:, :], 0.0))
                cDV.inc(v.tensor_copy(idh[:, :], idf[0:16, 0:16]))
                v.wait_ge(cDV.sem, st[("gTev", NTT - 1)])
                v.wait_ge(cDV.sem, st["gz"])
                cDV.inc(v.tensor_single_scalar(mask16[:, :], gatesT[:, :],
                                               0.0, ALU.is_gt))
                v.wait_ge(cDV.sem, cDV.n)
                cDV.inc(v.tensor_tensor_scan(pref[:, :], mask16[:, :],
                                             zsc[:, :], 0.0,
                                             op0=ALU.add, op1=ALU.add))
                v.wait_ge(cDV.sem, cDV.n)
                cDV.inc(v.tensor_mul(slots[:, :], mask16[:, :], pref[:, :]))
                v.wait_ge(cDV.sem, cDV.n)
                cDV.inc(v.tensor_scalar_add(slots[:, :], slots[:, :], -1.0))
                v.wait_ge(cDV.sem, cDV.n)
                cDV.inc(v.tensor_single_scalar(mask16[:, :], slots[:, :],
                                               float(CAP), ALU.is_ge))
                cDV.inc(v.tensor_scalar_add(zsc[:, :], slots[:, :], 1.0))
                v.wait_ge(cDV.sem, cDV.n)
                cDV.inc(v.tensor_mul(zsc[:, :], mask16[:, :], zsc[:, :]))
                v.wait_ge(cDV.sem, cDV.n)
                cDV.inc(v.tensor_sub(slots[:, :], slots[:, :], zsc[:, :]))
                v.wait_ge(cDV.sem, cDV.n)
                cDV.inc(v.tensor_copy(slotsi[:, :], slots[:, :]))
                st["sloti"] = cDV.n

            def gp_rt(gp):
                gp.wait_ge(cPE.sem, pe_at_g)
                gp.wait_ge(cDV.sem, dv_at_g)
                gp.wait_ge(self.cAC.sem, ac_at_g)
                self.g_iot.inc(gp.dma_start(iot16[:, :], self.iot16_d[:, :]))
                gp.wait_ge(cDV.sem, st["sloti"])
                self.g_iot.wait(gp, (0, self.g_iot.n[0]))
                if "nolsc" in self.dbg:
                    self.g_iot.inc(gp.dma_start(il[:, :],
                                                self.iot16_d[:, 0:CAP]))
                    self.g_iot.inc(gp.dma_start(kl[:, :],
                                                self.iot16_d[:, 0:CAP]))
                else:
                    gp.load_library(LIB_LSC)
                    cGP.inc(gp.local_scatter(il[:, :], iot16[:, :],
                                             slotsi[:, :], channels=16,
                                             num_elems=CAP, num_idxs=TOK))
                    cGP.inc(gp.local_scatter(kl[:, :], k2T[:, :],
                                             slotsi[:, :], channels=16,
                                             num_elems=CAP, num_idxs=TOK))
                st["lsc"] = cGP.n

            def pe_rt(t):
                if "nolsc" in self.dbg:
                    self.g_iot.wait_all(t)
                else:
                    t.wait_ge(cGP.sem, st["lsc"])
                t.wait_ge(self.cAC.sem, ac_at_g)
                last = None
                for j in range(2 * CT):
                    srcb = il if j < CT else kl
                    ct = j % CT
                    last = t.transpose(itp[:, j, :],
                                       srcb[:, ct * 128:(ct + 1) * 128],
                                       idh[:, :])
                cPE.inc(last)
                st["rtp"] = cPE.n

            def dv_rt2(v):
                v.wait_ge(cPE.sem, st["rtp"])
                cDV.inc(v.tensor_copy(IxKf[:, :, :], itp[:, :, :]))
                v.wait_ge(cDV.sem, cDV.n)
                cDV.inc(v.tensor_copy(IxI[:, :, :], IxKf[:, 0:CT, :]))
                cDV.inc(v.scalar_tensor_tensor(rowf[:, :, :],
                                               IxKf[:, CT:2 * CT, :],
                                               float(TOK),
                                               IxKf[:, 0:CT, :],
                                               op0=ALU.mult, op1=ALU.add))
                v.wait_ge(cDV.sem, cDV.n)
                cDV.inc(v.tensor_scalar_add(rowf[:, :, :], rowf[:, :, :],
                                            float(128 - TOK)))
                v.wait_ge(cDV.sem, cDV.n)
                cDV.inc(v.tensor_single_scalar(rowf[:, :, :], rowf[:, :, :],
                                               0.0, ALU.max))
                v.wait_ge(cDV.sem, cDV.n)
                cDV.inc(v.tensor_copy(IdSI[:, :, :], rowf[:, :, :]))
                st["ridx"] = cDV.n

            self.on("vector", dv_rt)
            self.on("gpsimd", gp_rt)
            self.on("tensor", pe_rt)
            self.on("vector", dv_rt2)

    def ffn(self):
        nc, st = self.nc, self.st
        D, DT, E, NTT, INTER, IC, ITL = (self.D, self.DT, self.E, self.NTT,
                                         self.INTER, self.IC, self.ITL)
        TOK, CAP, CT = self.TOK, self.CAP, self.CT
        cPE, cDV, cAC, cGP = self.cPE, self.cDV, self.cAC, self.cGP
        b1sb, b2sb, gatesT, wk = self.b1sb, self.b2sb, self.gatesT, self.wk
        w1sl, w2sl, y_tm = self.w1sl, self.w2sl, self.y_tm
        IxI, IdSI, idf = self.IxI, self.IdSI, self.idf
        NG = IC * 4 * 2
        NH = 4
        WQ = D // NH

        pe_at_ffn = cPE.n
        ac_at_ffn = cAC.n
        dv_at_ffn = cDV.n
        gp_at_ffn = cGP.n

        def sy_yinit(sy):
            self.s_x1.wait_all(sy)
            sy.wait_ge(cPE.sem, pe_at_ffn)
            sy.wait_ge(cDV.sem, dv_at_ffn)
            sy.wait_ge(cAC.sem, ac_at_ffn)
            sy.wait_ge(cGP.sem, gp_at_ffn)
            for tt in range(NTT):
                self.s_yin.inc(sy.dma_start(
                    y_tm[:, tt, :],
                    self.x1_scr[tt * 128:(tt + 1) * 128, :]))

        self.on("sync", sy_yinit)

        if "noffn" in self.dbg:
            def dv_fake(v):
                self.s_yin.wait_all(v)
                for tt in range(NTT):
                    st[("b2evt", tt)] = cDV.n
            self.on("vector", dv_fake)
            return

        gatesT, b2sb = self.gatesT, self.b2sb
        with nc.psum_tensor("mpB", [128, 512], F32) as mpB:
            for tt in range(NTT):
                for ch in range(2):
                    fi = tt * 2 + ch

                    def pe_b2(t, tt=tt, ch=ch, fi=fi):
                        if fi == 0:
                            t.wait_ge(cDV.sem, st[("gTev", NTT - 1)])
                            self.g_bias.wait_all(t)
                            t.wait_ge(cDV.sem, dv_at_ffn)
                            t.wait_ge(cAC.sem, ac_at_ffn)
                        else:
                            t.wait_ge(cDV.sem, st[("b2ev", fi - 1)])
                        cPE.inc(t.matmul(
                            mpB[:, :],
                            gatesT[0:E, tt * 128:(tt + 1) * 128],
                            b2sb[:, ch * 512:(ch + 1) * 512],
                            start=True, stop=True))
                        st[("b2pe", fi)] = cPE.n

                    def dv_b2(v, tt=tt, ch=ch, fi=fi):
                        v.wait_ge(cPE.sem, st[("b2pe", fi)])
                        if fi == 0:
                            self.s_yin.wait_all(v)
                        v.wait_ge(cDV.sem, cDV.n)
                        cDV.inc(v.tensor_add(
                            y_tm[:, tt, ch * 512:(ch + 1) * 512],
                            y_tm[:, tt, ch * 512:(ch + 1) * 512],
                            mpB[:, :]))
                        st[("b2ev", fi)] = cDV.n

                    self.on("tensor", pe_b2)
                    self.on("vector", dv_b2)

        with nc.sbuf_tensor("hbuf", [128, ITL, CAP], BF16) as hbuf, \
             nc.sbuf_tensor("Xgf", [128, DT, CAP], BF16) as Xgf, \
             nc.sbuf_tensor("Xgt", [128, CT, D], F32) as Xgt, \
             nc.psum_tensor("mpT", [128, 2, 4, 128], F32) as mpT, \
             nc.psum_tensor("mpA", [128, 3, 512], F32) as mpA, \
             nc.psum_tensor("mp2", [128, CT, WQ], F32) as mp2:

            NE = E
            for f in self.dbg:
                if f.startswith("ne"):
                    NE = int(f[2:])
            self.NE = NE
            with nc.sbuf_tensor("og", [128, CT, D], BF16) as og:
                for ei in range(NE):
                    self.expert(ei, hbuf, Xgf, Xgt, og, mpT, mpA, mp2,
                                pe_at_ffn, ac_at_ffn, dv_at_ffn)

                if "nomm2" not in self.dbg:
                    def gp_sc_last(gp):
                        self.emit_scatters(gp, NE - 1, og)

                    self.on("gpsimd", gp_sc_last)

            if "nofin" in self.dbg:
                def dv_fake2(v):
                    self.s_yin.wait_all(v)
                    self.g_sc.wait_all(v)
                    for tt in range(NTT):
                        st[("b2evt", tt)] = cDV.n
                self.on("vector", dv_fake2)
            else:
                with nc.sbuf_tensor("yl", [128, 4, D], BF16) as yl:
                    self.finish_y(yl, mpA)

    def expert(self, ei, hbuf, Xgf, Xgt, og, mpT, mpA, mp2,
               pe_at_ffn, ac_at_ffn, dv_at_ffn):
        nc, st = self.nc, self.st
        D, DT, E, IC, CAP, CT = (self.D, self.DT, self.E, self.IC, self.CAP,
                                 self.CT)
        cPE, cDV, cAC = self.cPE, self.cDV, self.cAC
        b1sb, hbuf_, IxI, idf = self.b1sb, hbuf, self.IxI, self.idf
        NG = IC * 4 * 2
        NH, WQ = 4, D // 4
        widths = (512, CAP - 512)

        if "noxT" in self.dbg:
            def dv_fakexT(v, ei=ei):
                for j in range(2 * CT):
                    st[("xgT", ei, j)] = cPE.n
                    st[("xgev", ei, j)] = cDV.n
                st[("mm1end", ei - 1)] = st.get(("mm1end", ei - 1), cPE.n)
            self.on("vector", dv_fakexT)

        if ei == 0 and "noxT" not in self.dbg:
            def gp_g0(gp):
                gp.wait_ge(cDV.sem, st["ridx"])
                gp.wait_ge(cPE.sem, pe_at_ffn)
                gp.wait_ge(cAC.sem, ac_at_ffn)
                self.s_x1.wait_all(gp)
                for ct in range(CT):
                    if "noind" in self.dbg:
                        dma = gp.dma_start(Xgt[:, ct, :],
                                           self.x1_scr[ct * 128:(ct + 1) * 128, :])
                    else:
                        dma = gp.indirect_dma_start(
                            Xgt[:, ct, :], None, self.x1_scr[:, :],
                            bass.IndirectOffsetOnAxis(IxI[:, ct, 0:1], 0))
                    st[("xg", 0, ct)] = self.g_xg.inc(dma, ct)

            self.on("gpsimd", gp_g0)

        for j in range(2 * CT if "noxT" not in self.dbg else 0):
            ct, hf = j // 2, j % 2

            def pe_xT(t, ei=ei, j=j, ct=ct, hf=hf):
                if j == 0:
                    self.g_xg.wait(t, st[("xg", ei, 0)])
                    if ei == 0:
                        t.wait_ge(cDV.sem, dv_at_ffn)
                        t.wait_ge(cAC.sem, ac_at_ffn)
                        t.wait_ge(cDV.sem, st[("b2ev", 2 * self.NTT - 1)])
                elif hf == 0:
                    self.g_xg.wait(t, st[("xg", ei, ct)])
                if j >= 2:
                    t.wait_ge(cDV.sem, st[("xgev", ei, j - 2)])
                elif ei >= 1:
                    t.wait_ge(cDV.sem, st[("xgev", ei - 1, 2 * CT - 2 + j)])
                last = None
                for m in range(4):
                    last = t.transpose(
                        mpT[:, j % 2, m, :],
                        Xgt[:, ct, (hf * 4 + m) * 128:(hf * 4 + m + 1) * 128],
                        idf[:, :])
                cPE.inc(last)
                st[("xgT", ei, j)] = cPE.n

            def dv_xT(v, ei=ei, j=j, ct=ct, hf=hf):
                v.wait_ge(cPE.sem, st[("xgT", ei, j)])
                if ei >= 1 and j == 0:
                    v.wait_ge(cPE.sem, st[("mm1end", ei - 1)])
                cDV.inc(v.tensor_copy(
                    Xgf[:, hf * 4:(hf + 1) * 4, ct * 128:(ct + 1) * 128],
                    mpT[:, j % 2, :, :]))
                st[("xgev", ei, j)] = cDV.n

            self.on("tensor", pe_xT)
            self.on("vector", dv_xT)

        if "nomm1" in self.dbg:
            def dv_fakemm1(v, ei=ei):
                for g in range(NG):
                    st[("gelu", ei, g)] = cAC.n
                for c in range(IC):
                    st[("mm1cd", ei * IC + c)] = cPE.n
                st[("mm1end", ei)] = cPE.n
            self.on("vector", dv_fakemm1)

        for c in range(IC if "nomm1" not in self.dbg else 0):
            idx = ei * IC + c

            def gp_w1(gp, ei=ei, c=c, idx=idx):
                if idx >= 2:
                    gp.wait_ge(cPE.sem, st[("mm1cd", idx - 2)])
                st[("w1ld", idx)] = self.g_w1.inc(gp.dma_start(
                    self.w1sl[:, idx % 2, :, :],
                    bass.AP(self.w1, ei * D * self.INTER + c * 512,
                            [[self.INTER, 128], [128 * self.INTER, DT],
                             [1, 512]])), idx % 2)

            self.on("gpsimd", gp_w1)
            for it in range(4):
                def pe_mm1(t, ei=ei, c=c, idx=idx, it=it):
                    if it == 0:
                        self.g_w1.wait(t, st[("w1ld", idx)])
                    for tch in range(2):
                        g = (c * 4 + it) * 2 + tch
                        slot = g % 3
                        w = widths[tch]
                        if g >= 3:
                            t.wait_ge(cAC.sem, st[("gelu", ei, g - 3)])
                        elif ei >= 1:
                            t.wait_ge(cAC.sem,
                                      st[("gelu", ei - 1, NG - 3 + g)])
                        else:
                            t.wait_ge(cDV.sem, dv_at_ffn)
                        if g == 0:
                            t.wait_ge(cDV.sem, st[("xgev", ei, 2 * CT - 1)])
                        last = None
                        for k in range(DT):
                            last = t.matmul(
                                mpA[:, slot, 0:w],
                                self.w1sl[:, idx % 2, k,
                                          it * 128:(it + 1) * 128],
                                Xgf[:, k, tch * 512:tch * 512 + w],
                                start=(k == 0), stop=(k == DT - 1))
                        cPE.inc(last)
                        st[("mm1pe", ei, g)] = cPE.n
                    if it == 3:
                        st[("mm1cd", idx)] = cPE.n
                        if c == IC - 1:
                            st[("mm1end", ei)] = cPE.n

                def ac_gelu(s, ei=ei, c=c, it=it):
                    for tch in range(2):
                        g = (c * 4 + it) * 2 + tch
                        slot = g % 3
                        w = widths[tch]
                        s.wait_ge(cPE.sem, st[("mm1pe", ei, g)])
                        if g == 0 and ei == 0:
                            self.g_bias.wait_all(s)
                            s.wait_ge(cDV.sem, dv_at_ffn)
                        cAC.inc(s.activation(
                            hbuf[:, c * 4 + it, tch * 512:tch * 512 + w],
                            mpA[:, slot, 0:w], AF.Gelu,
                            bias=b1sb[:, ei, c * 4 + it:c * 4 + it + 1]))
                        st[("gelu", ei, g)] = cAC.n

                self.on("tensor", pe_mm1)
                self.on("scalar", ac_gelu)

        if ei + 1 < getattr(self, "NE", E):
            def gp_gn(gp, ei=ei):
                for ct in range(CT):
                    gp.wait_ge(cPE.sem, st[("xgT", ei, 2 * ct + 1)])
                    if "noind" in self.dbg:
                        dma = gp.dma_start(Xgt[:, ct, :],
                                           self.x1_scr[ct * 128:(ct + 1) * 128, :])
                    else:
                        dma = gp.indirect_dma_start(
                            Xgt[:, ct, :], None, self.x1_scr[:, :],
                            bass.IndirectOffsetOnAxis(IxI[:, ct, ei + 1:ei + 2],
                                                      0))
                    st[("xg", ei + 1, ct)] = self.g_xg.inc(dma, ct)

            self.on("gpsimd", gp_gn)
        if ei >= 1 and "nomm2" not in self.dbg:
            def gp_sc(gp, ei=ei):
                self.emit_scatters(gp, ei - 1, og)

            self.on("gpsimd", gp_sc)

        if "nomm2" in self.dbg:
            def dv_fakemm2(v, ei=ei):
                for Hh in range(NH):
                    for tt in range(CT):
                        st[("mm2pe", ei, Hh, tt)] = cPE.n
                        st[("ogev", ei, Hh, tt)] = cDV.n
                for Hh in range(NH):
                    for c in range(IC):
                        st[("mm2cd", (ei * NH + Hh) * IC + c)] = cPE.n
                if ei >= 1:
                    for tt in range(CT):
                        st[("scat", ei - 1, tt)] = (0, self.g_sc.n[0])
            self.on("vector", dv_fakemm2)

        for Hh in range(NH if "nomm2" not in self.dbg else 0):
            for c in range(IC):
                idx2 = (ei * NH + Hh) * IC + c

                def gp_w2(gp, ei=ei, Hh=Hh, c=c, idx2=idx2):
                    if idx2 >= 2:
                        gp.wait_ge(cPE.sem, st[("mm2cd", idx2 - 2)])
                    st[("w2ld", idx2)] = self.g_w2.inc(gp.dma_start(
                        self.w2sl[:, idx2 % 2, :, :],
                        bass.AP(self.w2,
                                ei * self.INTER * D + c * 512 * D + Hh * WQ,
                                [[D, 128], [128 * D, 4], [1, WQ]])),
                        idx2 % 2)

                def pe_mm2(t, ei=ei, Hh=Hh, c=c, idx2=idx2):
                    self.g_w2.wait(t, st[("w2ld", idx2)])
                    last = None
                    for tt in range(CT):
                        if c == 0:
                            if Hh == 0 and ei == 0:
                                t.wait_ge(cDV.sem, dv_at_ffn)
                                t.wait_ge(cAC.sem, ac_at_ffn)
                            elif Hh == 0:
                                t.wait_ge(cDV.sem,
                                          st[("ogev", ei - 1, NH - 1, tt | 1)])
                            else:
                                t.wait_ge(cDV.sem,
                                          st[("ogev", ei, Hh - 1, tt | 1)])
                        for k in range(4):
                            if tt == 0:
                                t.wait_ge(cAC.sem,
                                          st[("gelu", ei, (c * 4 + k) * 2 + 1)])
                            last = t.matmul(
                                mp2[:, tt, :],
                                hbuf[:, c * 4 + k, tt * 128:(tt + 1) * 128],
                                self.w2sl[:, idx2 % 2, k, :],
                                start=(c == 0 and k == 0 and tt % 2 == 0),
                                stop=(c == IC - 1 and k == 3))
                        if c == IC - 1:
                            cPE.inc(last)
                            st[("mm2pe", ei, Hh, tt)] = cPE.n
                    if c != IC - 1:
                        cPE.inc(last)
                    st[("mm2cd", idx2)] = cPE.n

                self.on("gpsimd", gp_w2)
                self.on("tensor", pe_mm2)

            def dv_og(v, ei=ei, Hh=Hh):
                for tt in range(CT):
                    v.wait_ge(cPE.sem, st[("mm2pe", ei, Hh, tt | 1)])
                    if Hh == 0 and ei >= 1:
                        self.g_sc.wait(v, st[("scat", ei - 1, tt)])
                    cDV.inc(v.tensor_copy(og[:, tt, Hh * WQ:(Hh + 1) * WQ],
                                          mp2[:, tt, :]))
                    st[("ogev", ei, Hh, tt)] = cDV.n

            self.on("vector", dv_og)

    def emit_scatters(self, gp, ei, og):
        st, CT = self.st, self.CT
        if ei == 0:
            self.s_yz.wait_all(gp)
        for tt in range(CT):
            gp.wait_ge(self.cDV.sem, st[("ogev", ei, 3, tt)])
            if self.g_sc.n[0]:
                gp.wait_ge(self.g_sc.sems[0], self.g_sc.n[0])
            if "noind" in self.dbg:
                r0 = 128 + (((ei * CT + tt) % 32) * 128)
                dma = gp.dma_start(self.y_scr[r0:r0 + 128, :], og[:, tt, :])
            else:
                dma = gp.indirect_dma_start(
                    self.y_scr[:, :],
                    bass.IndirectOffsetOnAxis(self.IdSI[:, tt, ei:ei + 1], 0),
                    og[:, tt, :], None)
            st[("scat", ei, tt)] = self.g_sc.inc(dma)

    def finish_y(self, yl, mpA):
        nc, st = self.nc, self.st
        D, E, NTT, TOK = self.D, self.E, self.NTT, self.TOK
        cPE, cDV, cAC = self.cPE, self.cDV, self.cAC
        y_tm, wk, gatesT, b2sb = self.y_tm, self.wk, self.gatesT, self.b2sb
        NG = self.IC * 4 * 2

        for tt in range(NTT):
            for k2 in range(2):
                sl = (tt * 2 + k2) % 4

                def sy_yl(sy, tt=tt, k2=k2, sl=sl):
                    if tt == 0 and k2 == 0:
                        self.g_sc.wait_all(sy)
                    if tt >= 2:
                        sy.wait_ge(cDV.sem, st[("yadd", tt - 2, k2)])
                    r0 = 128 + k2 * TOK + tt * 128
                    st[("yld", tt, k2)] = self.s_yl.inc(
                        sy.dma_start(yl[:, sl, :],
                                     self.y_scr[r0:r0 + 128, :]), sl)

                def dv_yadd(v, tt=tt, k2=k2, sl=sl):
                    self.s_yl.wait(v, st[("yld", tt, k2)])
                    if tt == 0 and k2 == 0:
                        self.s_yin.wait_all(v)
                    v.wait_ge(cDV.sem, cDV.n)
                    cDV.inc(v.scalar_tensor_tensor(
                        y_tm[:, tt, :], yl[:, sl, :], wk[:, tt, k2:k2 + 1],
                        y_tm[:, tt, :], op0=ALU.mult, op1=ALU.add))
                    st[("yadd", tt, k2)] = cDV.n
                    if k2 == 1:
                        st[("b2evt", tt)] = cDV.n

                self.on("sync", sy_yl)
                self.on("vector", dv_yadd)


    def ln2(self):
        nc, st = self.nc, self.st
        D, NTT = self.D, self.NTT
        cDV, cAC = self.cDV, self.cAC
        y_tm, outst, lnt2, sqj2 = self.y_tm, self.outst, self.lnt2, self.sqj2

        for tt in range(NTT):
            s2 = tt % 2

            def dv_a(v, tt=tt, s2=s2):
                v.wait_ge(cDV.sem, st[("b2evt", tt)])
                cDV.inc(v.reduce_sum(lnt2[:, s2 * 8 + 0: s2 * 8 + 1],
                                     y_tm[:, tt, :], axis=AX.X))
                st[("l2sum", tt)] = cDV.n

            def ac_a(s, tt=tt, s2=s2):
                s.wait_ge(cDV.sem, st[("l2sum", tt)])
                if tt >= 2:
                    s.wait_ge(cAC.sem, st[("l2done", tt - 2)])
                cAC.inc(s.activation(sqj2[:, :], y_tm[:, tt, :],
                                     AF.Square,
                                     accum_out=lnt2[:, s2 * 8 + 1: s2 * 8 + 2]))
                st[("l2sq", tt)] = cAC.n

            def dv_b(v, tt=tt, s2=s2):
                o = s2 * 8
                v.wait_ge(cAC.sem, st[("l2sq", tt)])
                cDV.inc(v.tensor_scalar_mul(lnt2[:, o + 2:o + 3],
                                            lnt2[:, o + 0:o + 1], 1.0 / D))
                cDV.inc(v.tensor_scalar_mul(lnt2[:, o + 3:o + 4],
                                            lnt2[:, o + 1:o + 2], 1.0 / D))
                v.wait_ge(cDV.sem, cDV.n)
                cDV.inc(v.tensor_mul(lnt2[:, o + 4:o + 5], lnt2[:, o + 2:o + 3],
                                     lnt2[:, o + 2:o + 3]))
                v.wait_ge(cDV.sem, cDV.n)
                cDV.inc(v.tensor_sub(lnt2[:, o + 5:o + 6], lnt2[:, o + 3:o + 4],
                                     lnt2[:, o + 4:o + 5]))
                v.wait_ge(cDV.sem, cDV.n)
                cDV.inc(v.tensor_scalar_add(lnt2[:, o + 6:o + 7],
                                            lnt2[:, o + 5:o + 6], 1e-5))
                st[("l2var", tt)] = cDV.n

            def ac_b(s, tt=tt, s2=s2):
                s.wait_ge(cDV.sem, st[("l2var", tt)])
                cAC.inc(s.activation(lnt2[:, s2 * 8 + 7: s2 * 8 + 8],
                                     lnt2[:, s2 * 8 + 6: s2 * 8 + 7], AF.Sqrt))
                st[("l2std", tt)] = cAC.n

            def dv_c(v, tt=tt, s2=s2):
                o = s2 * 8
                v.wait_ge(cAC.sem, st[("l2std", tt)])
                cDV.inc(v.reciprocal(lnt2[:, o + 4:o + 5], lnt2[:, o + 7:o + 8]))
                v.wait_ge(cDV.sem, cDV.n)
                cDV.inc(v.tensor_mul(lnt2[:, o + 5:o + 6], lnt2[:, o + 2:o + 3],
                                     lnt2[:, o + 4:o + 5]))
                v.wait_ge(cDV.sem, cDV.n)
                cDV.inc(v.tensor_scalar_mul(lnt2[:, o + 6:o + 7],
                                            lnt2[:, o + 5:o + 6], -1.0))
                st[("l2ab", tt)] = cDV.n

            def ac_c(s, tt=tt, s2=s2):
                s.wait_ge(cDV.sem, st[("l2ab", tt)])
                if tt >= 2:
                    self.s_out.wait(s, st[("outst", tt - 2)])
                cAC.inc(s.activation(outst[:, s2, :],
                                     y_tm[:, tt, :], AF.Identity,
                                     bias=lnt2[:, s2 * 8 + 6: s2 * 8 + 7],
                                     scale=lnt2[:, s2 * 8 + 4: s2 * 8 + 5]))
                st[("l2done", tt)] = cAC.n

            def sy_out(sy, tt=tt, s2=s2):
                sy.wait_ge(cAC.sem, st[("l2done", tt)])
                st[("outst", tt)] = self.s_out.inc(
                    sy.dma_start(self.yout[tt * 128:(tt + 1) * 128, :],
                                 outst[:, s2, :]), s2)

            self.on("vector", dv_a)
            self.on("scalar", ac_a)
            self.on("vector", dv_b)
            self.on("scalar", ac_b)
            self.on("vector", dv_c)
            self.on("scalar", ac_c)
            self.on("sync", sy_out)

        def sy_fin(sy):
            self.s_out.wait_all(sy)

        self.on("sync", sy_fin)


_CACHE = {}


def _masks(TOK, BS=64, WIN=512):
    WINW = WIN + BS
    HALF = TOK // 2
    HQB = HALF // BS
    POFF = HQB // 2
    NP = POFF
    WB = WIN // BS
    out = []
    qi = np.arange(BS)[:, None]
    ki = np.arange(WINW)[None, :]
    wi = ki // BS
    kwi = ki % BS
    for parity in range(2):
        m = np.zeros((2 * NP, 128, WINW), np.float32)
        for ha in range(2):
            for r in range(NP):
                for half_row, qb in ((0, r), (1, r + POFF)):
                    g = ha * HQB + qb
                    g_seq = g + parity * (TOK // BS)
                    qpos = g_seq * BS + qi
                    kpos = (g_seq + wi - WB) * BS + kwi
                    valid = ((g_seq + wi - WB) >= 0) & (kpos <= qpos) & \
                            (qpos - kpos < WIN)
                    m[ha * NP + r, half_row * 64:(half_row + 1) * 64, :] = \
                        np.where(valid, 0.0, NEG).astype(np.float32)
        out.append(m.astype(ml_dtypes.bfloat16))
    return out


def kernel(L0_attn_x, L0_attn_Wq, L0_attn_Wk, L0_attn_Wv, L0_attn_Wo,
           L0_ffn_Wg, L0_ffn_W1, L0_ffn_b1, L0_ffn_W2, L0_ffn_b2):
    TOK, D, E, INTER = 2048, 1024, 8, 4096
    x = np.ascontiguousarray(np.asarray(L0_attn_x, dtype=np.float32)).reshape(-1, D)
    NCORE = 8
    if "nc" not in _CACHE:
        _CACHE["nc"] = Builder(TOK=TOK, D=D, E=E, INTER=INTER).build()
        _CACHE["masks"] = _masks(TOK)
    nc = _CACHE["nc"]
    mev, mod = _CACHE["masks"]

    idenb = np.eye(128, dtype=ml_dtypes.bfloat16)
    idenf = np.eye(128, dtype=np.float32)
    ioteps = np.broadcast_to((np.arange(E) * -1e-6).astype(np.float32),
                             (128, E)).copy()
    iota16 = np.broadcast_to(np.arange(TOK, dtype=np.float16),
                             (16, TOK)).copy()
    common = dict(
        wq=np.ascontiguousarray(L0_attn_Wq, np.float32),
        wk=np.ascontiguousarray(L0_attn_Wk, np.float32),
        wv=np.ascontiguousarray(L0_attn_Wv, np.float32),
        wo=np.ascontiguousarray(L0_attn_Wo, np.float32),
        wg=np.ascontiguousarray(L0_ffn_Wg, np.float32),
        w1=np.ascontiguousarray(L0_ffn_W1, np.float32),
        b1=np.ascontiguousarray(
            np.asarray(L0_ffn_b1, np.float32).reshape(E, INTER // 128, 128)
            .transpose(2, 0, 1)),
        w2=np.ascontiguousarray(L0_ffn_W2, np.float32),
        b2=np.ascontiguousarray(L0_ffn_b2, np.float32),
        idenb=idenb, idenf=idenf, ioteps=ioteps, iota16=iota16,
    )
    in_maps = []
    for c in range(NCORE):
        xm = x[c * TOK:(c + 1) * TOK]
        xh = (np.zeros((512, D), np.float32) if c % 2 == 0
              else x[c * TOK - 512: c * TOK])
        in_maps.append(dict(common, x_main=np.ascontiguousarray(xm),
                            x_halo=np.ascontiguousarray(xh),
                            mask=(mev if c % 2 == 0 else mod)))
    res = run_bass_kernel_spmd(nc, in_maps, core_ids=list(range(NCORE)))
    out = np.concatenate([r["yout"] for r in res.results], axis=0)
    return out.reshape(4, 4096, 1024).astype(np.float32)
